# version 06
# baseline (speedup 1.0000x reference)
import sys
if "/opt/trn_rl_repo" not in sys.path:
    sys.path.insert(0, "/opt/trn_rl_repo")

from contextlib import ExitStack
import math
import numpy as np

import concourse.bass as bass
import concourse.mybir as mybir
from concourse.bass_utils import run_bass_kernel_spmd

F32 = mybir.dt.float32
BF16 = mybir.dt.bfloat16
AF = mybir.ActivationFunctionType
ALU = mybir.AluOpType
AX = mybir.AxisListType

B, T, D, H, DH = 2, 8192, 2048, 16, 128
DQ, DKV, DKR = 682, 1024, 64
DCKV = DKV + DKR
NCQ = 6
CQ_TAIL = DQ - 5 * 128
NCKV = 8
NCD = 16
TCORE = 2048
NT = 8
NSTRIP = 4
SW = 512
QG = 256
ATT_SCALE = 1.0 / math.sqrt(DH)
EPS = 1e-5
NE, RLORA = 8, 16
ER = NE * RLORA

QNT = 4
K_SBLK = H * DH * SW
V_SBLK = H * SW * DH
SBLK = K_SBLK + V_SBLK + 1024
KOFFs, VOFFs, NOFFs = 0, K_SBLK, K_SBLK + V_SBLK
CONTRIB = NSTRIP * SBLK


def AP(t, off, dims):
    return bass.AP(t, off, [list(d) for d in dims])


class Sem:
    __slots__ = ("h", "n")

    def __init__(self, h):
        self.h = h
        self.n = 0


class Streams:
    NAMES = ("sync", "tensor", "vector", "scalar", "gpsimd")

    def __init__(self):
        self.s = {n: [] for n in self.NAMES}

    def emit(self, eng, fn, sem=None, k=1):
        if sem is None:
            self.s[eng].append(fn)
            return None
        sem.n += k
        self.s[eng].append(lambda e, f=fn, sh=sem.h, kk=k: f(e).then_inc(sh, kk))
        return sem.n

    def wait(self, eng, sem, val=None):
        v = sem.n if val is None else val
        if v > 0:
            self.s[eng].append(lambda e, sh=sem.h, vv=v: e.wait_ge(sh, vv))

    def flush(self, nc):
        with nc.Block() as block:
            for n in self.NAMES:
                ops = self.s[n]
                if ops:
                    getattr(block, n)(lambda e, ops=ops: [f(e) for f in ops])
        self.s = {n: [] for n in self.NAMES}


class MK:
    def __init__(self, taps=(), stop_after=None):
        self.taps = list(taps)
        self.stop_after = stop_after
        self.nc = bass.Bass("TRN2", target_bir_lowering=False)

    def build(self):
        nc = self.nc

        def din(name, shape):
            return nc.dram_tensor(name, list(shape), F32, kind="ExternalInput")

        self.x_own = din("x_own", [TCORE, D])
        self.w_dq = din("w_dq", [DQ, D])
        self.b_dq = din("b_dq", [DQ])
        self.qng = din("qng", [DQ])
        self.qnb = din("qnb", [DQ])
        self.w_uq = din("w_uq", [D, DQ])
        self.b_uq = din("b_uq", [D])
        self.w_dkv = din("w_dkv", [DCKV, D])
        self.b_dkv = din("b_dkv", [DCKV])
        self.kvg = din("kvg", [DKV])
        self.kvb = din("kvb", [DKV])
        self.w_ukv = din("w_ukv", [DKV + D, DKV])
        self.b_ukv = din("b_ukv", [DKV + D])
        self.w_o = din("w_o", [D, D])
        self.b_o = din("b_o", [D])
        self.w_ffn = din("w_ffn", [D, D])
        self.b_ffn = din("b_ffn", [D])
        self.lora_a = din("lora_a", [NE, RLORA, D])
        self.lora_b = din("lora_b", [NE, D, RLORA])
        self.gate_w = din("gate_w", [NE, D])
        self.gate_b = din("gate_b", [NE])
        self.cosrep = din("cosrep", [128, D])
        self.sinrep = din("sinrep", [128, D])
        self.mthr = din("mthr", [128, 1])

        self.out = nc.dram_tensor("out", [TCORE, D], F32, kind="ExternalOutput")

        self.contrib = nc.dram_tensor("contrib", [CONTRIB], BF16)
        self.gathered = nc.dram_tensor("gathered", [4 * CONTRIB], BF16)
        self.q_spill = nc.dram_tensor("q_spill", [H * QNT * DH * SW], BF16)
        self.ot_spill = nc.dram_tensor("ot_spill", [H * QNT * DH * SW], BF16)
        self.wo_sp = nc.dram_tensor("wo_sp", [D * D], BF16)
        self.wffn_sp = nc.dram_tensor("wffn_sp", [D * D], BF16)
        self.acat_sp = nc.dram_tensor("acat_sp", [ER * D], BF16)
        self.bcat_sp = nc.dram_tensor("bcat_sp", [ER * D], BF16)
        self.gate_sp = nc.dram_tensor("gate_sp", [128 * NCD * NE], BF16)
        self.dbg1 = nc.dram_tensor("dbg1", [TCORE * D], BF16)
        self.dbg2 = nc.dram_tensor("dbg2", [TCORE * D], BF16)
        self.dbg3 = nc.dram_tensor("dbg3", [TCORE * ER], BF16)

        es = ExitStack()
        self.es = es

        def sb(name, shape, dt=F32, side="left"):
            return es.enter_context(nc.sbuf_tensor(name, list(shape), dt, side=side))
        self.sb = sb

        with es:
            self.identb = sb("identb", [128, 128], BF16)
            self.identf = sb("identf", [128, 128], F32)
            self.ones_bf = sb("ones_bf", [128, 1], BF16)
            self.onesA = sb("onesA", [128, 1], BF16)
            self.onesB = sb("onesB", [128, 1], BF16)
            self.onesrow_bf = sb("onesrow_bf", [1, 128], BF16)
            self.onesrow_f = sb("onesrow_f", [1, 128], F32)
            self.epsb = sb("epsb", [128, 1], F32)
            self.bdq_c = sb("bdq_c", [128, NCQ], F32)
            self.buq_c = sb("buq_c", [128, NCD], F32)
            self.bdkv_c = sb("bdkv_c", [128, 9], F32)
            self.buk_c = sb("buk_c", [128, NCKV], F32)
            self.qng_c = sb("qng_c", [128, NCQ], F32)
            self.qnb_c = sb("qnb_c", [128, NCQ], F32)
            self.kvg_c = sb("kvg_c", [128, NCKV], F32)
            self.kvb_c = sb("kvb_c", [128, NCKV], F32)
            self.bukvv_r = sb("bukvv_r", [1, D], BF16)
            self.maxq2 = sb("maxq2", [1, H * QNT], F32)
            self.maxk2 = sb("maxk2", [1, H], F32)
            self.negm_bc = sb("negm_bc", [128, QNT * H], F32)

            for nm in ("sLD", "sGP", "sGE", "sPE", "sDV", "sAC", "sST", "sCC",
                       "sLDa", "sLDb", "sQTa", "sQTb", "sKNa", "sKNb", "sKR",
                       "sXa", "sXb", "sKVa", "sKVb", "sQBa", "sQBb", "sOTa", "sOTb",
                       "sWL", "sOLa", "sOLb", "sXT"):
                setattr(self, nm, Sem(es.enter_context(nc.semaphore(nm))))

            cs = ExitStack()
            self.cosr = cs.enter_context(nc.sbuf_tensor("cosr", [128, D], F32, side="right"))
            self.sinr = cs.enter_context(nc.sbuf_tensor("sinr", [128, D], F32, side="right"))
            with nc.named_scope('p0c'):
                self.p0_consts()
            wa = ExitStack()
            self.WdqT = wa.enter_context(nc.sbuf_tensor("WdqT", [128, NCD * DQ], BF16))
            self.WuqT = wa.enter_context(nc.sbuf_tensor("WuqT", [128, NCQ * D], BF16))
            self.WdkvT = wa.enter_context(nc.sbuf_tensor("WdkvT", [128, NCD * DCKV], BF16))
            self.WukT = wa.enter_context(nc.sbuf_tensor("WukT", [128, NCKV * DKV], BF16))
            self.WuvT = wa.enter_context(nc.sbuf_tensor("WuvT", [128, NCKV * D], BF16))
            with nc.named_scope('p0w'):
                self.p0_weights()
            cs.close()
            if self.stop_after == "p0":
                wa.close()
                self._emit_taps()
                return nc
            with nc.named_scope('p1'):
                self.p1()
            wa.close()
            self.mask = sb("mask", [128, 16 * SW], BF16)
            self.bo_r = sb("bo_r", [1, D], BF16)
            self.bffn_r = sb("bffn_r", [1, D], BF16)
            self.gateb_r = sb("gateb_r", [1, NE], BF16)
            self.WoT = sb("WoT", [128, NCD * D], BF16)
            self.WffnT = sb("WffnT", [128, NCD * D], BF16)
            self.AcatT = sb("AcatT", [128, NCD * ER], BF16)
            self.gateT = sb("gateT", [128, NCD * NE], BF16)
            self.BcatT = sb("BcatT", [128, D], BF16)
            with nc.named_scope('ag'):
                self.ag()
            if self.stop_after == "p1":
                self._emit_taps()
                return nc
            with nc.named_scope('p2'):
                self.p2()
            if self.stop_after == "p2":
                self._emit_taps()
                return nc
            with nc.named_scope('p3'):
                self.p3()
            self._emit_taps()
        return nc

    def _emit_taps(self):
        if not self.taps:
            return
        st = Streams()
        tmap = {"contrib": (self.contrib, CONTRIB),
                "gathered": (self.gathered, 4 * CONTRIB),
                "q_spill": (self.q_spill, H * NT * DH * QG),
                "ot_spill": (self.ot_spill, H * NT * DH * QG),
                "wo_sp": (self.wo_sp, D * D),
                "wffn_sp": (self.wffn_sp, D * D),
                "acat_sp": (self.acat_sp, ER * D),
                "bcat_sp": (self.bcat_sp, ER * D),
                "gate_sp": (self.gate_sp, 128 * NCD * NE),
                "dbg1": (self.dbg1, TCORE * D), "dbg2": (self.dbg2, TCORE * D),
                "dbg3": (self.dbg3, TCORE * ER)}
        for t in self.taps:
            th, size = tmap[t]
            tout = self.nc.dram_tensor("tap_" + t, [size], BF16, kind="ExternalOutput")
            per = size // 128
            st.emit("sync", lambda e, a=tout, b=th, p=per: e.dma_start(
                out=AP(a, 0, [[p, 128], [1, p]]), in_=AP(b, 0, [[p, 128], [1, p]])),
                self.sST, 16)
        st.wait("sync", self.sST)
        st.flush(self.nc)

    def p0_consts(self):
        nc = self.nc
        sLD, sGP, sGE, sDV, sAC = self.sLD, self.sGP, self.sGE, self.sDV, self.sAC
        st = Streams()
        with ExitStack() as p0:
            vrow = p0.enter_context(nc.sbuf_tensor("p0vrow", [1, DKV + D], F32, side="right"))
            vrow2 = p0.enter_context(nc.sbuf_tensor("p0vrow2", [1, D], F32, side="right"))
            rscr = p0.enter_context(nc.sbuf_tensor("p0rscr", [1, D], F32, side="right"))

            st.emit("gpsimd", lambda e: e.iota(self.identf[:], [[1, 128]], channel_multiplier=-1,
                                               allow_small_or_imprecise_dtypes=True), sGE)
            st.emit("gpsimd", lambda e: e.memset(self.ones_bf[:], 1.0), sGE)
            st.emit("gpsimd", lambda e: e.memset(self.onesrow_bf[:], 1.0), sGE)
            st.emit("gpsimd", lambda e: e.memset(self.onesrow_f[:], 1.0), sGE)
            st.emit("gpsimd", lambda e: e.memset(self.epsb[:], EPS), sGE)
            st.emit("gpsimd", lambda e: e.iota(self.onesA[:], [[0, 1]], channel_multiplier=1,
                                               allow_small_or_imprecise_dtypes=True), sGE)
            st.emit("sync", lambda e: e.dma_start(out=self.cosr[:], in_=self.cosrep.ap()), sLD, 16)
            st.emit("sync", lambda e: e.dma_start(out=self.sinr[:], in_=self.sinrep.ap()), sLD, 16)
            st.wait("vector", sGE)
            st.wait("vector", sLD)
            st.emit("vector", lambda e: e.tensor_scalar(self.identf[:], self.identf[:], 0.0, None,
                                                        op0=ALU.is_equal), sDV)
            st.emit("vector", lambda e: e.tensor_scalar(self.onesB[:], self.onesA[:], 63.5, None,
                                                        op0=ALU.is_gt), sDV)
            st.wait("vector", sDV)
            st.emit("vector", lambda e: e.tensor_scalar(self.onesA[:], self.onesA[:], 63.5, None,
                                                        op0=ALU.is_lt), sDV)
            st.wait("vector", sDV)
            st.emit("vector", lambda e: e.tensor_copy(self.identb[:], self.identf[:]), sDV)

            def vec_load(dram, n):
                st.wait("sync", sGP)
                st.wait("sync", sDV)
                st.emit("sync", lambda e, d=dram, nn=n: e.dma_start(
                    out=vrow[0:1, 0:nn], in_=AP(d, 0, [[nn, 1], [1, nn]])), sLD, 16)

            def col_store(dst, n, nch, src=None, src_off=0):
                s = vrow if src is None else src
                st.wait("gpsimd", sLD)
                st.wait("gpsimd", sDV)
                st.wait("gpsimd", self.sGE)
                for c in range(nch):
                    w = min(128, n - c * 128)
                    st.emit("gpsimd", lambda e, dd=dst, cc=c, ww=w, so=src_off, ss=s: e.dma_start(
                        out=dd[0:ww, cc:cc + 1],
                        in_=ss[0:1, so + cc * 128:so + cc * 128 + ww]), sGP, 16)

            for dram, dst in ((self.b_dq, self.bdq_c), (self.qng, self.qng_c),
                              (self.qnb, self.qnb_c)):
                st.emit("gpsimd", lambda e, dd=dst: e.memset(dd[:], 0.0), sGE)
                vec_load(dram, DQ)
                col_store(dst, DQ, NCQ)
            st.emit("gpsimd", lambda e: e.memset(self.bdkv_c[:], 0.0), sGE)
            vec_load(self.b_dkv, DCKV)
            col_store(self.bdkv_c, DCKV, 9)
            for dram, dst in ((self.kvg, self.kvg_c), (self.kvb, self.kvb_c)):
                vec_load(dram, DKV)
                col_store(dst, DKV, NCKV)
            vec_load(self.b_ukv, DKV + D)
            st.wait("gpsimd", sLD)
            for co in range(NCKV):
                st.emit("gpsimd", lambda e, cc=co: e.dma_start(
                    out=self.buk_c[0:64, cc:cc + 1],
                    in_=vrow[0:1, 384 * cc:384 * cc + 64]), sGP, 16)
                st.emit("gpsimd", lambda e, cc=co: e.dma_start(
                    out=self.buk_c[64:128, cc:cc + 1],
                    in_=vrow[0:1, 384 * cc + 192:384 * cc + 192 + 64]), sGP, 16)
            st.wait("vector", sLD)
            st.emit("vector", lambda e: e.tensor_copy(
                self.bukvv_r[:], AP(vrow, 64, [[DKV + D, 1], [192, H], [1, 128]])), sDV)

            rc = lambda t, w, p=1: AP(t, 64, [[w, p], [128, H], [1, 64]])
            rp = lambda t, w, p=1: AP(t, 96, [[w, p], [128, H], [-32, 2], [1, 32]])
            vec_load(self.b_uq, D)
            st.wait("vector", sLD)
            st.emit("vector", lambda e: e.tensor_copy(vrow2[0:1, :], vrow[0:1, 0:D]), sDV)
            st.wait("vector", sDV)
            st.emit("vector", lambda e: e.tensor_tensor(
                rc(vrow2, D), rc(vrow, DKV + D), rc(self.cosr, D), op=ALU.mult), sDV)
            st.emit("vector", lambda e: e.tensor_tensor(
                rc(rscr, D), rp(vrow, DKV + D), rc(self.sinr, D), op=ALU.mult), sDV)
            st.wait("vector", sDV)
            st.emit("vector", lambda e: e.tensor_tensor(
                rc(vrow2, D), rc(vrow2, D), rc(rscr, D), op=ALU.add), sDV)
            col_store(self.buq_c, D, NCD, src=vrow2)
            st.wait("sync", sGP)
            st.flush(nc)

    def p0_weights(self):
        nc = self.nc
        sLD, sGP, sPE, sDV = self.sLD, self.sGP, self.sPE, self.sDV
        st = Streams()
        with ExitStack() as p0:
            stage = [p0.enter_context(nc.sbuf_tensor("p0st%d" % i, [128, D], F32, side="right"))
                     for i in range(2)]
            rot = p0.enter_context(nc.sbuf_tensor("p0rot", [128, D], F32, side="right"))
            rots = p0.enter_context(nc.sbuf_tensor("p0rots", [128, D], F32, side="right"))
            rots2 = p0.enter_context(nc.sbuf_tensor("p0rots2", [128, D], F32, side="right"))
            spb = p0.enter_context(nc.sbuf_tensor("p0spb", [128, D], BF16, side="right"))
            ptr = [p0.enter_context(nc.psum_tensor("p0ps%d" % i, [128, 512], F32))
                   for i in range(2)]
            stage_free = [0, 0]
            ps_free = [0, 0]
            sLDs = [self.sLDa, self.sLDb]
            cur = [0]

            def load_stripe(dram, off_elems, rh, cols):
                i = cur[0] % 2
                cur[0] += 1
                b = stage[i]
                st.wait("sync", sPE, stage_free[i])
                st.emit("sync", lambda e, d=dram, ro=off_elems, r=rh, C=cols, bb=b: e.dma_start(
                    out=bb[0:r, 0:C], in_=AP(d, ro, [[C, r], [1, C]])), sLDs[i], 16)
                return i

            def tr(i, c0, rh, cw, evac):
                k = sPE.n % 2
                p = ptr[k]
                st.wait("tensor", sLDs[i], sLDs[i].n)
                st.wait("tensor", sDV, ps_free[k])
                st.emit("tensor", lambda e, bb=stage[i], pp=p, r=rh, cc=c0, w=cw: e.transpose(
                    pp[0:w, 0:r], bb[0:r, cc:cc + w], self.identf[0:r, 0:r]), sPE)
                st.wait("vector", sPE)
                evac(p)
                ps_free[k] = sDV.n
                stage_free[i] = sPE.n

            def cp_evac(tgt, coff, rh, cw=128):
                def f(p):
                    st.emit("vector", lambda e, pp=p, t=tgt, co=coff, r=rh, w=cw: e.tensor_copy(
                        t[0:w, co:co + r], pp[0:w, 0:r]), sDV)
                return f

            def simple_w(dram, R, C, tgt, blk):
                for ro in range((R + 127) // 128):
                    rh = min(128, R - ro * 128)
                    i = load_stripe(dram, ro * 128 * C, rh, C)
                    for cc in range(C // 128):
                        tr(i, cc * 128, rh, 128, cp_evac(tgt, cc * blk + ro * 128, rh))

            simple_w(self.w_dq, DQ, D, self.WdqT, DQ)
            simple_w(self.w_dkv, DCKV, D, self.WdkvT, DCKV)
            for hh in range(H):
                i = load_stripe(self.w_ukv, (192 * hh) * DKV, 64, DKV)
                for cc in range(NCKV):
                    tr(i, cc * 128, 64, 128, cp_evac(self.WukT, cc * DKV + 64 * hh, 64))
            for hh in range(H):
                i = load_stripe(self.w_ukv, (192 * hh + 64) * DKV, 128, DKV)
                for cc in range(NCKV):
                    tr(i, cc * 128, 128, 128, cp_evac(self.WuvT, cc * D + 128 * hh, 128))

            rc = lambda t, w, p: AP(t, 64, [[w, p], [128, H], [1, 64]])
            rp = lambda t, w, p: AP(t, 96, [[w, p], [128, H], [-32, 2], [1, 32]])
            for ccq in range(NCQ):
                cw = min(128, DQ - ccq * 128)
                for ro in range(NCD):
                    i = load_stripe(self.w_uq, ro * 128 * DQ, 128, DQ)
                    tr(i, ccq * 128, 128, cw, cp_evac(rot, ro * 128, 128, cw))
                st.wait("vector", sDV)
                base = ccq * D
                st.emit("vector", lambda e, w=cw, b=base: e.tensor_copy(
                    AP(self.WuqT, b, [[NCQ * D, w], [128, H], [1, 64]]),
                    AP(rot, 0, [[D, w], [128, H], [1, 64]])), sDV)
                st.emit("vector", lambda e, w=cw: e.tensor_tensor(
                    rc(rots, D, w), rc(rot, D, w), rc(self.cosr, D, w), op=ALU.mult), sDV)
                st.emit("vector", lambda e, w=cw: e.tensor_tensor(
                    rc(rots2, D, w), rp(rot, D, w), rc(self.sinr, D, w), op=ALU.mult), sDV)
                st.wait("vector", sDV)
                st.emit("vector", lambda e, w=cw, b=base: e.tensor_tensor(
                    AP(self.WuqT, b + 64, [[NCQ * D, w], [128, H], [1, 64]]),
                    rc(rots, D, w), rc(rots2, D, w), op=ALU.add), sDV)
                st.wait("vector", sDV)

            def spill_stripes(dram, sp, nro, spill_ap_fn):
                for ro in range(nro):
                    i = load_stripe(dram, ro * 128 * D, 128, D)
                    st.wait("vector", sGP)
                    for cc in range(NCD):
                        tr(i, cc * 128, 128, 128, cp_evac(spb, cc * 128, 128))
                    st.wait("gpsimd", sDV)
                    st.emit("gpsimd", lambda e, s=sp, r=ro, f=spill_ap_fn: e.dma_start(
                        out=f(s, r), in_=spb[:]), sGP, 16)

            wsp_ap = lambda s, r: AP(s, r * 128, [[NCD * D, 128], [D, NCD], [1, 128]])
            spill_stripes(self.w_o, self.wo_sp, NCD, wsp_ap)
            spill_stripes(self.w_ffn, self.wffn_sp, NCD, wsp_ap)

            i = load_stripe(self.lora_a, 0, 128, D)
            st.wait("vector", sGP)
            for cc in range(NCD):
                tr(i, cc * 128, 128, 128, cp_evac(spb, cc * 128, 128))
            st.wait("gpsimd", sDV)
            st.emit("gpsimd", lambda e: e.dma_start(
                out=AP(self.acat_sp, 0, [[NCD * 128, 128], [1, NCD * 128]]), in_=spb[:]), sGP, 16)

            i = load_stripe(self.gate_w, 0, NE, D)
            st.wait("vector", sGP)
            for cc in range(NCD):
                tr(i, cc * 128, NE, 128, cp_evac(spb, cc * NE, NE))
            st.wait("gpsimd", sDV)
            st.emit("gpsimd", lambda e: e.dma_start(
                out=AP(self.gate_sp, 0, [[NCD * NE, 128], [1, NCD * NE]]),
                in_=spb[0:128, 0:NCD * NE]), sGP, 16)

            st.wait("vector", sGP)
            for co in range(NCD):
                ii = cur[0] % 2
                cur[0] += 1
                b = stage[ii]
                st.wait("sync", sPE, stage_free[ii])
                st.emit("sync", lambda e, c0=co * 128, bb=b: e.dma_start(
                    out=bb[0:128, 0:ER],
                    in_=AP(self.lora_b, c0 * RLORA,
                           [[RLORA, 128], [D * RLORA, NE], [1, RLORA]])), sLDs[ii], 16)
                tr(ii, 0, 128, 128, cp_evac(spb, co * 128, 128))
            st.wait("gpsimd", sDV)
            st.emit("gpsimd", lambda e: e.dma_start(
                out=AP(self.bcat_sp, 0, [[NCD * 128, 128], [1, NCD * 128]]),
                in_=spb[0:128, 0:NCD * 128]), sGP, 16)
            st.wait("gpsimd", sGP)
            st.flush(nc)

    def p1(self):
        nc = self.nc
        sLD, sGP, sPE, sDV, sAC, sST = self.sLD, self.sGP, self.sPE, self.sDV, self.sAC, self.sST
        st = Streams()
        with ExitStack() as p1:
            def rsb(name, shape, dt=F32):
                return p1.enter_context(nc.sbuf_tensor(name, list(shape), dt, side="right"))

            xT = rsb("xT", [128, NCD * SW], BF16)
            nrmb = rsb("nrmb", [128, NCKV * SW], BF16)
            raw = rsb("raw", [128, NCKV * SW], BF16)
            krr = rsb("krr", [64, SW], BF16)
            xv = rsb("xv", [128, 4 * D], BF16)
            cqn, kvn = nrmb, nrmb
            cqr, ckvr = raw, raw
            xbf, vsb = xv, xv
            sq = [rsb("sq%d" % i, [128, SW], BF16) for i in range(2)]
            s2 = [rsb("s2%d" % i, [128, SW], F32) for i in range(2)]
            qT = [rsb("qT%d" % i, [128, SW], BF16) for i in range(2)]
            knr = [rsb("knr%d" % i, [128, SW], BF16) for i in range(2)]
            murow = rsb("murow", [1, SW], F32)
            rrow = rsb("rrow", [1, SW], F32)
            varow = rsb("varow", [1, SW], F32)
            kr2row = rsb("kr2row", [1, SW], F32)
            tiny = rsb("tiny", [1, 16], F32)
            nrmbf = rsb("nrmbf", [1, 1024], BF16)

            pA = [p1.enter_context(nc.psum_tensor("p1a%d" % i, [128, 512], F32))
                  for i in range(4)]
            pS = p1.enter_context(nc.psum_tensor("p1s", [1, 1024], F32))
            pMu = p1.enter_context(nc.psum_tensor("p1mu", [128, 512], F32))
            pR = p1.enter_context(nc.psum_tensor("p1r", [128, 512], F32))

            self._pa_free = [0, 0, 0, 0]
            self._pa_i = 0
            self._tr_free = [0, 0]
            sQT = [self.sQTa, self.sQTb]
            sKN = [self.sKNa, self.sKNb]
            sKR = self.sKR

            def proj(nchunks, stat_fn, mov, mov_base_fn, out_rows, evac, kdims=None,
                     extra=None):
                k = self._pa_i % 4
                self._pa_i += 1
                p = pA[k]
                st.wait("tensor", sDV, self._pa_free[k])
                for c in range(nchunks):
                    kd = 128 if kdims is None else kdims[c]
                    last = (c == nchunks - 1) and extra is None
                    st.emit("tensor",
                            lambda e, pp=p, s=stat_fn(c), m=mov, mb=mov_base_fn(c), orr=out_rows,
                            r=kd, c0=(c == 0), st_=last: e.matmul(
                                pp[0:orr, 0:SW], s, m[0:r, mb:mb + SW], start=c0, stop=st_,
                                skip_group_check=True),
                            sPE if last else None)
                if extra is not None:
                    extra(p)
                st.wait("vector", sPE)
                evac(p)
                self._pa_free[k] = sDV.n
                return p

            for s in range(NSTRIP):
                st.wait("gpsimd", sPE)
                st.wait("gpsimd", sST)
                st.wait("tensor", sDV)
                sX = [self.sXa, self.sXb]
                for half in range(2):
                    st.emit("gpsimd", lambda e, ss=s, hf=half: e.dma_start(
                        out=xbf[0:128, 0:2 * D] if hf == 0 else xbf[0:128, 2 * D:4 * D],
                        in_=AP(self.x_own, (ss * SW + hf * 256) * D,
                               [[D, 128], [128 * D, 2], [1, D]])), sX[half], 16)
                    st.wait("tensor", sX[half], sX[half].n)
                    for bank in range(4):
                        k = bank % 2
                        p = pMu if k == 0 else pR
                        st.wait("tensor", sDV, self._tr_free[k])
                        for j in range(8):
                            t_idx = bank * 8 + j
                            cx, i = t_idx // 2, t_idx % 2
                            st.emit("tensor", lambda e, pp=p, j_=j, cx_=cx, i_=i, hf=half:
                                    e.transpose(
                                        pp.bitcast(BF16)[0:128, j_ * 128:j_ * 128 + 128],
                                        xbf[0:128, (2 * hf + i_) * D + cx_ * 128:
                                            (2 * hf + i_) * D + cx_ * 128 + 128],
                                        self.identb[:]), sPE if j == 7 else None)
                        st.wait("vector", sPE)
                        st.emit("vector", lambda e, pp=p, b=bank, hf=half: e.tensor_copy(
                            AP(xT, (b * 4) * SW + hf * 256,
                               [[NCD * SW, 128], [SW, 4], [1, 256]]),
                            AP(pp.bitcast(BF16), 0, [[1024, 128], [256, 4], [1, 256]])), sDV)
                        self._tr_free[k] = sDV.n

                st.wait("tensor", sDV)
                for co in range(NCQ):
                    w = 128 if co < 5 else CQ_TAIL

                    def evac_cq(p, co=co, w=w):
                        st.emit("vector", lambda e, pp=p: e.tensor_scalar(
                            cqr[0:w, co * SW:(co + 1) * SW], pp[0:w, 0:SW],
                            self.bdq_c[0:w, co:co + 1], None, op0=ALU.add), sDV)
                        st.wait("vector", sDV)
                        st.emit("vector", lambda e: e.tensor_tensor(
                            sq[co % 2][0:w, 0:SW], cqr[0:w, co * SW:(co + 1) * SW],
                            cqr[0:w, co * SW:(co + 1) * SW], op=ALU.mult), sDV)

                    proj(NCD, lambda c, co=co, w=w:
                         self.WdqT[0:128, c * DQ + co * 128:c * DQ + co * 128 + w],
                         xT, lambda c: c * SW, w, evac_cq)
                    st.wait("tensor", sDV)
                    st.emit("tensor", lambda e, co=co, w=w: e.matmul(
                        pS[0:1, 0:512], self.ones_bf[0:w, :], cqr[0:w, co * SW:(co + 1) * SW],
                        start=(co == 0), stop=(co == NCQ - 1), skip_group_check=True), None)
                    st.emit("tensor", lambda e, co=co, w=w: e.matmul(
                        pS[0:1, 512:1024], self.ones_bf[0:w, :], sq[co % 2][0:w, 0:SW],
                        start=(co == 0), stop=(co == NCQ - 1), skip_group_check=True),
                        sPE if co == NCQ - 1 else None)

                self._ln(st, pS, pMu, pR, murow, rrow, varow, DQ, NCQ, cqr, cqn, s2,
                         self.qng_c, self.qnb_c, CQ_TAIL, SW)

                st.wait("tensor", sDV)
                kdims_q = [128] * 5 + [CQ_TAIL]
                for h in range(H):
                    def evac_q(p, h=h):
                        qb = qT[h % 2]
                        st.wait("vector", sQT[h % 2], sQT[h % 2].n)
                        st.emit("vector", lambda e, pp=p: e.tensor_scalar(
                            qb[:], pp[0:128, 0:SW], self.buq_c[0:128, h:h + 1], None,
                            op0=ALU.add), sDV)
                        st.wait("vector", sDV)
                        st.emit("vector", lambda e: e.tensor_tensor(
                            sq[h % 2][:], qb[:], qb[:], op=ALU.mult), sDV)

                    proj(NCQ, lambda c, h=h:
                         self.WuqT[0:kdims_q[c], c * D + h * 128:c * D + h * 128 + 128],
                         cqn, lambda c: c * SW, 128, evac_q, kdims=kdims_q)
                    st.wait("sync", sDV)
                    st.emit("sync", lambda e, h=h, ss=s: e.dma_start(
                        out=AP(self.q_spill, (h * QNT + ss) * DH * SW, [[SW, 128], [1, SW]]),
                        in_=qT[h % 2][0:128, 0:SW]), sQT[h % 2], 16)
                    st.wait("tensor", sDV)
                    st.emit("tensor", lambda e, h=h: e.matmul(
                        pS[0:1, 0:512], self.ones_bf[:], sq[h % 2][:],
                        start=True, stop=True, skip_group_check=True), sPE)
                    st.wait("vector", sPE)
                    st.emit("vector", lambda e, h=h, ss=s: e.tensor_reduce(
                        self.maxq2[0:1, h * QNT + ss:h * QNT + ss + 1], pS[0:1, 0:SW],
                        axis=AX.X, op=ALU.max), sDV)
                    st.wait("tensor", sDV)

                for co in range(9):
                    w = 128 if co < 8 else 64

                    def evac_ckv(p, co=co, w=w):
                        tgt = (ckvr, co * SW) if co < 8 else (krr, 0)
                        if co == 8:
                            st.wait("vector", sKR, sKR.n)
                        st.emit("vector", lambda e, pp=p, tg=tgt: e.tensor_scalar(
                            tg[0][0:w, tg[1]:tg[1] + SW], pp[0:w, 0:SW],
                            self.bdkv_c[0:w, co:co + 1], None, op0=ALU.add), sDV)
                        st.wait("vector", sDV)
                        st.emit("vector", lambda e, tg=tgt: e.tensor_tensor(
                            sq[co % 2][0:w, 0:SW], tg[0][0:w, tg[1]:tg[1] + SW],
                            tg[0][0:w, tg[1]:tg[1] + SW], op=ALU.mult), sDV)

                    proj(NCD, lambda c, co=co, w=w:
                         self.WdkvT[0:128, c * DCKV + co * 128:c * DCKV + co * 128 + w],
                         xT, lambda c: c * SW, w, evac_ckv)
                    if co < 8:
                        st.wait("tensor", sDV)
                        st.emit("tensor", lambda e, co=co: e.matmul(
                            pS[0:1, 0:512], self.ones_bf[:], ckvr[0:128, co * SW:(co + 1) * SW],
                            start=(co == 0), stop=(co == 7), skip_group_check=True), None)
                        st.emit("tensor", lambda e, co=co: e.matmul(
                            pS[0:1, 512:1024], self.ones_bf[:], sq[co % 2][:],
                            start=(co == 0), stop=(co == 7), skip_group_check=True),
                            sPE if co == 7 else None)

                self._ln(st, pS, pMu, pR, murow, rrow, varow, DKV, NCKV, ckvr, kvn, s2,
                         self.kvg_c, self.kvb_c, 128, SW)

                st.wait("tensor", sDV)
                st.emit("tensor", lambda e: e.matmul(
                    pS[0:1, 0:512], self.onesA[0:64, :] if False else self.ones_bf[0:64, :],
                    sq[0][0:64, 0:SW], start=True, stop=True, skip_group_check=True), sPE)
                st.wait("vector", sPE)
                st.emit("vector", lambda e: e.tensor_copy(kr2row[0:1, :], pS[0:1, 0:512]), sDV)
                st.wait("tensor", sDV)

                st.wait("sync", sDV)
                for h in range(H):
                    st.emit("sync", lambda e, h=h, ss=s: e.dma_start(
                        out=AP(self.contrib, ss * SBLK + (h * DH + 64) * SW,
                               [[SW, 64], [1, SW]]),
                        in_=krr[0:64, 0:SW]), sKR, 16)

                st.wait("tensor", sDV)
                for co in range(NCKV):
                    def evac_k(p, co=co):
                        kb = knr[co % 2]
                        st.wait("vector", sKN[co % 2], sKN[co % 2].n)
                        st.emit("vector", lambda e, pp=p: e.tensor_scalar(
                            kb[:], pp[0:128, 0:SW], self.buk_c[0:128, co:co + 1], None,
                            op0=ALU.add), sDV)
                        st.wait("vector", sDV)
                        st.emit("vector", lambda e: e.tensor_tensor(
                            sq[co % 2][:], kb[:], kb[:], op=ALU.mult), sDV)

                    proj(NCKV, lambda c, co=co:
                         self.WukT[0:128, c * DKV + co * 128:c * DKV + co * 128 + 128],
                         kvn, lambda c: c * SW, 128, evac_k)
                    st.wait("sync", sDV)
                    for hf in range(2):
                        h = 2 * co + hf
                        st.emit("sync", lambda e, h=h, hf=hf, co=co, ss=s: e.dma_start(
                            out=AP(self.contrib, ss * SBLK + h * DH * SW,
                                   [[SW, 64], [1, SW]]),
                            in_=knr[co % 2][hf * 64:hf * 64 + 64, 0:SW]), sKN[co % 2], 16)
                    st.wait("tensor", sDV)
                    st.emit("tensor", lambda e, co=co: e.matmul(
                        pS[0:1, 0:512], self.onesA[:], sq[co % 2][:],
                        start=True, stop=True, skip_group_check=True), None)
                    st.emit("tensor", lambda e, co=co: e.matmul(
                        pS[0:1, 512:1024], self.onesB[:], sq[co % 2][:],
                        start=True, stop=True, skip_group_check=True), sPE)
                    st.wait("vector", sPE)
                    st.emit("vector", lambda e: e.tensor_tensor(
                        varow[0:1, 0:SW], AP(pS, 0, [[1024, 1], [1, SW]]),
                        kr2row[0:1, 0:SW], op=ALU.add), sDV)
                    st.emit("vector", lambda e: e.tensor_tensor(
                        murow[0:1, 0:SW], AP(pS, 512, [[1024, 1], [1, SW]]),
                        kr2row[0:1, 0:SW], op=ALU.add), sDV)
                    st.wait("vector", sDV)
                    st.emit("vector", lambda e: e.tensor_reduce(
                        tiny[0:1, 0:1], varow[0:1, 0:SW], axis=AX.X, op=ALU.max), sDV)
                    st.emit("vector", lambda e: e.tensor_reduce(
                        tiny[0:1, 1:2], murow[0:1, 0:SW], axis=AX.X, op=ALU.max), sDV)
                    st.wait("vector", sDV)
                    for hf in range(2):
                        h = 2 * co + hf
                        st.emit("vector", lambda e, h=h, hf=hf: e.tensor_copy(
                            self.maxk2[0:1, h:h + 1], tiny[0:1, hf:hf + 1]), sDV)
                    st.wait("tensor", sDV)

                for i in range(4):
                    for vs in range(4):
                        def evac_v(p, i=i, vs=vs):
                            st.emit("vector", lambda e, pp=p: e.tensor_copy(
                                vsb[0:128, i * D + vs * 512:i * D + vs * 512 + 512],
                                pp[0:128, 0:512]), sDV)

                        def extra_bias(p, vs=vs):
                            st.emit("tensor", lambda e, pp=p: e.matmul(
                                pp[0:128, 0:512], self.onesrow_bf[0:1, :],
                                self.bukvv_r[0:1, vs * 512:(vs + 1) * 512],
                                start=False, stop=True, skip_group_check=True), sPE)

                        proj(NCKV, lambda c, i=i:
                             kvn[0:128, c * SW + i * 128:c * SW + i * 128 + 128],
                             self.WuvT, lambda c, vs=vs: c * D + vs * 512, 128,
                             evac_v, extra=extra_bias)
                st.wait("sync", sDV)
                for h in range(H):
                    st.emit("sync", lambda e, h=h, ss=s: e.dma_start(
                        out=AP(self.contrib, ss * SBLK + VOFFs + h * SW * DH,
                               [[DH, 128], [128 * DH, 4], [1, DH]]),
                        in_=AP(vsb, h * 128, [[4 * D, 128], [D, 4], [1, DH]])), sST, 16)

                if s == 0:
                    st.emit("gpsimd", lambda e: e.memset(nrmbf[:], 0.0), self.sGE)
                st.wait("vector", sDV)
                st.wait("vector", self.sGE)
                st.wait("vector", sST)
                st.emit("vector", lambda e: e.tensor_scalar(
                    nrmbf[0:1, 0:16], self.maxk2[0:1, :], 1.02, None, op0=ALU.mult), sDV)
                st.wait("sync", sDV)
                st.emit("sync", lambda e, ss=s: e.dma_start(
                    out=AP(self.contrib, ss * SBLK + NOFFs, [[1024, 1], [1, 1024]]),
                    in_=nrmbf[0:1, :]), sST, 16)
                for sx in (sST, sKR, sKN[0], sKN[1]):
                    st.wait("gpsimd", sx, sx.n)
                st.emit("gpsimd", lambda e, ss=s: e.collective_compute(
                    "AllGather", ALU.bypass,
                    replica_groups=[[0, 1, 2, 3], [4, 5, 6, 7]],
                    ins=[AP(self.contrib, ss * SBLK, [[SBLK, 1], [1, SBLK]])],
                    outs=[AP(self.gathered, ss * 4 * SBLK,
                             [[4 * SBLK, 1], [1, 4 * SBLK]])]), self.sCC)

            st.wait("sync", sST)
            st.wait("gpsimd", sGP)
            for sx in (sQT[0], sQT[1],):
                st.wait("gpsimd", sx, sx.n)
            st.flush(nc)

    def _ln(self, st, pS, pMu, pR, murow, rrow, varow, DIM, NCH, raw, dst, s2, g_c, b_c,
            tail, width):
        sPE, sDV, sAC = self.sPE, self.sDV, self.sAC
        inv = 1.0 / DIM
        st.wait("vector", sPE)
        st.emit("vector", lambda e: e.tensor_scalar(
            murow[0:1, 0:width], pS[0:1, 0:512], inv, None, op0=ALU.mult), sDV)
        st.emit("vector", lambda e: e.tensor_scalar(
            varow[0:1, 0:width], pS[0:1, 512:1024], inv, None, op0=ALU.mult), sDV)
        st.wait("vector", sDV)
        st.emit("vector", lambda e: e.tensor_tensor(
            rrow[0:1, 0:width], murow[0:1, 0:width], murow[0:1, 0:width], op=ALU.mult), sDV)
        st.wait("vector", sDV)
        st.emit("vector", lambda e: e.tensor_tensor(
            varow[0:1, 0:width], varow[0:1, 0:width], rrow[0:1, 0:width], op=ALU.subtract), sDV)
        st.wait("scalar", sDV)
        st.emit("scalar", lambda e: e.activation(out=rrow[0:1, 0:width], in_=varow[0:1, 0:width],
                                                 func=AF.Ln, bias=self.epsb[0:1, :],
                                                 scale=1.0), sAC)
        st.wait("scalar", sAC)
        st.emit("scalar", lambda e: e.activation(out=rrow[0:1, 0:width], in_=rrow[0:1, 0:width],
                                                 func=AF.Exp, bias=0.0, scale=-0.5), sAC)
        st.wait("tensor", sAC)
        st.wait("tensor", sDV)
        st.emit("tensor", lambda e: e.matmul(pMu[0:128, 0:width], self.onesrow_f[0:1, :],
                                             murow[0:1, 0:width], start=True, stop=True,
                                             skip_group_check=True), None)
        st.emit("tensor", lambda e: e.matmul(pR[0:128, 0:width], self.onesrow_f[0:1, :],
                                             rrow[0:1, 0:width], start=True, stop=True,
                                             skip_group_check=True), sPE)
        st.wait("vector", sPE)
        for c in range(NCH):
            w = 128 if c < NCH - 1 else tail
            st.emit("vector", lambda e, c=c, w=w: e.tensor_tensor(
                s2[c % 2][0:w, 0:width], raw[0:w, c * width:(c + 1) * width],
                pMu[0:w, 0:width], op=ALU.subtract), sDV)
            st.wait("vector", sDV)
            st.emit("vector", lambda e, c=c, w=w: e.tensor_tensor(
                s2[c % 2][0:w, 0:width], s2[c % 2][0:w, 0:width], pR[0:w, 0:width],
                op=ALU.mult), sDV)
            st.wait("vector", sDV)
            st.emit("vector", lambda e, c=c, w=w: e.tensor_scalar(
                dst[0:w, c * width:(c + 1) * width], s2[c % 2][0:w, 0:width],
                g_c[0:w, c:c + 1], b_c[0:w, c:c + 1], op0=ALU.mult, op1=ALU.add), sDV)
            st.wait("vector", sDV)

    def ag(self):
        nc = self.nc
        st = Streams()
        sGE, sDV, sLD = self.sGE, self.sDV, self.sLD
        with ExitStack() as ages:
            scr = ages.enter_context(nc.sbuf_tensor("agscr", [128, 16 * SW], F32, side="right"))
            mthr_t = ages.enter_context(nc.sbuf_tensor("agmthr", [128, 1], F32, side="right"))
            vr = ages.enter_context(nc.sbuf_tensor("agvr", [1, D], F32, side="right"))
            st.emit("gpsimd", lambda e: e.iota(scr[:], [[-128, 16], [1, SW]],
                                               channel_multiplier=-1,
                                               allow_small_or_imprecise_dtypes=True), sGE)
            st.emit("sync", lambda e: e.dma_start(out=mthr_t[:], in_=self.mthr.ap()), sLD, 16)
            st.wait("vector", sGE)
            st.wait("vector", sLD)
            st.emit("vector", lambda e: e.tensor_scalar(self.mask[:], scr[:], mthr_t[:, 0:1],
                                                        None, op0=ALU.is_ge), sDV)
            for dram, dst, n in ((self.b_o, self.bo_r, D), (self.b_ffn, self.bffn_r, D),
                                 (self.gate_b, self.gateb_r, NE)):
                st.wait("sync", sDV)
                st.emit("sync", lambda e, d=dram, nn=n: e.dma_start(
                    out=vr[0:1, 0:nn], in_=AP(d, 0, [[nn, 1], [1, nn]])), sLD, 16)
                st.wait("vector", sLD)
                st.emit("vector", lambda e, dd=dst, nn=n: e.tensor_copy(
                    dd[0:1, 0:nn], vr[0:1, 0:nn]), sDV)
            st.wait("gpsimd", sDV)
            st.flush(nc)

    def p2(self):
        nc = self.nc
        sPE, sDV, sAC, sCC = self.sPE, self.sDV, self.sAC, self.sCC
        sWL = self.sWL
        sKV = [self.sKVa, self.sKVb]
        sQB = [self.sQBa, self.sQBb]
        sOT = [self.sOTa, self.sOTb]
        st = Streams()
        with ExitStack() as p2:
            def rsb(name, shape, dt=BF16):
                return p2.enter_context(nc.sbuf_tensor(name, list(shape), dt, side="right"))

            qbuf = [rsb("qbuf%d" % i, [128, SW]) for i in range(2)]
            Kbuf = [rsb("Kbuf%d" % i, [128, 1024]) for i in range(2)]
            Vbuf = [rsb("Vbuf%d" % i, [128, 1024]) for i in range(2)]
            pt = [rsb("pt%d" % i, [128, 1024]) for i in range(2)]
            ot_sb = [rsb("ot%d" % i, [128, SW]) for i in range(2)]
            recip = p2.enter_context(nc.sbuf_tensor("recip", [1, SW], F32, side="right"))
            recip_bc = p2.enter_context(nc.sbuf_tensor("recip_bc", [128, SW], F32, side="right"))
            nrm4 = p2.enter_context(nc.sbuf_tensor("nrm4", [1, 64], BF16, side="right"))
            nrm4f = p2.enter_context(nc.sbuf_tensor("nrm4f", [1, 64], F32, side="right"))
            mkrun = p2.enter_context(nc.sbuf_tensor("mkrun", [1, H], F32, side="right"))
            prodr = p2.enter_context(nc.sbuf_tensor("prodr", [1, H], F32, side="right"))
            psS = [p2.enter_context(nc.psum_tensor("p2s%d" % i, [128, 1024], F32))
                   for i in range(2)]
            psO = [p2.enter_context(nc.psum_tensor("p2o%d" % i, [128, SW], F32))
                   for i in range(2)]
            psR = p2.enter_context(nc.psum_tensor("p2r", [1, SW], F32))
            psB = p2.enter_context(nc.psum_tensor("p2b", [128, SW], F32))

            for sp, tgt, cols in ((self.wo_sp, self.WoT, NCD * D),
                                  (self.wffn_sp, self.WffnT, NCD * D),
                                  (self.acat_sp, self.AcatT, NCD * ER),
                                  (self.gate_sp, self.gateT, NCD * NE),
                                  (self.bcat_sp, self.BcatT, D)):
                st.emit("gpsimd", lambda e, s=sp, t=tgt, cc=cols: e.dma_start(
                    out=t[0:128, 0:cc], in_=AP(s, 0, [[cc, 128], [1, cc]])), sWL, 16)

            pe_qfree = [0, 0]
            pe_kvfree = [0, 0]
            ac_psfree = [0, 0]
            pe_ptfree = [0, 0]
            dv_osfree = [0, 0]
            ac_rfree = 0
            dv_bfree = 0

            for kcls in range(1, QNT + 1):
                st.wait("gpsimd", sCC, kcls)
                st.emit("gpsimd", lambda e, k=kcls: e.dma_start(
                    out=nrm4[0:1, 0:64],
                    in_=AP(self.gathered, (k - 1) * 4 * SBLK + NOFFs,
                           [[64, 1], [SBLK, 4], [1, 16]])), self.sGP, 16)
                st.wait("vector", self.sGP)
                st.emit("vector", lambda e: e.tensor_copy(nrm4f[:], nrm4[:]), sDV)
                st.wait("vector", sDV)
                st.emit("vector", lambda e: e.tensor_reduce(
                    prodr[0:1, 0:H], AP(nrm4f, 0, [[64, 1], [1, 16], [16, 4]]),
                    axis=AX.X, op=ALU.max), sDV)
                st.wait("vector", sDV)
                if kcls == 1:
                    st.emit("vector", lambda e: e.tensor_copy(mkrun[:], prodr[0:1, 0:H]), sDV)
                else:
                    st.emit("vector", lambda e: e.tensor_tensor(
                        mkrun[:], mkrun[:], prodr[0:1, 0:H], op=ALU.max), sDV)
                st.wait("vector", sDV)
                st.emit("vector", lambda e, k=kcls: e.tensor_tensor(
                    prodr[0:1, 0:H],
                    AP(self.maxq2, k - 1, [[H * QNT, 1], [QNT, H]]),
                    mkrun[0:1, 0:H], op=ALU.mult), sDV)
                st.wait("scalar", sDV)
                st.emit("scalar", lambda e: e.activation(out=prodr[:], in_=prodr[:],
                                                         func=AF.Ln, bias=self.epsb[0:1, :],
                                                         scale=1.0), sAC)
                st.wait("scalar", sAC)
                st.emit("scalar", lambda e: e.activation(out=prodr[:], in_=prodr[:],
                                                         func=AF.Exp, bias=0.0,
                                                         scale=0.5), sAC)
                st.wait("vector", sAC)
                st.emit("vector", lambda e: e.tensor_scalar(
                    prodr[:], prodr[:], ATT_SCALE, 80.0, op0=ALU.mult, op1=ALU.min), sDV)
                st.wait("vector", sDV)
                st.emit("vector", lambda e: e.tensor_scalar(
                    prodr[:], prodr[:], -1.0, None, op0=ALU.mult), sDV)
                st.wait("tensor", sDV)
                st.wait("tensor", sDV, dv_bfree)
                st.emit("tensor", lambda e: e.matmul(
                    psB[0:128, 0:H], self.onesrow_f[0:1, :], prodr[0:1, 0:H],
                    start=True, stop=True, skip_group_check=True), sPE)
                st.wait("vector", sPE)
                st.emit("vector", lambda e, k=kcls: e.tensor_copy(
                    self.negm_bc[0:128, (k - 1) * H:k * H], psB[0:128, 0:H]), sDV)
                dv_bfree = self.sDV.n
                st.wait("tensor", sDV)

                for h in range(H):
                    sidx = (kcls - 1) * H + h
                    psw = sidx % 2
                    nblk = 2 * kcls
                    npair = 4 * nblk

                    st.wait("sync", sPE, pe_qfree[psw])
                    st.emit("sync", lambda e, h=h, k=kcls, b=qbuf[psw]: e.dma_start(
                        out=b[:], in_=AP(self.q_spill, (h * QNT + k - 1) * DH * SW,
                                         [[SW, 128], [1, SW]])), sQB[psw], 16)
                    qld = sQB[psw].n

                    kvld = [0] * nblk

                    def emit_load(blk, h=h):
                        pb = blk % 2
                        strip, rank0 = blk // 2, (2 * blk) % 4
                        base = strip * 4 * SBLK + rank0 * SBLK
                        st.wait("sync", sPE, pe_kvfree[pb])
                        st.emit("sync", lambda e, ba=base, b=Kbuf[pb]: e.dma_start(
                            out=AP(b, 0, [[1024, 128], [512, 2], [1, 512]]),
                            in_=AP(self.gathered, ba + KOFFs + h * DH * SW,
                                   [[SW, 128], [SBLK, 2], [1, 512]])), sKV[pb], 16)
                        for pc in range(2):
                            st.emit("sync", lambda e, ba=base, b=Vbuf[pb], pc=pc:
                                    e.dma_start(
                                        out=AP(b, pc * 512,
                                               [[1024, 128], [128, 4], [1, 128]]),
                                        in_=AP(self.gathered,
                                               ba + pc * SBLK + VOFFs + h * SW * DH,
                                               [[DH, 128], [128 * DH, 4], [1, DH]])),
                                    sKV[pb], 16)
                        kvld[blk] = sKV[pb].n

                    emit_load(0)
                    if nblk > 1:
                        emit_load(1)

                    pend = [None]

                    def flush_rpv():
                        if pend[0] is not None:
                            pend[0]()
                            pend[0] = None

                    for g in range(npair):
                        blk, gb = g // 4, g % 4
                        pb = blk % 2
                        gp = g % 2
                        masked = (blk >= nblk - 2)
                        first = (g == 0)
                        last = (g == npair - 1)

                        if gb == 0:
                            st.wait("tensor", sKV[pb], kvld[blk])
                        if first:
                            st.wait("tensor", sQB[psw], qld)
                        st.wait("tensor", sAC, ac_psfree[gp])
                        for j in range(2):
                            cc = 2 * gb + j
                            st.emit("tensor", lambda e, pS=psS[gp], Kb=Kbuf[pb],
                                    qb=qbuf[psw], cc=cc, j=j: e.matmul(
                                        pS[0:128, j * SW:(j + 1) * SW],
                                        Kb[0:128, cc * 128:cc * 128 + 128], qb[:],
                                        start=True, stop=True, skip_group_check=True),
                                    sPE if j == 1 else None)
                        spe_s = sPE.n

                        st.wait("scalar", sPE, spe_s)
                        st.wait("scalar", sPE, pe_ptfree[gp])
                        st.emit("scalar", lambda e, pS=psS[gp], p=pt[gp], sidx=sidx:
                                e.activation(out=p[:], in_=pS[0:128, 0:1024], func=AF.Exp,
                                             bias=self.negm_bc[0:128, sidx:sidx + 1],
                                             scale=ATT_SCALE), sAC)
                        ac_psfree[gp] = sAC.n
                        ac_exp = sAC.n

                        dv_mask = None
                        if masked:
                            jp = g - (npair - 8)
                            st.wait("vector", sAC, ac_exp)
                            st.emit("vector", lambda e, p=pt[gp], jp=jp: e.tensor_tensor(
                                p[:], p[:], self.mask[0:128, jp * 1024:(jp + 1) * 1024],
                                op=ALU.mult), sDV)
                            dv_mask = sDV.n

                        flush_rpv()

                        def mk_rpv(g=g, gp=gp, pb=pb, ac_exp=ac_exp, dv_mask=dv_mask,
                                   first=first, last=last, psw=psw):
                            def emit():
                                if dv_mask is not None:
                                    st.wait("tensor", sDV, dv_mask)
                                else:
                                    st.wait("tensor", sAC, ac_exp)
                                if first:
                                    st.wait("tensor", sDV, dv_osfree[psw])
                                    st.wait("tensor", sAC, ac_rfree)
                                for j in range(2):
                                    cc = 2 * (g % 4) + j
                                    st.emit("tensor", lambda e, p=pt[gp], j=j: e.matmul(
                                        psR[0:1, :], self.ones_bf[:],
                                        p[0:128, j * SW:(j + 1) * SW],
                                        start=(first and j == 0), stop=(last and j == 1),
                                        skip_group_check=True), None)
                                    st.emit("tensor", lambda e, p=pt[gp], Vb=Vbuf[pb],
                                            cc=cc, j=j: e.matmul(
                                                psO[psw][:],
                                                Vb[0:128, cc * 128:cc * 128 + 128],
                                                p[0:128, j * SW:(j + 1) * SW],
                                                start=(first and j == 0),
                                                stop=(last and j == 1),
                                                skip_group_check=True),
                                            sPE if j == 1 else None)
                                pe_ptfree[gp] = sPE.n
                                if g % 4 == 3 or last:
                                    pe_kvfree[pb] = sPE.n
                                if last:
                                    pe_qfree[psw] = sPE.n
                            return emit

                        pend[0] = mk_rpv()
                        if g >= 1 and (g - 1) % 4 == 3:
                            nxt = (g - 1) // 4 + 2
                            if nxt < nblk:
                                emit_load(nxt)
                    flush_rpv()
                    spe_sweep = sPE.n

                    st.wait("scalar", sPE, spe_sweep)
                    st.emit("scalar", lambda e: e.activation(
                        out=recip[0:1, :], in_=psR[0:1, :], func=AF.Ln,
                        bias=0.0, scale=1.0), sAC)
                    ac_rfree = sAC.n
                    st.wait("scalar", sAC)
                    st.emit("scalar", lambda e: e.activation(
                        out=recip[0:1, :], in_=recip[0:1, :], func=AF.Exp,
                        bias=0.0, scale=-1.0), sAC)
                    ac_rec = sAC.n
                    st.wait("tensor", sAC, ac_rec)
                    st.wait("tensor", sDV, dv_bfree)
                    st.emit("tensor", lambda e: e.matmul(
                        psB[:], self.onesrow_f[0:1, :], recip[0:1, :],
                        start=True, stop=True, skip_group_check=True), sPE)
                    spe_bc = sPE.n
                    st.wait("vector", sPE, spe_bc)
                    st.emit("vector", lambda e: e.tensor_copy(recip_bc[:], psB[:]), sDV)
                    dv_bfree = self.sDV.n
                    st.wait("vector", sDV, dv_bfree)
                    st.wait("vector", sOT[psw], sOT[psw].n)
                    st.emit("vector", lambda e, pO=psO[psw], o=ot_sb[psw]: e.tensor_tensor(
                        o[:], pO[:], recip_bc[:], op=ALU.mult), sDV)
                    dv_osfree[psw] = self.sDV.n
                    st.wait("sync", sDV, dv_osfree[psw])
                    st.emit("sync", lambda e, h=h, k=kcls, o=ot_sb[psw]: e.dma_start(
                        out=AP(self.ot_spill, (h * QNT + k - 1) * DH * SW,
                               [[SW, 128], [1, SW]]),
                        in_=o[:]), sOT[psw], 16)

            st.wait("gpsimd", sOT[0], sOT[0].n)
            st.wait("gpsimd", sOT[1], sOT[1].n)
            st.wait("gpsimd", sWL, sWL.n)
            st.flush(nc)

    def p3(self):
        nc = self.nc
        sPE, sDV, sAC, sST = self.sPE, self.sDV, self.sAC, self.sST
        sOL = [self.sOLa, self.sOLb]
        sXT = self.sXT
        st = Streams()
        with ExitStack() as p3:
            def rsb(name, shape, dt=F32):
                return p3.enter_context(nc.sbuf_tensor(name, list(shape), dt, side="right"))

            otb = [rsb("otb%d" % i, [128, NCD * 128], BF16) for i in range(2)]
            xt = rsb("xt", [128, D])
            ysb = rsb("ysb", [128, D])
            x1f = rsb("x1f", [128, D])
            x1b = rsb("x1b", [128, D], BF16)
            x1T = rsb("x1T", [128, NCD * 128], BF16)
            xaw = rsb("xaw", [128, ER], BF16)
            xawT = rsb("xawT", [128, ER], BF16)
            stat = rsb("stat", [128, 24])
            mv = rsb("mv", [128, 2])
            rln = rsb("rln", [128, 1])
            eq1 = rsb("eq1", [128, 8])
            l2t = rsb("l2t", [128, 8])
            w8t = rsb("w8t", [128, 8])
            t1 = rsb("t1", [128, 1])
            t2 = rsb("t2", [128, 1])
            te = rsb("te", [128, 1])
            tg1 = rsb("tg1", [128, 1])
            tg2 = rsb("tg2", [128, 1])
            pBig = p3.enter_context(nc.psum_tensor("p3big", [128, D], F32))
            psG = p3.enter_context(nc.psum_tensor("p3g", [128, ER], F32))
            psL = p3.enter_context(nc.psum_tensor("p3l", [128, NE], F32))
            psT = [p3.enter_context(nc.psum_tensor("p3t%d" % i, [128, 1024], BF16))
                   for i in range(2)]

            pe_otfree = [0, 0]
            dv_pbig = 0
            dv_xt = 0

            st.wait("tensor", self.sWL, self.sWL.n)
            for i in range(NT * 2):
                par = i % 2
                lt, hf = i // 2, i % 2

                st.wait("sync", sPE, pe_otfree[par])
                for h in range(H):
                    st.emit("sync", lambda e, h=h, i=i, b=otb[par]: e.dma_start(
                        out=b[0:128, h * 128:h * 128 + 128],
                        in_=AP(self.ot_spill, (h * QNT + i // 4) * DH * SW + (i % 4) * 128,
                               [[SW, 128], [1, 128]])), sOL[par], 16)
                st.wait("sync", sDV, dv_xt)
                st.emit("sync", lambda e, i=i: e.dma_start(
                    out=xt[:], in_=AP(self.x_own, i * 128 * D, [[D, 128], [1, D]])),
                    sXT, 16)
                xtld = self.sXT.n

                st.wait("tensor", sOL[par], sOL[par].n)
                st.wait("tensor", sDV, dv_pbig)
                for cd in range(NCD):
                    for vs in range(4):
                        st.emit("tensor", lambda e, cd=cd, vs=vs, b=otb[par]: e.matmul(
                            pBig[0:128, vs * 512:(vs + 1) * 512],
                            b[0:128, cd * 128:cd * 128 + 128],
                            self.WoT[0:128, cd * D + vs * 512:cd * D + vs * 512 + 512],
                            start=(cd == 0), stop=False, skip_group_check=True), None)
                for vs in range(4):
                    st.emit("tensor", lambda e, vs=vs: e.matmul(
                        pBig[0:128, vs * 512:(vs + 1) * 512], self.onesrow_bf[0:1, :],
                        self.bo_r[0:1, vs * 512:(vs + 1) * 512],
                        start=False, stop=True, skip_group_check=True),
                        sPE if vs == 3 else None)
                spe_wo = self.sPE.n
                pe_otfree[par] = spe_wo

                st.wait("vector", sPE, spe_wo)
                st.wait("vector", sXT, xtld)
                st.wait("vector", sST, self.sST.n)
                st.emit("vector", lambda e: e.tensor_tensor(ysb[:], pBig[:], xt[:],
                                                            op=ALU.add), sDV)
                dv_xt = self.sDV.n
                st.wait("vector", sDV)
                for cst in range(4):
                    st.emit("vector", lambda e, cst=cst: e.bn_stats(
                        stat[0:128, cst * 6:cst * 6 + 6],
                        ysb[0:128, cst * 512:(cst + 1) * 512]), sDV)
                st.wait("vector", sDV)
                st.emit("vector", lambda e: e.bn_aggr(mv[:], stat[:]), sDV)
                st.wait("scalar", sDV, self.sDV.n)
                st.emit("scalar", lambda e: e.activation(out=rln[:], in_=mv[0:128, 1:2],
                                                         func=AF.Ln, bias=self.epsb[:],
                                                         scale=1.0), sAC)
                st.wait("scalar", sAC)
                st.emit("scalar", lambda e: e.activation(out=rln[:], in_=rln[:],
                                                         func=AF.Exp, bias=0.0,
                                                         scale=-0.5), sAC)
                st.wait("vector", sAC)
                st.emit("vector", lambda e: e.tensor_scalar(
                    x1f[:], ysb[:], mv[0:128, 0:1], rln[:], op0=ALU.subtract,
                    op1=ALU.mult), sDV)
                st.wait("vector", sDV)
                st.emit("vector", lambda e: e.tensor_copy(x1b[:], x1f[:]), sDV)
                dv_x1 = self.sDV.n
                if self.taps:
                    st.wait("gpsimd", sDV, self.sDV.n)
                    st.emit("gpsimd", lambda e, i=i: e.dma_start(
                        out=AP(self.dbg2, i * 128 * D, [[D, 128], [1, D]]),
                        in_=x1b[:]), self.sGP, 16)
                    st.wait("vector", self.sGP, self.sGP.n)

                st.wait("tensor", sDV, dv_x1)
                for bank in range(2):
                    for j in range(8):
                        cd = bank * 8 + j
                        st.emit("tensor", lambda e, cd=cd, j=j, p=psT[bank]: e.transpose(
                            p[0:128, j * 128:j * 128 + 128],
                            x1b[0:128, cd * 128:cd * 128 + 128], self.identb[:]),
                            sPE if j == 7 else None)
                    st.wait("vector", sPE)
                    st.emit("vector", lambda e, bank=bank, p=psT[bank]: e.tensor_copy(
                        x1T[0:128, bank * 1024:(bank + 1) * 1024], p[:]), sDV)
                dv_x1T = self.sDV.n

                st.wait("tensor", sDV, dv_x1T)
                for cd in range(NCD):
                    for vs in range(4):
                        st.emit("tensor", lambda e, cd=cd, vs=vs: e.matmul(
                            pBig[0:128, vs * 512:(vs + 1) * 512],
                            x1T[0:128, cd * 128:cd * 128 + 128],
                            self.WffnT[0:128, cd * D + vs * 512:cd * D + vs * 512 + 512],
                            start=(cd == 0), stop=False, skip_group_check=True), None)
                    st.emit("tensor", lambda e, cd=cd: e.matmul(
                        psL[0:128, 0:NE], x1T[0:128, cd * 128:cd * 128 + 128],
                        self.gateT[0:128, cd * NE:(cd + 1) * NE],
                        start=(cd == 0), stop=False, skip_group_check=True), None)
                    st.emit("tensor", lambda e, cd=cd: e.matmul(
                        psG[0:128, 0:ER], x1T[0:128, cd * 128:cd * 128 + 128],
                        self.AcatT[0:128, cd * ER:(cd + 1) * ER],
                        start=(cd == 0), stop=(cd == NCD - 1), skip_group_check=True), None)
                for vs in range(4):
                    st.emit("tensor", lambda e, vs=vs: e.matmul(
                        pBig[0:128, vs * 512:(vs + 1) * 512], self.onesrow_bf[0:1, :],
                        self.bffn_r[0:1, vs * 512:(vs + 1) * 512],
                        start=False, stop=False, skip_group_check=True), None)
                st.emit("tensor", lambda e: e.matmul(
                    psL[0:128, 0:NE], self.onesrow_bf[0:1, :], self.gateb_r[0:1, :],
                    start=False, stop=True, skip_group_check=True), sPE)
                spe_gx = self.sPE.n

                st.wait("vector", sPE, spe_gx)
                st.emit("vector", lambda e: e.tensor_reduce(
                    t1[:], psL[0:128, 0:NE], axis=AX.X, op=ALU.max), sDV)
                st.wait("vector", sDV)
                st.emit("vector", lambda e: e.tensor_scalar(
                    eq1[:], psL[0:128, 0:NE], t1[:], None, op0=ALU.is_equal), sDV)
                st.wait("vector", sDV)
                st.emit("vector", lambda e: e.scalar_tensor_tensor(
                    l2t[:], eq1[:], -1e30, psL[0:128, 0:NE], op0=ALU.mult,
                    op1=ALU.add), sDV)
                st.wait("vector", sDV)
                st.emit("vector", lambda e: e.tensor_reduce(
                    t2[:], l2t[:], axis=AX.X, op=ALU.max), sDV)
                st.wait("vector", sDV)
                st.emit("vector", lambda e: e.tensor_tensor(
                    te[:], t2[:], t1[:], op=ALU.subtract), sDV)
                st.wait("scalar", sDV, self.sDV.n)
                st.emit("scalar", lambda e: e.activation(out=te[:], in_=te[:], func=AF.Exp,
                                                         bias=0.0, scale=1.0), sAC)
                st.wait("vector", sAC)
                st.emit("vector", lambda e: e.tensor_scalar(
                    tg1[:], te[:], 1.0, None, op0=ALU.add), sDV)
                st.wait("vector", sDV)
                st.emit("vector", lambda e: e.reciprocal(tg1[:], tg1[:]), sDV)
                st.wait("vector", sDV)
                st.emit("vector", lambda e: e.tensor_scalar(
                    tg2[:], tg1[:], -1.0, 1.0, op0=ALU.mult, op1=ALU.add), sDV)
                st.emit("vector", lambda e: e.tensor_scalar(
                    eq1[:], eq1[:], tg1[:], None, op0=ALU.mult), sDV)
                st.emit("vector", lambda e: e.tensor_scalar(
                    l2t[:], l2t[:], t2[:], None, op0=ALU.is_equal), sDV)
                st.wait("vector", sDV)
                st.emit("vector", lambda e: e.scalar_tensor_tensor(
                    w8t[:], l2t[:], tg2[:], eq1[:], op0=ALU.mult, op1=ALU.add), sDV)
                st.wait("vector", sDV)
                st.emit("vector", lambda e: e.tensor_tensor(
                    xaw[:].rearrange("p (e r) -> p e r", r=RLORA),
                    AP(psG, 0, [[ER, 128], [16, NE], [1, RLORA]]),
                    w8t[:].unsqueeze(2).broadcast_to([128, NE, RLORA]),
                    op=ALU.mult), sDV)
                dv_xaw = self.sDV.n

                st.wait("tensor", sDV, dv_xaw)
                st.emit("tensor", lambda e: e.transpose(
                    psT[0][0:128, 0:128], xaw[:], self.identb[:]), sPE)
                st.wait("vector", sPE)
                st.emit("vector", lambda e: e.tensor_copy(
                    xawT[:], psT[0][0:128, 0:128]), sDV)
                st.wait("tensor", sDV)
                for vs in range(4):
                    st.emit("tensor", lambda e, vs=vs: e.matmul(
                        pBig[0:128, vs * 512:(vs + 1) * 512], xawT[:],
                        self.BcatT[0:128, vs * 512:(vs + 1) * 512],
                        start=False, stop=True, skip_group_check=True),
                        sPE if vs == 3 else None)
                spe_ffn = self.sPE.n

                st.wait("vector", sPE, spe_ffn)
                st.emit("vector", lambda e: e.tensor_tensor(ysb[:], pBig[:], x1f[:],
                                                            op=ALU.add), sDV)
                dv_pbig = self.sDV.n
                if self.taps:
                    st.wait("gpsimd", sDV, self.sDV.n)
                    st.emit("gpsimd", lambda e, i=i: e.dma_start(
                        out=AP(self.dbg1, i * 128 * D, [[D, 128], [1, D]]),
                        in_=ysb[:]), self.sGP, 16)
                    st.emit("gpsimd", lambda e, i=i: e.dma_start(
                        out=AP(self.dbg3, i * 128 * ER, [[ER, 128], [1, ER]]),
                        in_=xaw[:]), self.sGP, 16)
                    st.wait("vector", self.sGP, self.sGP.n)
                st.wait("vector", sDV)
                for cst in range(4):
                    st.emit("vector", lambda e, cst=cst: e.bn_stats(
                        stat[0:128, cst * 6:cst * 6 + 6],
                        ysb[0:128, cst * 512:(cst + 1) * 512]), sDV)
                st.wait("vector", sDV)
                st.emit("vector", lambda e: e.bn_aggr(mv[:], stat[:]), sDV)
                st.wait("scalar", sDV, self.sDV.n)
                st.emit("scalar", lambda e: e.activation(out=rln[:], in_=mv[0:128, 1:2],
                                                         func=AF.Ln, bias=self.epsb[:],
                                                         scale=1.0), sAC)
                st.wait("scalar", sAC)
                st.emit("scalar", lambda e: e.activation(out=rln[:], in_=rln[:],
                                                         func=AF.Exp, bias=0.0,
                                                         scale=-0.5), sAC)
                st.wait("vector", sAC)
                st.emit("vector", lambda e: e.tensor_scalar(
                    ysb[:], ysb[:], mv[0:128, 0:1], rln[:], op0=ALU.subtract,
                    op1=ALU.mult), sDV)
                st.wait("sync", sDV)
                st.emit("sync", lambda e, i=i: e.dma_start(
                    out=AP(self.out, i * 128 * D, [[D, 128], [1, D]]), in_=ysb[:]),
                    sST, 16)

            st.wait("sync", sST)
            st.wait("gpsimd", sST, self.sST.n)
            st.flush(nc)


def _rotary_tables():
    j = np.arange(32, dtype=np.float64)
    f = (1.0 / 10000.0) ** ((2.0 * j) / 64.0)
    h = np.arange(H, dtype=np.float64)
    ang = h[:, None] * f[None, :]
    c, s = np.cos(ang), np.sin(ang)
    C = np.ones(D, dtype=np.float64)
    S = np.zeros(D, dtype=np.float64)
    for hh in range(H):
        C[128 * hh + 64:128 * hh + 96] = c[hh]
        C[128 * hh + 96:128 * hh + 128] = c[hh]
        S[128 * hh + 64:128 * hh + 96] = s[hh]
        S[128 * hh + 96:128 * hh + 128] = -s[hh]
    return (np.ascontiguousarray(np.broadcast_to(C.astype(np.float32), (128, D))),
            np.ascontiguousarray(np.broadcast_to(S.astype(np.float32), (128, D))))


def make_in_maps(inputs):
    C, S = _rotary_tables()
    x = np.asarray(inputs["L0_attn_x"], dtype=np.float32)
    f32c = lambda k: np.ascontiguousarray(np.asarray(inputs[k], dtype=np.float32))
    shared = {
        "w_dq": f32c("L0_attn_W_d_q"), "b_dq": f32c("L0_attn_b_d_q"),
        "qng": f32c("L0_attn_q_norm_g"), "qnb": f32c("L0_attn_q_norm_b"),
        "w_uq": f32c("L0_attn_W_u_q"), "b_uq": f32c("L0_attn_b_u_q"),
        "w_dkv": f32c("L0_attn_W_d_kv"), "b_dkv": f32c("L0_attn_b_d_kv"),
        "kvg": f32c("L0_attn_kv_norm_g"), "kvb": f32c("L0_attn_kv_norm_b"),
        "w_ukv": f32c("L0_attn_W_u_kv"), "b_ukv": f32c("L0_attn_b_u_kv"),
        "w_o": f32c("L0_attn_W_o"), "b_o": f32c("L0_attn_b_o"),
        "w_ffn": f32c("L0_ffn_W"), "b_ffn": f32c("L0_ffn_b"),
        "lora_a": f32c("L0_ffn_lora_A"), "lora_b": f32c("L0_ffn_lora_B"),
        "gate_w": f32c("L0_ffn_gate_w"), "gate_b": f32c("L0_ffn_gate_b"),
        "cosrep": C, "sinrep": S,
    }
    in_maps = []
    for c in range(8):
        b, rho = c // 4, c % 4
        tiles = [x[b, 256 * (4 * k + rho):256 * (4 * k + rho) + 256] for k in range(NT)]
        m = dict(shared)
        m["x_own"] = np.ascontiguousarray(np.concatenate(tiles, axis=0))
        m["mthr"] = np.full((128, 1), -512.0 * rho, dtype=np.float32)
        in_maps.append(m)
    return in_maps


def assemble_out(results):
    y = np.empty((B, T, D), dtype=np.float32)
    for c in range(8):
        b, rho = c // 4, c % 4
        o = np.asarray(results[c]["out"])
        for k in range(NT):
            y[b, 256 * (4 * k + rho):256 * (4 * k + rho) + 256] = o[256 * k:256 * k + 256]
    return y


def kernel(**inputs):
    mk = MK()
    nc = mk.build()
    in_maps = make_in_maps(inputs)
    res = run_bass_kernel_spmd(nc, in_maps, core_ids=list(range(8)))
    return assemble_out(res.results)


# version 08
# speedup vs baseline: 1.1346x; 1.1346x over previous
import sys
if "/opt/trn_rl_repo" not in sys.path:
    sys.path.insert(0, "/opt/trn_rl_repo")

from contextlib import ExitStack
import math
import numpy as np

import concourse.bass as bass
import concourse.mybir as mybir
from concourse.bass_utils import run_bass_kernel_spmd

F32 = mybir.dt.float32
BF16 = mybir.dt.bfloat16
AF = mybir.ActivationFunctionType
ALU = mybir.AluOpType
AX = mybir.AxisListType

B, T, D, H, DH = 2, 8192, 2048, 16, 128
DQ, DKV, DKR = 682, 1024, 64
DCKV = DKV + DKR
NCQ = 6
CQ_TAIL = DQ - 5 * 128
NCKV = 8
NCD = 16
TCORE = 2048
NT = 8
NSTRIP = 4
SW = 512
QG = 256
ATT_SCALE = 1.0 / math.sqrt(DH)
EPS = 1e-5
NE, RLORA = 8, 16
ER = NE * RLORA

QNT = 4
K_SBLK = H * DH * SW
V_SBLK = H * SW * DH
SBLK = K_SBLK + V_SBLK + 1024
KOFFs, VOFFs, NOFFs = 0, K_SBLK, K_SBLK + V_SBLK
CONTRIB = NSTRIP * SBLK


def AP(t, off, dims):
    return bass.AP(t, off, [list(d) for d in dims])


class Sem:
    __slots__ = ("h", "n")

    def __init__(self, h):
        self.h = h
        self.n = 0


class Streams:
    NAMES = ("sync", "tensor", "vector", "scalar", "gpsimd")

    def __init__(self):
        self.s = {n: [] for n in self.NAMES}

    def emit(self, eng, fn, sem=None, k=1):
        if sem is None:
            self.s[eng].append(fn)
            return None
        sem.n += k
        self.s[eng].append(lambda e, f=fn, sh=sem.h, kk=k: f(e).then_inc(sh, kk))
        return sem.n

    def wait(self, eng, sem, val=None):
        v = sem.n if val is None else val
        if v > 0:
            self.s[eng].append(lambda e, sh=sem.h, vv=v: e.wait_ge(sh, vv))

    def flush(self, nc):
        with nc.Block() as block:
            for n in self.NAMES:
                ops = self.s[n]
                if ops:
                    getattr(block, n)(lambda e, ops=ops: [f(e) for f in ops])
        self.s = {n: [] for n in self.NAMES}


class MK:
    def __init__(self, taps=(), stop_after=None):
        self.taps = list(taps)
        self.stop_after = stop_after
        self.nc = bass.Bass("TRN2", target_bir_lowering=False)

    def build(self):
        nc = self.nc

        def din(name, shape):
            return nc.dram_tensor(name, list(shape), F32, kind="ExternalInput")

        self.x_own = din("x_own", [TCORE, D])
        self.w_dq = din("w_dq", [DQ, D])
        self.b_dq = din("b_dq", [DQ])
        self.qng = din("qng", [DQ])
        self.qnb = din("qnb", [DQ])
        self.w_uq = din("w_uq", [D, DQ])
        self.b_uq = din("b_uq", [D])
        self.w_dkv = din("w_dkv", [DCKV, D])
        self.b_dkv = din("b_dkv", [DCKV])
        self.kvg = din("kvg", [DKV])
        self.kvb = din("kvb", [DKV])
        self.w_ukv = din("w_ukv", [DKV + D, DKV])
        self.b_ukv = din("b_ukv", [DKV + D])
        self.w_o = din("w_o", [D, D])
        self.b_o = din("b_o", [D])
        self.w_ffn = din("w_ffn", [D, D])
        self.b_ffn = din("b_ffn", [D])
        self.lora_a = din("lora_a", [NE, RLORA, D])
        self.lora_b = din("lora_b", [NE, D, RLORA])
        self.gate_w = din("gate_w", [NE, D])
        self.gate_b = din("gate_b", [NE])
        self.cosrep = din("cosrep", [128, D])
        self.sinrep = din("sinrep", [128, D])
        self.mthr = din("mthr", [128, 1])

        self.out = nc.dram_tensor("out", [TCORE, D], F32, kind="ExternalOutput")

        self.contrib = nc.dram_tensor("contrib", [CONTRIB], BF16)
        self.gathered = nc.dram_tensor("gathered", [4 * CONTRIB], BF16)
        self.q_spill = nc.dram_tensor("q_spill", [H * QNT * DH * SW], BF16)
        self.ot_spill = nc.dram_tensor("ot_spill", [H * QNT * DH * SW], BF16)
        self.wo_sp = nc.dram_tensor("wo_sp", [D * D], BF16)
        self.wffn_sp = nc.dram_tensor("wffn_sp", [D * D], BF16)
        self.acat_sp = nc.dram_tensor("acat_sp", [ER * D], BF16)
        self.bcat_sp = nc.dram_tensor("bcat_sp", [ER * D], BF16)
        self.gate_sp = nc.dram_tensor("gate_sp", [128 * NCD * NE], BF16)
        self.dbg1 = nc.dram_tensor("dbg1", [TCORE * D], BF16)
        self.dbg2 = nc.dram_tensor("dbg2", [TCORE * D], BF16)
        self.dbg3 = nc.dram_tensor("dbg3", [TCORE * ER], BF16)

        es = ExitStack()
        self.es = es

        def sb(name, shape, dt=F32, side="left"):
            return es.enter_context(nc.sbuf_tensor(name, list(shape), dt, side=side))
        self.sb = sb

        with es:
            self.identb = sb("identb", [128, 128], BF16)
            self.identf = sb("identf", [128, 128], F32)
            self.ones_bf = sb("ones_bf", [128, 1], BF16)
            self.onesA = sb("onesA", [128, 1], BF16)
            self.onesB = sb("onesB", [128, 1], BF16)
            self.onesrow_bf = sb("onesrow_bf", [1, 128], BF16)
            self.onesrow_f = sb("onesrow_f", [1, 128], F32)
            self.epsb = sb("epsb", [128, 1], F32)
            self.bdq_c = sb("bdq_c", [128, NCQ], F32)
            self.buq_c = sb("buq_c", [128, NCD], F32)
            self.bdkv_c = sb("bdkv_c", [128, 9], F32)
            self.buk_c = sb("buk_c", [128, NCKV], F32)
            self.qng_c = sb("qng_c", [128, NCQ], F32)
            self.qnb_c = sb("qnb_c", [128, NCQ], F32)
            self.kvg_c = sb("kvg_c", [128, NCKV], F32)
            self.kvb_c = sb("kvb_c", [128, NCKV], F32)
            self.bukvv_r = sb("bukvv_r", [1, D], BF16)
            self.maxq2 = sb("maxq2", [1, H * QNT], F32)
            self.maxk2 = sb("maxk2", [1, H], F32)
            self.negm_bc = sb("negm_bc", [128, QNT * H], F32)

            for nm in ("sLD", "sGP", "sGE", "sPE", "sDV", "sAC", "sST", "sCC",
                       "sLDa", "sLDb", "sQTa", "sQTb", "sKNa", "sKNb", "sKR",
                       "sXa", "sXb", "sKVa", "sKVb", "sQBa", "sQBb", "sOTa", "sOTb",
                       "sWL", "sOLa", "sOLb", "sXT"):
                setattr(self, nm, Sem(es.enter_context(nc.semaphore(nm))))

            cs = ExitStack()
            self.cosr = cs.enter_context(nc.sbuf_tensor("cosr", [128, D], F32, side="right"))
            self.sinr = cs.enter_context(nc.sbuf_tensor("sinr", [128, D], F32, side="right"))
            with nc.named_scope('p0c'):
                self.p0_consts()
            wa = ExitStack()
            self.WdqT = wa.enter_context(nc.sbuf_tensor("WdqT", [128, NCD * DQ], BF16))
            self.WuqT = wa.enter_context(nc.sbuf_tensor("WuqT", [128, NCQ * D], BF16))
            self.WdkvT = wa.enter_context(nc.sbuf_tensor("WdkvT", [128, NCD * DCKV], BF16))
            self.WukT = wa.enter_context(nc.sbuf_tensor("WukT", [128, NCKV * DKV], BF16))
            self.WuvT = wa.enter_context(nc.sbuf_tensor("WuvT", [128, NCKV * D], BF16))
            with nc.named_scope('p0w'):
                self.p0_weights()
            cs.close()
            if self.stop_after == "p0":
                wa.close()
                self._emit_taps()
                return nc
            with nc.named_scope('p1'):
                self.p1()
            wa.close()
            self.mask = sb("mask", [128, 16 * SW], BF16)
            self.bo_r = sb("bo_r", [1, D], BF16)
            self.bffn_r = sb("bffn_r", [1, D], BF16)
            self.gateb_r = sb("gateb_r", [1, NE], BF16)
            self.WoT = sb("WoT", [128, NCD * D], BF16)
            self.WffnT = sb("WffnT", [128, NCD * D], BF16)
            self.AcatT = sb("AcatT", [128, NCD * ER], BF16)
            self.gateT = sb("gateT", [128, NCD * NE], BF16)
            self.BcatT = sb("BcatT", [128, D], BF16)
            with nc.named_scope('ag'):
                self.ag()
            if self.stop_after == "p1":
                self._emit_taps()
                return nc
            with nc.named_scope('p2'):
                self.p2()
            if self.stop_after == "p2":
                self._emit_taps()
                return nc
            with nc.named_scope('p3'):
                self.p3()
            self._emit_taps()
        return nc

    def _emit_taps(self):
        if not self.taps:
            return
        st = Streams()
        tmap = {"contrib": (self.contrib, CONTRIB),
                "gathered": (self.gathered, 4 * CONTRIB),
                "q_spill": (self.q_spill, H * NT * DH * QG),
                "ot_spill": (self.ot_spill, H * NT * DH * QG),
                "wo_sp": (self.wo_sp, D * D),
                "wffn_sp": (self.wffn_sp, D * D),
                "acat_sp": (self.acat_sp, ER * D),
                "bcat_sp": (self.bcat_sp, ER * D),
                "gate_sp": (self.gate_sp, 128 * NCD * NE),
                "dbg1": (self.dbg1, TCORE * D), "dbg2": (self.dbg2, TCORE * D),
                "dbg3": (self.dbg3, TCORE * ER)}
        for t in self.taps:
            th, size = tmap[t]
            tout = self.nc.dram_tensor("tap_" + t, [size], BF16, kind="ExternalOutput")
            per = size // 128
            st.emit("sync", lambda e, a=tout, b=th, p=per: e.dma_start(
                out=AP(a, 0, [[p, 128], [1, p]]), in_=AP(b, 0, [[p, 128], [1, p]])),
                self.sST, 16)
        st.wait("sync", self.sST)
        st.flush(self.nc)

    def p0_consts(self):
        nc = self.nc
        sLD, sGP, sGE, sDV, sAC = self.sLD, self.sGP, self.sGE, self.sDV, self.sAC
        st = Streams()
        with ExitStack() as p0:
            vrow = p0.enter_context(nc.sbuf_tensor("p0vrow", [1, DKV + D], F32, side="right"))
            vrow2 = p0.enter_context(nc.sbuf_tensor("p0vrow2", [1, D], F32, side="right"))
            rscr = p0.enter_context(nc.sbuf_tensor("p0rscr", [1, D], F32, side="right"))

            st.emit("gpsimd", lambda e: e.iota(self.identf[:], [[1, 128]], channel_multiplier=-1,
                                               allow_small_or_imprecise_dtypes=True), sGE)
            st.emit("gpsimd", lambda e: e.memset(self.ones_bf[:], 1.0), sGE)
            st.emit("gpsimd", lambda e: e.memset(self.onesrow_bf[:], 1.0), sGE)
            st.emit("gpsimd", lambda e: e.memset(self.onesrow_f[:], 1.0), sGE)
            st.emit("gpsimd", lambda e: e.memset(self.epsb[:], EPS), sGE)
            st.emit("gpsimd", lambda e: e.iota(self.onesA[:], [[0, 1]], channel_multiplier=1,
                                               allow_small_or_imprecise_dtypes=True), sGE)
            st.emit("sync", lambda e: e.dma_start(out=self.cosr[:], in_=self.cosrep.ap()), sLD, 16)
            st.emit("sync", lambda e: e.dma_start(out=self.sinr[:], in_=self.sinrep.ap()), sLD, 16)
            st.wait("vector", sGE)
            st.wait("vector", sLD)
            st.emit("vector", lambda e: e.tensor_scalar(self.identf[:], self.identf[:], 0.0, None,
                                                        op0=ALU.is_equal), sDV)
            st.emit("vector", lambda e: e.tensor_scalar(self.onesB[:], self.onesA[:], 63.5, None,
                                                        op0=ALU.is_gt), sDV)
            st.wait("vector", sDV)
            st.emit("vector", lambda e: e.tensor_scalar(self.onesA[:], self.onesA[:], 63.5, None,
                                                        op0=ALU.is_lt), sDV)
            st.wait("vector", sDV)
            st.emit("vector", lambda e: e.tensor_copy(self.identb[:], self.identf[:]), sDV)

            def vec_load(dram, n):
                st.wait("sync", sGP)
                st.wait("sync", sDV)
                st.emit("sync", lambda e, d=dram, nn=n: e.dma_start(
                    out=vrow[0:1, 0:nn], in_=AP(d, 0, [[nn, 1], [1, nn]])), sLD, 16)

            def col_store(dst, n, nch, src=None, src_off=0):
                s = vrow if src is None else src
                st.wait("gpsimd", sLD)
                st.wait("gpsimd", sDV)
                st.wait("gpsimd", self.sGE)
                for c in range(nch):
                    w = min(128, n - c * 128)
                    st.emit("gpsimd", lambda e, dd=dst, cc=c, ww=w, so=src_off, ss=s: e.dma_start(
                        out=dd[0:ww, cc:cc + 1],
                        in_=ss[0:1, so + cc * 128:so + cc * 128 + ww]), sGP, 16)

            for dram, dst in ((self.b_dq, self.bdq_c), (self.qng, self.qng_c),
                              (self.qnb, self.qnb_c)):
                st.emit("gpsimd", lambda e, dd=dst: e.memset(dd[:], 0.0), sGE)
                vec_load(dram, DQ)
                col_store(dst, DQ, NCQ)
            st.emit("gpsimd", lambda e: e.memset(self.bdkv_c[:], 0.0), sGE)
            vec_load(self.b_dkv, DCKV)
            col_store(self.bdkv_c, DCKV, 9)
            for dram, dst in ((self.kvg, self.kvg_c), (self.kvb, self.kvb_c)):
                vec_load(dram, DKV)
                col_store(dst, DKV, NCKV)
            vec_load(self.b_ukv, DKV + D)
            st.wait("gpsimd", sLD)
            for co in range(NCKV):
                st.emit("gpsimd", lambda e, cc=co: e.dma_start(
                    out=self.buk_c[0:64, cc:cc + 1],
                    in_=vrow[0:1, 384 * cc:384 * cc + 64]), sGP, 16)
                st.emit("gpsimd", lambda e, cc=co: e.dma_start(
                    out=self.buk_c[64:128, cc:cc + 1],
                    in_=vrow[0:1, 384 * cc + 192:384 * cc + 192 + 64]), sGP, 16)
            st.wait("vector", sLD)
            st.emit("vector", lambda e: e.tensor_copy(
                self.bukvv_r[:], AP(vrow, 64, [[DKV + D, 1], [192, H], [1, 128]])), sDV)

            rc = lambda t, w, p=1: AP(t, 64, [[w, p], [128, H], [1, 64]])
            rp = lambda t, w, p=1: AP(t, 96, [[w, p], [128, H], [-32, 2], [1, 32]])
            vec_load(self.b_uq, D)
            st.wait("vector", sLD)
            st.emit("vector", lambda e: e.tensor_copy(vrow2[0:1, :], vrow[0:1, 0:D]), sDV)
            st.wait("vector", sDV)
            st.emit("vector", lambda e: e.tensor_tensor(
                rc(vrow2, D), rc(vrow, DKV + D), rc(self.cosr, D), op=ALU.mult), sDV)
            st.emit("vector", lambda e: e.tensor_tensor(
                rc(rscr, D), rp(vrow, DKV + D), rc(self.sinr, D), op=ALU.mult), sDV)
            st.wait("vector", sDV)
            st.emit("vector", lambda e: e.tensor_tensor(
                rc(vrow2, D), rc(vrow2, D), rc(rscr, D), op=ALU.add), sDV)
            col_store(self.buq_c, D, NCD, src=vrow2)
            st.wait("sync", sGP)
            st.flush(nc)

    def p0_weights(self):
        nc = self.nc
        sLD, sGP, sPE, sDV = self.sLD, self.sGP, self.sPE, self.sDV
        st = Streams()
        with ExitStack() as p0:
            stage = [p0.enter_context(nc.sbuf_tensor("p0st%d" % i, [128, D], F32, side="right"))
                     for i in range(2)]
            rot = p0.enter_context(nc.sbuf_tensor("p0rot", [128, D], F32, side="right"))
            rots = p0.enter_context(nc.sbuf_tensor("p0rots", [128, D], F32, side="right"))
            rots2 = p0.enter_context(nc.sbuf_tensor("p0rots2", [128, D], F32, side="right"))
            spb = p0.enter_context(nc.sbuf_tensor("p0spb", [128, D], BF16, side="right"))
            ptr = [p0.enter_context(nc.psum_tensor("p0ps%d" % i, [128, 512], F32))
                   for i in range(2)]
            stage_free = [0, 0]
            ps_free = [0, 0]
            sLDs = [self.sLDa, self.sLDb]
            cur = [0]
            trn = [0]

            def load_stripe(dram, off_elems, rh, cols):
                i = cur[0] % 2
                cur[0] += 1
                b = stage[i]
                st.wait("sync", sPE, stage_free[i])
                st.emit("sync", lambda e, d=dram, ro=off_elems, r=rh, C=cols, bb=b: e.dma_start(
                    out=bb[0:r, 0:C], in_=AP(d, ro, [[C, r], [1, C]])), sLDs[i], 16)
                return i

            def tr(i, c0, rh, cw, evac):
                k = trn[0] % 2
                trn[0] += 1
                p = ptr[k]
                st.wait("tensor", sLDs[i], sLDs[i].n)
                st.wait("tensor", sDV, ps_free[k])
                st.emit("tensor", lambda e, bb=stage[i], pp=p, r=rh, cc=c0, w=cw: e.transpose(
                    pp[0:w, 0:r], bb[0:r, cc:cc + w], self.identf[0:r, 0:r]), sPE)
                st.wait("vector", sPE)
                evac(p)
                ps_free[k] = sDV.n
                stage_free[i] = sPE.n

            def cp_evac(tgt, coff, rh, cw=128):
                def f(p):
                    st.emit("vector", lambda e, pp=p, t=tgt, co=coff, r=rh, w=cw: e.tensor_copy(
                        t[0:w, co:co + r], pp[0:w, 0:r]), sDV)
                return f

            def tr4(i, c0, rh, tgt, coff, blk):
                k = trn[0] % 2
                trn[0] += 1
                p = ptr[k]
                st.wait("tensor", sLDs[i], sLDs[i].n)
                st.wait("tensor", sDV, ps_free[k])
                for j in range(4):
                    st.emit("tensor", lambda e, bb=stage[i], pp=p, r=rh,
                            cc=c0 + 128 * j, j=j: e.transpose(
                                pp[0:128, j * 128:j * 128 + r], bb[0:r, cc:cc + 128],
                                self.identf[0:r, 0:r]), sPE if j == 3 else None)
                st.wait("vector", sPE)
                tw = tgt.shape[1]
                st.emit("vector", lambda e, pp=p, t=tgt, co=coff, r=rh, b=blk, w=tw:
                        e.tensor_copy(AP(t, co, [[w, 128], [b, 4], [1, r]]),
                                      AP(pp, 0, [[512, 128], [128, 4], [1, r]])), sDV)
                ps_free[k] = sDV.n
                stage_free[i] = sPE.n

            def simple_w(dram, R, C, tgt, blk):
                for ro in range((R + 127) // 128):
                    rh = min(128, R - ro * 128)
                    i = load_stripe(dram, ro * 128 * C, rh, C)
                    for cc4 in range(C // 512):
                        tr4(i, cc4 * 512, rh, tgt, (cc4 * 4) * blk + ro * 128, blk)

            simple_w(self.w_dq, DQ, D, self.WdqT, DQ)
            simple_w(self.w_dkv, DCKV, D, self.WdkvT, DCKV)
            for hh in range(H):
                i = load_stripe(self.w_ukv, (192 * hh) * DKV, 64, DKV)
                for cc4 in range(NCKV // 4):
                    tr4(i, cc4 * 512, 64, self.WukT, (cc4 * 4) * DKV + 64 * hh, DKV)
            for hh in range(H):
                i = load_stripe(self.w_ukv, (192 * hh + 64) * DKV, 128, DKV)
                for cc4 in range(NCKV // 4):
                    tr4(i, cc4 * 512, 128, self.WuvT, (cc4 * 4) * D + 128 * hh, D)

            rc = lambda t, w, p: AP(t, 64, [[w, p], [128, H], [1, 64]])
            rp = lambda t, w, p: AP(t, 96, [[w, p], [128, H], [-32, 2], [1, 32]])
            for ccq in range(NCQ):
                cw = min(128, DQ - ccq * 128)
                for ro in range(NCD):
                    i = load_stripe(self.w_uq, ro * 128 * DQ, 128, DQ)
                    tr(i, ccq * 128, 128, cw, cp_evac(rot, ro * 128, 128, cw))
                st.wait("vector", sDV)
                base = ccq * D
                st.emit("vector", lambda e, w=cw, b=base: e.tensor_copy(
                    AP(self.WuqT, b, [[NCQ * D, w], [128, H], [1, 64]]),
                    AP(rot, 0, [[D, w], [128, H], [1, 64]])), sDV)
                st.emit("vector", lambda e, w=cw: e.tensor_tensor(
                    rc(rots, D, w), rc(rot, D, w), rc(self.cosr, D, w), op=ALU.mult), sDV)
                st.emit("vector", lambda e, w=cw: e.tensor_tensor(
                    rc(rots2, D, w), rp(rot, D, w), rc(self.sinr, D, w), op=ALU.mult), sDV)
                st.wait("vector", sDV)
                st.emit("vector", lambda e, w=cw, b=base: e.tensor_tensor(
                    AP(self.WuqT, b + 64, [[NCQ * D, w], [128, H], [1, 64]]),
                    rc(rots, D, w), rc(rots2, D, w), op=ALU.add), sDV)
                st.wait("vector", sDV)

            def spill_stripes(dram, sp, nro, spill_ap_fn):
                for ro in range(nro):
                    i = load_stripe(dram, ro * 128 * D, 128, D)
                    st.wait("vector", sGP)
                    for cc4 in range(NCD // 4):
                        tr4(i, cc4 * 512, 128, spb, cc4 * 512, 128)
                    st.wait("gpsimd", sDV)
                    st.emit("gpsimd", lambda e, s=sp, r=ro, f=spill_ap_fn: e.dma_start(
                        out=f(s, r), in_=spb[:]), sGP, 16)

            wsp_ap = lambda s, r: AP(s, r * 128, [[NCD * D, 128], [D, NCD], [1, 128]])
            spill_stripes(self.w_o, self.wo_sp, NCD, wsp_ap)
            spill_stripes(self.w_ffn, self.wffn_sp, NCD, wsp_ap)

            i = load_stripe(self.lora_a, 0, 128, D)
            st.wait("vector", sGP)
            for cc4 in range(NCD // 4):
                tr4(i, cc4 * 512, 128, spb, cc4 * 512, 128)
            st.wait("gpsimd", sDV)
            st.emit("gpsimd", lambda e: e.dma_start(
                out=AP(self.acat_sp, 0, [[NCD * 128, 128], [1, NCD * 128]]), in_=spb[:]), sGP, 16)

            i = load_stripe(self.gate_w, 0, NE, D)
            st.wait("vector", sGP)
            for cc in range(NCD):
                tr(i, cc * 128, NE, 128, cp_evac(spb, cc * NE, NE))
            st.wait("gpsimd", sDV)
            st.emit("gpsimd", lambda e: e.dma_start(
                out=AP(self.gate_sp, 0, [[NCD * NE, 128], [1, NCD * NE]]),
                in_=spb[0:128, 0:NCD * NE]), sGP, 16)

            st.wait("vector", sGP)
            for co in range(NCD):
                ii = cur[0] % 2
                cur[0] += 1
                b = stage[ii]
                st.wait("sync", sPE, stage_free[ii])
                st.emit("sync", lambda e, c0=co * 128, bb=b: e.dma_start(
                    out=bb[0:128, 0:ER],
                    in_=AP(self.lora_b, c0 * RLORA,
                           [[RLORA, 128], [D * RLORA, NE], [1, RLORA]])), sLDs[ii], 16)
                tr(ii, 0, 128, 128, cp_evac(spb, co * 128, 128))
            st.wait("gpsimd", sDV)
            st.emit("gpsimd", lambda e: e.dma_start(
                out=AP(self.bcat_sp, 0, [[NCD * 128, 128], [1, NCD * 128]]),
                in_=spb[0:128, 0:NCD * 128]), sGP, 16)
            st.wait("gpsimd", sGP)
            st.flush(nc)

    def p1(self):
        nc = self.nc
        sLD, sGP, sPE, sDV, sAC, sST = self.sLD, self.sGP, self.sPE, self.sDV, self.sAC, self.sST
        st = Streams()
        with ExitStack() as p1:
            def rsb(name, shape, dt=F32):
                return p1.enter_context(nc.sbuf_tensor(name, list(shape), dt, side="right"))

            xT = rsb("xT", [128, NCD * SW], BF16)
            nrmb = rsb("nrmb", [128, NCKV * SW], BF16)
            raw = rsb("raw", [128, NCKV * SW], BF16)
            krr = rsb("krr", [64, SW], BF16)
            xv = rsb("xv", [128, 4 * D], BF16)
            cqn, kvn = nrmb, nrmb
            cqr, ckvr = raw, raw
            xbf, vsb = xv, xv
            sq = [rsb("sq%d" % i, [128, SW], BF16) for i in range(2)]
            s2 = [rsb("s2%d" % i, [128, SW], F32) for i in range(2)]
            qT = [rsb("qT%d" % i, [128, SW], BF16) for i in range(2)]
            knr = [rsb("knr%d" % i, [128, SW], BF16) for i in range(2)]
            murow = rsb("murow", [1, SW], F32)
            rrow = rsb("rrow", [1, SW], F32)
            varow = rsb("varow", [1, SW], F32)
            kr2row = rsb("kr2row", [1, SW], F32)
            tiny = rsb("tiny", [1, 16], F32)
            nrmbf = rsb("nrmbf", [1, 1024], BF16)

            pA = [p1.enter_context(nc.psum_tensor("p1a%d" % i, [128, 512], F32))
                  for i in range(4)]
            pS = p1.enter_context(nc.psum_tensor("p1s", [1, 1024], F32))
            pMu = p1.enter_context(nc.psum_tensor("p1mu", [128, 512], F32))
            pR = p1.enter_context(nc.psum_tensor("p1r", [128, 512], F32))

            self._pa_free = [0, 0, 0, 0]
            self._pa_i = 0
            self._tr_free = [0, 0]
            sQT = [self.sQTa, self.sQTb]
            sKN = [self.sKNa, self.sKNb]
            sKR = self.sKR

            def proj(nchunks, stat_fn, mov, mov_base_fn, out_rows, evac, kdims=None,
                     extra=None):
                k = self._pa_i % 4
                self._pa_i += 1
                p = pA[k]
                st.wait("tensor", sDV, self._pa_free[k])
                for c in range(nchunks):
                    kd = 128 if kdims is None else kdims[c]
                    last = (c == nchunks - 1) and extra is None
                    st.emit("tensor",
                            lambda e, pp=p, s=stat_fn(c), m=mov, mb=mov_base_fn(c), orr=out_rows,
                            r=kd, c0=(c == 0), st_=last: e.matmul(
                                pp[0:orr, 0:SW], s, m[0:r, mb:mb + SW], start=c0, stop=st_,
                                skip_group_check=True),
                            sPE if last else None)
                if extra is not None:
                    extra(p)
                st.wait("vector", sPE)
                evac(p)
                self._pa_free[k] = sDV.n
                return p

            for s in range(NSTRIP):
                st.wait("gpsimd", sPE)
                st.wait("gpsimd", sST)
                st.wait("tensor", sDV)
                sX = [self.sXa, self.sXb]
                for half in range(2):
                    st.emit("gpsimd", lambda e, ss=s, hf=half: e.dma_start(
                        out=xbf[0:128, 0:2 * D] if hf == 0 else xbf[0:128, 2 * D:4 * D],
                        in_=AP(self.x_own, (ss * SW + hf * 256) * D,
                               [[D, 128], [128 * D, 2], [1, D]])), sX[half], 16)
                    st.wait("tensor", sX[half], sX[half].n)
                    for bank in range(4):
                        k = bank % 2
                        p = pMu if k == 0 else pR
                        st.wait("tensor", sDV, self._tr_free[k])
                        for j in range(8):
                            t_idx = bank * 8 + j
                            cx, i = t_idx // 2, t_idx % 2
                            st.emit("tensor", lambda e, pp=p, j_=j, cx_=cx, i_=i, hf=half:
                                    e.transpose(
                                        pp.bitcast(BF16)[0:128, j_ * 128:j_ * 128 + 128],
                                        xbf[0:128, (2 * hf + i_) * D + cx_ * 128:
                                            (2 * hf + i_) * D + cx_ * 128 + 128],
                                        self.identb[:]), sPE if j == 7 else None)
                        st.wait("vector", sPE)
                        st.emit("vector", lambda e, pp=p, b=bank, hf=half: e.tensor_copy(
                            AP(xT, (b * 4) * SW + hf * 256,
                               [[NCD * SW, 128], [SW, 4], [1, 256]]),
                            AP(pp.bitcast(BF16), 0, [[1024, 128], [256, 4], [1, 256]])), sDV)
                        self._tr_free[k] = sDV.n

                st.wait("tensor", sDV)
                for co in range(NCQ):
                    w = 128 if co < 5 else CQ_TAIL

                    def evac_cq(p, co=co, w=w):
                        st.emit("vector", lambda e, pp=p: e.tensor_scalar(
                            cqr[0:w, co * SW:(co + 1) * SW], pp[0:w, 0:SW],
                            self.bdq_c[0:w, co:co + 1], None, op0=ALU.add), sDV)
                        st.wait("vector", sDV)
                        st.emit("vector", lambda e: e.tensor_tensor(
                            sq[co % 2][0:w, 0:SW], cqr[0:w, co * SW:(co + 1) * SW],
                            cqr[0:w, co * SW:(co + 1) * SW], op=ALU.mult), sDV)

                    proj(NCD, lambda c, co=co, w=w:
                         self.WdqT[0:128, c * DQ + co * 128:c * DQ + co * 128 + w],
                         xT, lambda c: c * SW, w, evac_cq)
                    st.wait("tensor", sDV)
                    st.emit("tensor", lambda e, co=co, w=w: e.matmul(
                        pS[0:1, 0:512], self.ones_bf[0:w, :], cqr[0:w, co * SW:(co + 1) * SW],
                        start=(co == 0), stop=(co == NCQ - 1), skip_group_check=True), None)
                    st.emit("tensor", lambda e, co=co, w=w: e.matmul(
                        pS[0:1, 512:1024], self.ones_bf[0:w, :], sq[co % 2][0:w, 0:SW],
                        start=(co == 0), stop=(co == NCQ - 1), skip_group_check=True),
                        sPE if co == NCQ - 1 else None)

                self._ln(st, pS, pMu, pR, murow, rrow, varow, DQ, NCQ, cqr, cqn, s2,
                         self.qng_c, self.qnb_c, CQ_TAIL, SW)

                st.wait("tensor", sDV)
                kdims_q = [128] * 5 + [CQ_TAIL]
                for h in range(H):
                    def evac_q(p, h=h):
                        qb = qT[h % 2]
                        st.wait("vector", sQT[h % 2], sQT[h % 2].n)
                        st.emit("vector", lambda e, pp=p: e.tensor_scalar(
                            qb[:], pp[0:128, 0:SW], self.buq_c[0:128, h:h + 1], None,
                            op0=ALU.add), sDV)
                        st.wait("vector", sDV)
                        st.emit("vector", lambda e: e.tensor_tensor(
                            sq[h % 2][:], qb[:], qb[:], op=ALU.mult), sDV)

                    proj(NCQ, lambda c, h=h:
                         self.WuqT[0:kdims_q[c], c * D + h * 128:c * D + h * 128 + 128],
                         cqn, lambda c: c * SW, 128, evac_q, kdims=kdims_q)
                    st.wait("sync", sDV)
                    st.emit("sync", lambda e, h=h, ss=s: e.dma_start(
                        out=AP(self.q_spill, (h * QNT + ss) * DH * SW, [[SW, 128], [1, SW]]),
                        in_=qT[h % 2][0:128, 0:SW]), sQT[h % 2], 16)
                    st.wait("tensor", sDV)
                    st.emit("tensor", lambda e, h=h: e.matmul(
                        pS[0:1, 0:512], self.ones_bf[:], sq[h % 2][:],
                        start=True, stop=True, skip_group_check=True), sPE)
                    st.wait("vector", sPE)
                    st.emit("vector", lambda e, h=h, ss=s: e.tensor_reduce(
                        self.maxq2[0:1, h * QNT + ss:h * QNT + ss + 1], pS[0:1, 0:SW],
                        axis=AX.X, op=ALU.max), sDV)
                    st.wait("tensor", sDV)

                for co in range(9):
                    w = 128 if co < 8 else 64

                    def evac_ckv(p, co=co, w=w):
                        tgt = (ckvr, co * SW) if co < 8 else (krr, 0)
                        if co == 8:
                            st.wait("vector", sKR, sKR.n)
                        st.emit("vector", lambda e, pp=p, tg=tgt: e.tensor_scalar(
                            tg[0][0:w, tg[1]:tg[1] + SW], pp[0:w, 0:SW],
                            self.bdkv_c[0:w, co:co + 1], None, op0=ALU.add), sDV)
                        st.wait("vector", sDV)
                        st.emit("vector", lambda e, tg=tgt: e.tensor_tensor(
                            sq[co % 2][0:w, 0:SW], tg[0][0:w, tg[1]:tg[1] + SW],
                            tg[0][0:w, tg[1]:tg[1] + SW], op=ALU.mult), sDV)

                    proj(NCD, lambda c, co=co, w=w:
                         self.WdkvT[0:128, c * DCKV + co * 128:c * DCKV + co * 128 + w],
                         xT, lambda c: c * SW, w, evac_ckv)
                    if co < 8:
                        st.wait("tensor", sDV)
                        st.emit("tensor", lambda e, co=co: e.matmul(
                            pS[0:1, 0:512], self.ones_bf[:], ckvr[0:128, co * SW:(co + 1) * SW],
                            start=(co == 0), stop=(co == 7), skip_group_check=True), None)
                        st.emit("tensor", lambda e, co=co: e.matmul(
                            pS[0:1, 512:1024], self.ones_bf[:], sq[co % 2][:],
                            start=(co == 0), stop=(co == 7), skip_group_check=True),
                            sPE if co == 7 else None)

                self._ln(st, pS, pMu, pR, murow, rrow, varow, DKV, NCKV, ckvr, kvn, s2,
                         self.kvg_c, self.kvb_c, 128, SW)

                st.wait("tensor", sDV)
                st.emit("tensor", lambda e: e.matmul(
                    pS[0:1, 0:512], self.onesA[0:64, :] if False else self.ones_bf[0:64, :],
                    sq[0][0:64, 0:SW], start=True, stop=True, skip_group_check=True), sPE)
                st.wait("vector", sPE)
                st.emit("vector", lambda e: e.tensor_copy(kr2row[0:1, :], pS[0:1, 0:512]), sDV)
                st.wait("tensor", sDV)

                st.wait("sync", sDV)
                for h in range(H):
                    st.emit("sync", lambda e, h=h, ss=s: e.dma_start(
                        out=AP(self.contrib, ss * SBLK + (h * DH + 64) * SW,
                               [[SW, 64], [1, SW]]),
                        in_=krr[0:64, 0:SW]), sKR, 16)

                st.wait("tensor", sDV)
                for co in range(NCKV):
                    def evac_k(p, co=co):
                        kb = knr[co % 2]
                        st.wait("vector", sKN[co % 2], sKN[co % 2].n)
                        st.emit("vector", lambda e, pp=p: e.tensor_scalar(
                            kb[:], pp[0:128, 0:SW], self.buk_c[0:128, co:co + 1], None,
                            op0=ALU.add), sDV)
                        st.wait("vector", sDV)
                        st.emit("vector", lambda e: e.tensor_tensor(
                            sq[co % 2][:], kb[:], kb[:], op=ALU.mult), sDV)

                    proj(NCKV, lambda c, co=co:
                         self.WukT[0:128, c * DKV + co * 128:c * DKV + co * 128 + 128],
                         kvn, lambda c: c * SW, 128, evac_k)
                    st.wait("sync", sDV)
                    for hf in range(2):
                        h = 2 * co + hf
                        st.emit("sync", lambda e, h=h, hf=hf, co=co, ss=s: e.dma_start(
                            out=AP(self.contrib, ss * SBLK + h * DH * SW,
                                   [[SW, 64], [1, SW]]),
                            in_=knr[co % 2][hf * 64:hf * 64 + 64, 0:SW]), sKN[co % 2], 16)
                    st.wait("tensor", sDV)
                    st.emit("tensor", lambda e, co=co: e.matmul(
                        pS[0:1, 0:512], self.onesA[:], sq[co % 2][:],
                        start=True, stop=True, skip_group_check=True), None)
                    st.emit("tensor", lambda e, co=co: e.matmul(
                        pS[0:1, 512:1024], self.onesB[:], sq[co % 2][:],
                        start=True, stop=True, skip_group_check=True), sPE)
                    st.wait("vector", sPE)
                    st.emit("vector", lambda e: e.tensor_tensor(
                        varow[0:1, 0:SW], AP(pS, 0, [[1024, 1], [1, SW]]),
                        kr2row[0:1, 0:SW], op=ALU.add), sDV)
                    st.emit("vector", lambda e: e.tensor_tensor(
                        murow[0:1, 0:SW], AP(pS, 512, [[1024, 1], [1, SW]]),
                        kr2row[0:1, 0:SW], op=ALU.add), sDV)
                    st.wait("vector", sDV)
                    st.emit("vector", lambda e: e.tensor_reduce(
                        tiny[0:1, 0:1], varow[0:1, 0:SW], axis=AX.X, op=ALU.max), sDV)
                    st.emit("vector", lambda e: e.tensor_reduce(
                        tiny[0:1, 1:2], murow[0:1, 0:SW], axis=AX.X, op=ALU.max), sDV)
                    st.wait("vector", sDV)
                    for hf in range(2):
                        h = 2 * co + hf
                        st.emit("vector", lambda e, h=h, hf=hf: e.tensor_copy(
                            self.maxk2[0:1, h:h + 1], tiny[0:1, hf:hf + 1]), sDV)
                    st.wait("tensor", sDV)

                for i in range(4):
                    for vs in range(4):
                        def evac_v(p, i=i, vs=vs):
                            st.emit("vector", lambda e, pp=p: e.tensor_copy(
                                vsb[0:128, i * D + vs * 512:i * D + vs * 512 + 512],
                                pp[0:128, 0:512]), sDV)

                        def extra_bias(p, vs=vs):
                            st.emit("tensor", lambda e, pp=p: e.matmul(
                                pp[0:128, 0:512], self.onesrow_bf[0:1, :],
                                self.bukvv_r[0:1, vs * 512:(vs + 1) * 512],
                                start=False, stop=True, skip_group_check=True), sPE)

                        proj(NCKV, lambda c, i=i:
                             kvn[0:128, c * SW + i * 128:c * SW + i * 128 + 128],
                             self.WuvT, lambda c, vs=vs: c * D + vs * 512, 128,
                             evac_v, extra=extra_bias)
                st.wait("sync", sDV)
                for h in range(H):
                    st.emit("sync", lambda e, h=h, ss=s: e.dma_start(
                        out=AP(self.contrib, ss * SBLK + VOFFs + h * SW * DH,
                               [[DH, 128], [128 * DH, 4], [1, DH]]),
                        in_=AP(vsb, h * 128, [[4 * D, 128], [D, 4], [1, DH]])), sST, 16)

                if s == 0:
                    st.emit("gpsimd", lambda e: e.memset(nrmbf[:], 0.0), self.sGE)
                st.wait("vector", sDV)
                st.wait("vector", self.sGE)
                st.wait("vector", sST)
                st.emit("vector", lambda e: e.tensor_scalar(
                    nrmbf[0:1, 0:16], self.maxk2[0:1, :], 1.02, None, op0=ALU.mult), sDV)
                st.wait("sync", sDV)
                st.emit("sync", lambda e, ss=s: e.dma_start(
                    out=AP(self.contrib, ss * SBLK + NOFFs, [[1024, 1], [1, 1024]]),
                    in_=nrmbf[0:1, :]), sST, 16)
                for sx in (sST, sKR, sKN[0], sKN[1]):
                    st.wait("gpsimd", sx, sx.n)
                st.emit("gpsimd", lambda e, ss=s: e.collective_compute(
                    "AllGather", ALU.bypass,
                    replica_groups=[[0, 1, 2, 3], [4, 5, 6, 7]],
                    ins=[AP(self.contrib, ss * SBLK, [[SBLK, 1], [1, SBLK]])],
                    outs=[AP(self.gathered, ss * 4 * SBLK,
                             [[4 * SBLK, 1], [1, 4 * SBLK]])]), self.sCC)

            st.wait("sync", sST)
            st.wait("gpsimd", sGP)
            for sx in (sQT[0], sQT[1],):
                st.wait("gpsimd", sx, sx.n)
            st.flush(nc)

    def _ln(self, st, pS, pMu, pR, murow, rrow, varow, DIM, NCH, raw, dst, s2, g_c, b_c,
            tail, width):
        sPE, sDV, sAC = self.sPE, self.sDV, self.sAC
        inv = 1.0 / DIM
        st.wait("vector", sPE)
        st.emit("vector", lambda e: e.tensor_scalar(
            murow[0:1, 0:width], pS[0:1, 0:512], inv, None, op0=ALU.mult), sDV)
        st.emit("vector", lambda e: e.tensor_scalar(
            varow[0:1, 0:width], pS[0:1, 512:1024], inv, None, op0=ALU.mult), sDV)
        st.wait("vector", sDV)
        st.emit("vector", lambda e: e.tensor_tensor(
            rrow[0:1, 0:width], murow[0:1, 0:width], murow[0:1, 0:width], op=ALU.mult), sDV)
        st.wait("vector", sDV)
        st.emit("vector", lambda e: e.tensor_tensor(
            varow[0:1, 0:width], varow[0:1, 0:width], rrow[0:1, 0:width], op=ALU.subtract), sDV)
        st.wait("scalar", sDV)
        st.emit("scalar", lambda e: e.activation(out=rrow[0:1, 0:width], in_=varow[0:1, 0:width],
                                                 func=AF.Ln, bias=self.epsb[0:1, :],
                                                 scale=1.0), sAC)
        st.wait("scalar", sAC)
        st.emit("scalar", lambda e: e.activation(out=rrow[0:1, 0:width], in_=rrow[0:1, 0:width],
                                                 func=AF.Exp, bias=0.0, scale=-0.5), sAC)
        st.wait("tensor", sAC)
        st.wait("tensor", sDV)
        st.emit("tensor", lambda e: e.matmul(pMu[0:128, 0:width], self.onesrow_f[0:1, :],
                                             murow[0:1, 0:width], start=True, stop=True,
                                             skip_group_check=True), None)
        st.emit("tensor", lambda e: e.matmul(pR[0:128, 0:width], self.onesrow_f[0:1, :],
                                             rrow[0:1, 0:width], start=True, stop=True,
                                             skip_group_check=True), sPE)
        st.wait("vector", sPE)
        for c in range(NCH):
            w = 128 if c < NCH - 1 else tail
            st.emit("vector", lambda e, c=c, w=w: e.tensor_tensor(
                s2[c % 2][0:w, 0:width], raw[0:w, c * width:(c + 1) * width],
                pMu[0:w, 0:width], op=ALU.subtract), sDV)
            st.wait("vector", sDV)
            st.emit("vector", lambda e, c=c, w=w: e.tensor_tensor(
                s2[c % 2][0:w, 0:width], s2[c % 2][0:w, 0:width], pR[0:w, 0:width],
                op=ALU.mult), sDV)
            st.wait("vector", sDV)
            st.emit("vector", lambda e, c=c, w=w: e.tensor_scalar(
                dst[0:w, c * width:(c + 1) * width], s2[c % 2][0:w, 0:width],
                g_c[0:w, c:c + 1], b_c[0:w, c:c + 1], op0=ALU.mult, op1=ALU.add), sDV)
            st.wait("vector", sDV)

    def ag(self):
        nc = self.nc
        st = Streams()
        sGE, sDV, sLD = self.sGE, self.sDV, self.sLD
        with ExitStack() as ages:
            scr = ages.enter_context(nc.sbuf_tensor("agscr", [128, 16 * SW], F32, side="right"))
            mthr_t = ages.enter_context(nc.sbuf_tensor("agmthr", [128, 1], F32, side="right"))
            vr = ages.enter_context(nc.sbuf_tensor("agvr", [1, D], F32, side="right"))
            st.emit("gpsimd", lambda e: e.iota(scr[:], [[-128, 16], [1, SW]],
                                               channel_multiplier=-1,
                                               allow_small_or_imprecise_dtypes=True), sGE)
            st.emit("sync", lambda e: e.dma_start(out=mthr_t[:], in_=self.mthr.ap()), sLD, 16)
            st.wait("vector", sGE)
            st.wait("vector", sLD)
            st.emit("vector", lambda e: e.tensor_scalar(self.mask[:], scr[:], mthr_t[:, 0:1],
                                                        None, op0=ALU.is_ge), sDV)
            for dram, dst, n in ((self.b_o, self.bo_r, D), (self.b_ffn, self.bffn_r, D),
                                 (self.gate_b, self.gateb_r, NE)):
                st.wait("sync", sDV)
                st.emit("sync", lambda e, d=dram, nn=n: e.dma_start(
                    out=vr[0:1, 0:nn], in_=AP(d, 0, [[nn, 1], [1, nn]])), sLD, 16)
                st.wait("vector", sLD)
                st.emit("vector", lambda e, dd=dst, nn=n: e.tensor_copy(
                    dd[0:1, 0:nn], vr[0:1, 0:nn]), sDV)
            st.wait("gpsimd", sDV)
            st.flush(nc)

    def p2(self):
        nc = self.nc
        sPE, sDV, sAC, sCC = self.sPE, self.sDV, self.sAC, self.sCC
        sWL = self.sWL
        sKV = [self.sKVa, self.sKVb]
        sQB = [self.sQBa, self.sQBb]
        sOT = [self.sOTa, self.sOTb]
        st = Streams()
        with ExitStack() as p2:
            def rsb(name, shape, dt=BF16):
                return p2.enter_context(nc.sbuf_tensor(name, list(shape), dt, side="right"))

            qbuf = [rsb("qbuf%d" % i, [128, SW]) for i in range(2)]
            Kbuf = [rsb("Kbuf%d" % i, [128, 1024]) for i in range(2)]
            Vbuf = [rsb("Vbuf%d" % i, [128, 1024]) for i in range(2)]
            pt = [rsb("pt%d" % i, [128, 1024]) for i in range(2)]
            ot_sb = [rsb("ot%d" % i, [128, SW]) for i in range(2)]
            recip = p2.enter_context(nc.sbuf_tensor("recip", [1, SW], F32, side="right"))
            recip_bc = p2.enter_context(nc.sbuf_tensor("recip_bc", [128, SW], F32, side="right"))
            nrm4 = p2.enter_context(nc.sbuf_tensor("nrm4", [1, 64], BF16, side="right"))
            nrm4f = p2.enter_context(nc.sbuf_tensor("nrm4f", [1, 64], F32, side="right"))
            mkrun = p2.enter_context(nc.sbuf_tensor("mkrun", [1, H], F32, side="right"))
            prodr = p2.enter_context(nc.sbuf_tensor("prodr", [1, H], F32, side="right"))
            psS = [p2.enter_context(nc.psum_tensor("p2s%d" % i, [128, 1024], F32))
                   for i in range(2)]
            psO = [p2.enter_context(nc.psum_tensor("p2o%d" % i, [128, SW], F32))
                   for i in range(2)]
            psR = p2.enter_context(nc.psum_tensor("p2r", [1, SW], F32))
            psB = p2.enter_context(nc.psum_tensor("p2b", [128, SW], F32))

            for sp, tgt, cols in ((self.wo_sp, self.WoT, NCD * D),
                                  (self.wffn_sp, self.WffnT, NCD * D),
                                  (self.acat_sp, self.AcatT, NCD * ER),
                                  (self.gate_sp, self.gateT, NCD * NE),
                                  (self.bcat_sp, self.BcatT, D)):
                st.emit("gpsimd", lambda e, s=sp, t=tgt, cc=cols: e.dma_start(
                    out=t[0:128, 0:cc], in_=AP(s, 0, [[cc, 128], [1, cc]])), sWL, 16)

            self._p2_send = [None]
            pe_qfree = [0, 0]
            pe_kvfree = [0, 0]
            ac_psfree = [0, 0]
            pe_ptfree = [0, 0]
            dv_osfree = [0, 0]
            ac_rfree = 0
            self._p2_dvb = [0]

            for kcls in range(1, QNT + 1):
                st.wait("gpsimd", sCC, kcls)
                st.emit("gpsimd", lambda e, k=kcls: e.dma_start(
                    out=nrm4[0:1, 0:64],
                    in_=AP(self.gathered, (k - 1) * 4 * SBLK + NOFFs,
                           [[64, 1], [SBLK, 4], [1, 16]])), self.sGP, 16)
                st.wait("vector", self.sGP)
                st.emit("vector", lambda e: e.tensor_copy(nrm4f[:], nrm4[:]), sDV)
                st.wait("vector", sDV)
                st.emit("vector", lambda e: e.tensor_reduce(
                    prodr[0:1, 0:H], AP(nrm4f, 0, [[64, 1], [1, 16], [16, 4]]),
                    axis=AX.X, op=ALU.max), sDV)
                st.wait("vector", sDV)
                if kcls == 1:
                    st.emit("vector", lambda e: e.tensor_copy(mkrun[:], prodr[0:1, 0:H]), sDV)
                else:
                    st.emit("vector", lambda e: e.tensor_tensor(
                        mkrun[:], mkrun[:], prodr[0:1, 0:H], op=ALU.max), sDV)
                st.wait("vector", sDV)
                st.emit("vector", lambda e, k=kcls: e.tensor_tensor(
                    prodr[0:1, 0:H],
                    AP(self.maxq2, k - 1, [[H * QNT, 1], [QNT, H]]),
                    mkrun[0:1, 0:H], op=ALU.mult), sDV)
                st.wait("scalar", sDV)
                st.emit("scalar", lambda e: e.activation(out=prodr[:], in_=prodr[:],
                                                         func=AF.Ln, bias=self.epsb[0:1, :],
                                                         scale=1.0), sAC)
                st.wait("scalar", sAC)
                st.emit("scalar", lambda e: e.activation(out=prodr[:], in_=prodr[:],
                                                         func=AF.Exp, bias=0.0,
                                                         scale=0.5), sAC)
                st.wait("vector", sAC)
                st.emit("vector", lambda e: e.tensor_scalar(
                    prodr[:], prodr[:], ATT_SCALE, 80.0, op0=ALU.mult, op1=ALU.min), sDV)
                st.wait("vector", sDV)
                st.emit("vector", lambda e: e.tensor_scalar(
                    prodr[:], prodr[:], -1.0, None, op0=ALU.mult), sDV)
                if self._p2_send[0] is not None:
                    self._p2_send[0]()
                    self._p2_send[0] = None
                st.wait("tensor", sDV)
                st.wait("tensor", sDV, self._p2_dvb[0])
                st.emit("tensor", lambda e: e.matmul(
                    psB[0:128, 0:H], self.onesrow_f[0:1, :], prodr[0:1, 0:H],
                    start=True, stop=True, skip_group_check=True), sPE)
                st.wait("vector", sPE)
                st.emit("vector", lambda e, k=kcls: e.tensor_copy(
                    self.negm_bc[0:128, (k - 1) * H:k * H], psB[0:128, 0:H]), sDV)
                self._p2_dvb[0] = self.sDV.n
                st.wait("tensor", sDV)

                for h in range(H):
                    sidx = (kcls - 1) * H + h
                    psw = sidx % 2
                    nblk = 2 * kcls
                    npair = 4 * nblk

                    st.wait("sync", sPE, pe_qfree[psw])
                    st.emit("sync", lambda e, h=h, k=kcls, b=qbuf[psw]: e.dma_start(
                        out=b[:], in_=AP(self.q_spill, (h * QNT + k - 1) * DH * SW,
                                         [[SW, 128], [1, SW]])), sQB[psw], 16)
                    qld = sQB[psw].n

                    kvld = [0] * nblk

                    def emit_load(blk, h=h):
                        pb = blk % 2
                        strip, rank0 = blk // 2, (2 * blk) % 4
                        base = strip * 4 * SBLK + rank0 * SBLK
                        st.wait("sync", sPE, pe_kvfree[pb])
                        st.emit("sync", lambda e, ba=base, b=Kbuf[pb]: e.dma_start(
                            out=AP(b, 0, [[1024, 128], [512, 2], [1, 512]]),
                            in_=AP(self.gathered, ba + KOFFs + h * DH * SW,
                                   [[SW, 128], [SBLK, 2], [1, 512]])), sKV[pb], 16)
                        for pc in range(2):
                            st.emit("sync", lambda e, ba=base, b=Vbuf[pb], pc=pc:
                                    e.dma_start(
                                        out=AP(b, pc * 512,
                                               [[1024, 128], [128, 4], [1, 128]]),
                                        in_=AP(self.gathered,
                                               ba + pc * SBLK + VOFFs + h * SW * DH,
                                               [[DH, 128], [128 * DH, 4], [1, DH]])),
                                    sKV[pb], 16)
                        kvld[blk] = sKV[pb].n

                    emit_load(0)
                    if nblk > 1:
                        emit_load(1)

                    pend = [None]

                    def flush_rpv():
                        if pend[0] is not None:
                            pend[0]()
                            pend[0] = None

                    def flush_sweepend():
                        if self._p2_send[0] is not None:
                            self._p2_send[0]()
                            self._p2_send[0] = None

                    for g in range(npair):
                        blk, gb = g // 4, g % 4
                        pb = blk % 2
                        gp = g % 2
                        masked = (blk >= nblk - 2)
                        first = (g == 0)
                        last = (g == npair - 1)

                        if gb == 0:
                            st.wait("tensor", sKV[pb], kvld[blk])
                        if first:
                            st.wait("tensor", sQB[psw], qld)
                        st.wait("tensor", sAC, ac_psfree[gp])
                        for j in range(2):
                            cc = 2 * gb + j
                            st.emit("tensor", lambda e, pS=psS[gp], Kb=Kbuf[pb],
                                    qb=qbuf[psw], cc=cc, j=j: e.matmul(
                                        pS[0:128, j * SW:(j + 1) * SW],
                                        Kb[0:128, cc * 128:cc * 128 + 128], qb[:],
                                        start=True, stop=True, skip_group_check=True),
                                    sPE if j == 1 else None)
                        spe_s = sPE.n

                        st.wait("scalar", sPE, spe_s)
                        st.wait("scalar", sPE, pe_ptfree[gp])
                        st.emit("scalar", lambda e, pS=psS[gp], p=pt[gp], sidx=sidx:
                                e.activation(out=p[:], in_=pS[0:128, 0:1024], func=AF.Exp,
                                             bias=self.negm_bc[0:128, sidx:sidx + 1],
                                             scale=ATT_SCALE), sAC)
                        ac_psfree[gp] = sAC.n
                        ac_exp = sAC.n

                        dv_mask = None
                        if masked:
                            jp = g - (npair - 8)
                            st.wait("vector", sAC, ac_exp)
                            st.emit("vector", lambda e, p=pt[gp], jp=jp: e.tensor_tensor(
                                p[:], p[:], self.mask[0:128, jp * 1024:(jp + 1) * 1024],
                                op=ALU.mult), sDV)
                            dv_mask = sDV.n

                        flush_rpv()
                        if g == 0:
                            flush_sweepend()

                        def mk_rpv(g=g, gp=gp, pb=pb, ac_exp=ac_exp, dv_mask=dv_mask,
                                   first=first, last=last, psw=psw):
                            def emit():
                                if dv_mask is not None:
                                    st.wait("tensor", sDV, dv_mask)
                                else:
                                    st.wait("tensor", sAC, ac_exp)
                                if first:
                                    st.wait("tensor", sDV, dv_osfree[psw])
                                    st.wait("tensor", sAC, ac_rfree)
                                for j in range(2):
                                    cc = 2 * (g % 4) + j
                                    st.emit("tensor", lambda e, p=pt[gp], j=j: e.matmul(
                                        psR[0:1, :], self.ones_bf[:],
                                        p[0:128, j * SW:(j + 1) * SW],
                                        start=(first and j == 0), stop=(last and j == 1),
                                        skip_group_check=True), None)
                                    st.emit("tensor", lambda e, p=pt[gp], Vb=Vbuf[pb],
                                            cc=cc, j=j: e.matmul(
                                                psO[psw][:],
                                                Vb[0:128, cc * 128:cc * 128 + 128],
                                                p[0:128, j * SW:(j + 1) * SW],
                                                start=(first and j == 0),
                                                stop=(last and j == 1),
                                                skip_group_check=True),
                                            sPE if j == 1 else None)
                                pe_ptfree[gp] = sPE.n
                                if g % 4 == 3 or last:
                                    pe_kvfree[pb] = sPE.n
                                if last:
                                    pe_qfree[psw] = sPE.n
                            return emit

                        pend[0] = mk_rpv()
                        if g >= 1 and (g - 1) % 4 == 3:
                            nxt = (g - 1) // 4 + 2
                            if nxt < nblk:
                                emit_load(nxt)
                    flush_rpv()
                    spe_sweep = sPE.n

                    st.wait("scalar", sPE, spe_sweep)
                    st.emit("scalar", lambda e: e.activation(
                        out=recip[0:1, :], in_=psR[0:1, :], func=AF.Ln,
                        bias=0.0, scale=1.0), sAC)
                    ac_rfree = sAC.n
                    st.wait("scalar", sAC)
                    st.emit("scalar", lambda e: e.activation(
                        out=recip[0:1, :], in_=recip[0:1, :], func=AF.Exp,
                        bias=0.0, scale=-1.0), sAC)
                    ac_rec = sAC.n

                    def mk_sweepend(h=h, kcls=kcls, psw=psw, ac_rec=ac_rec):
                        def emit():
                            nonlocal_dv = None
                            st.wait("tensor", sAC, ac_rec)
                            st.wait("tensor", sDV, self._p2_dvb[0])
                            st.emit("tensor", lambda e: e.matmul(
                                psB[:], self.onesrow_f[0:1, :], recip[0:1, :],
                                start=True, stop=True, skip_group_check=True), sPE)
                            spe_bc = sPE.n
                            st.wait("vector", sPE, spe_bc)
                            st.emit("vector", lambda e: e.tensor_copy(
                                recip_bc[:], psB[:]), sDV)
                            self._p2_dvb[0] = self.sDV.n
                            st.wait("vector", sDV, self._p2_dvb[0])
                            st.wait("vector", sOT[psw], sOT[psw].n)
                            st.emit("vector", lambda e, pO=psO[psw], o=ot_sb[psw]:
                                    e.tensor_tensor(o[:], pO[:], recip_bc[:],
                                                    op=ALU.mult), sDV)
                            dv_osfree[psw] = self.sDV.n
                            st.wait("gpsimd", sDV, dv_osfree[psw])
                            st.emit("gpsimd", lambda e, h=h, k=kcls, o=ot_sb[psw]:
                                    e.dma_start(
                                        out=AP(self.ot_spill, (h * QNT + k - 1) * DH * SW,
                                               [[SW, 128], [1, SW]]),
                                        in_=o[:]), sOT[psw], 16)
                        return emit

                    self._p2_send[0] = mk_sweepend()

            if self._p2_send[0] is not None:
                self._p2_send[0]()
                self._p2_send[0] = None
            st.wait("gpsimd", sOT[0], sOT[0].n)
            st.wait("gpsimd", sOT[1], sOT[1].n)
            st.wait("gpsimd", sWL, sWL.n)
            st.flush(nc)

    def p3(self):
        nc = self.nc
        sPE, sDV, sAC, sST = self.sPE, self.sDV, self.sAC, self.sST
        sOL = [self.sOLa, self.sOLb]
        sXT = self.sXT
        st = Streams()
        with ExitStack() as p3:
            def rsb(name, shape, dt=F32):
                return p3.enter_context(nc.sbuf_tensor(name, list(shape), dt, side="right"))

            otb = [rsb("otb%d" % i, [128, NCD * 128], BF16) for i in range(2)]
            xt = rsb("xt", [128, D])
            ysb = rsb("ysb", [128, D])
            x1f = rsb("x1f", [128, D])
            x1b = rsb("x1b", [128, D], BF16)
            x1T = rsb("x1T", [128, NCD * 128], BF16)
            xaw = rsb("xaw", [128, ER], BF16)
            xawT = rsb("xawT", [128, ER], BF16)
            stat = rsb("stat", [128, 24])
            mv = rsb("mv", [128, 2])
            rln = rsb("rln", [128, 1])
            eq1 = rsb("eq1", [128, 8])
            l2t = rsb("l2t", [128, 8])
            w8t = rsb("w8t", [128, 8])
            t1 = rsb("t1", [128, 1])
            t2 = rsb("t2", [128, 1])
            te = rsb("te", [128, 1])
            tg1 = rsb("tg1", [128, 1])
            tg2 = rsb("tg2", [128, 1])
            pBig = p3.enter_context(nc.psum_tensor("p3big", [128, D], F32))
            psG = p3.enter_context(nc.psum_tensor("p3g", [128, ER], F32))
            psL = p3.enter_context(nc.psum_tensor("p3l", [128, NE], F32))
            psT = [p3.enter_context(nc.psum_tensor("p3t%d" % i, [128, 1024], BF16))
                   for i in range(2)]

            pe_otfree = [0, 0]
            dv_pbig = 0
            dv_xt = 0

            st.wait("tensor", self.sWL, self.sWL.n)
            for i in range(NT * 2):
                par = i % 2
                lt, hf = i // 2, i % 2

                st.wait("sync", sPE, pe_otfree[par])
                for h in range(H):
                    st.emit("sync", lambda e, h=h, i=i, b=otb[par]: e.dma_start(
                        out=b[0:128, h * 128:h * 128 + 128],
                        in_=AP(self.ot_spill, (h * QNT + i // 4) * DH * SW + (i % 4) * 128,
                               [[SW, 128], [1, 128]])), sOL[par], 16)
                st.wait("sync", sDV, dv_xt)
                st.emit("sync", lambda e, i=i: e.dma_start(
                    out=xt[:], in_=AP(self.x_own, i * 128 * D, [[D, 128], [1, D]])),
                    sXT, 16)
                xtld = self.sXT.n

                st.wait("tensor", sOL[par], sOL[par].n)
                st.wait("tensor", sDV, dv_pbig)
                for cd in range(NCD):
                    for vs in range(4):
                        st.emit("tensor", lambda e, cd=cd, vs=vs, b=otb[par]: e.matmul(
                            pBig[0:128, vs * 512:(vs + 1) * 512],
                            b[0:128, cd * 128:cd * 128 + 128],
                            self.WoT[0:128, cd * D + vs * 512:cd * D + vs * 512 + 512],
                            start=(cd == 0), stop=False, skip_group_check=True), None)
                for vs in range(4):
                    st.emit("tensor", lambda e, vs=vs: e.matmul(
                        pBig[0:128, vs * 512:(vs + 1) * 512], self.onesrow_bf[0:1, :],
                        self.bo_r[0:1, vs * 512:(vs + 1) * 512],
                        start=False, stop=True, skip_group_check=True),
                        sPE if vs == 3 else None)
                spe_wo = self.sPE.n
                pe_otfree[par] = spe_wo

                st.wait("vector", sPE, spe_wo)
                st.wait("vector", sXT, xtld)
                st.wait("vector", sST, self.sST.n)
                st.emit("vector", lambda e: e.tensor_tensor(ysb[:], pBig[:], xt[:],
                                                            op=ALU.add), sDV)
                dv_xt = self.sDV.n
                st.wait("vector", sDV)
                for cst in range(4):
                    st.emit("vector", lambda e, cst=cst: e.bn_stats(
                        stat[0:128, cst * 6:cst * 6 + 6],
                        ysb[0:128, cst * 512:(cst + 1) * 512]), sDV)
                st.wait("vector", sDV)
                st.emit("vector", lambda e: e.bn_aggr(mv[:], stat[:]), sDV)
                st.wait("scalar", sDV, self.sDV.n)
                st.emit("scalar", lambda e: e.activation(out=rln[:], in_=mv[0:128, 1:2],
                                                         func=AF.Ln, bias=self.epsb[:],
                                                         scale=1.0), sAC)
                st.wait("scalar", sAC)
                st.emit("scalar", lambda e: e.activation(out=rln[:], in_=rln[:],
                                                         func=AF.Exp, bias=0.0,
                                                         scale=-0.5), sAC)
                st.wait("vector", sAC)
                st.emit("vector", lambda e: e.tensor_scalar(
                    x1f[:], ysb[:], mv[0:128, 0:1], rln[:], op0=ALU.subtract,
                    op1=ALU.mult), sDV)
                st.wait("vector", sDV)
                st.emit("vector", lambda e: e.tensor_copy(x1b[:], x1f[:]), sDV)
                dv_x1 = self.sDV.n
                if self.taps:
                    st.wait("gpsimd", sDV, self.sDV.n)
                    st.emit("gpsimd", lambda e, i=i: e.dma_start(
                        out=AP(self.dbg2, i * 128 * D, [[D, 128], [1, D]]),
                        in_=x1b[:]), self.sGP, 16)
                    st.wait("vector", self.sGP, self.sGP.n)

                st.wait("tensor", sDV, dv_x1)
                for bank in range(2):
                    for j in range(8):
                        cd = bank * 8 + j
                        st.emit("tensor", lambda e, cd=cd, j=j, p=psT[bank]: e.transpose(
                            p[0:128, j * 128:j * 128 + 128],
                            x1b[0:128, cd * 128:cd * 128 + 128], self.identb[:]),
                            sPE if j == 7 else None)
                    st.wait("vector", sPE)
                    st.emit("vector", lambda e, bank=bank, p=psT[bank]: e.tensor_copy(
                        x1T[0:128, bank * 1024:(bank + 1) * 1024], p[:]), sDV)
                dv_x1T = self.sDV.n

                st.wait("tensor", sDV, dv_x1T)
                for cd in range(NCD):
                    for vs in range(4):
                        st.emit("tensor", lambda e, cd=cd, vs=vs: e.matmul(
                            pBig[0:128, vs * 512:(vs + 1) * 512],
                            x1T[0:128, cd * 128:cd * 128 + 128],
                            self.WffnT[0:128, cd * D + vs * 512:cd * D + vs * 512 + 512],
                            start=(cd == 0), stop=False, skip_group_check=True), None)
                    st.emit("tensor", lambda e, cd=cd: e.matmul(
                        psL[0:128, 0:NE], x1T[0:128, cd * 128:cd * 128 + 128],
                        self.gateT[0:128, cd * NE:(cd + 1) * NE],
                        start=(cd == 0), stop=False, skip_group_check=True), None)
                    st.emit("tensor", lambda e, cd=cd: e.matmul(
                        psG[0:128, 0:ER], x1T[0:128, cd * 128:cd * 128 + 128],
                        self.AcatT[0:128, cd * ER:(cd + 1) * ER],
                        start=(cd == 0), stop=(cd == NCD - 1), skip_group_check=True), None)
                for vs in range(4):
                    st.emit("tensor", lambda e, vs=vs: e.matmul(
                        pBig[0:128, vs * 512:(vs + 1) * 512], self.onesrow_bf[0:1, :],
                        self.bffn_r[0:1, vs * 512:(vs + 1) * 512],
                        start=False, stop=False, skip_group_check=True), None)
                st.emit("tensor", lambda e: e.matmul(
                    psL[0:128, 0:NE], self.onesrow_bf[0:1, :], self.gateb_r[0:1, :],
                    start=False, stop=True, skip_group_check=True), sPE)
                spe_gx = self.sPE.n

                st.wait("vector", sPE, spe_gx)
                st.emit("vector", lambda e: e.tensor_reduce(
                    t1[:], psL[0:128, 0:NE], axis=AX.X, op=ALU.max), sDV)
                st.wait("vector", sDV)
                st.emit("vector", lambda e: e.tensor_scalar(
                    eq1[:], psL[0:128, 0:NE], t1[:], None, op0=ALU.is_equal), sDV)
                st.wait("vector", sDV)
                st.emit("vector", lambda e: e.scalar_tensor_tensor(
                    l2t[:], eq1[:], -1e30, psL[0:128, 0:NE], op0=ALU.mult,
                    op1=ALU.add), sDV)
                st.wait("vector", sDV)
                st.emit("vector", lambda e: e.tensor_reduce(
                    t2[:], l2t[:], axis=AX.X, op=ALU.max), sDV)
                st.wait("vector", sDV)
                st.emit("vector", lambda e: e.tensor_tensor(
                    te[:], t2[:], t1[:], op=ALU.subtract), sDV)
                st.wait("scalar", sDV, self.sDV.n)
                st.emit("scalar", lambda e: e.activation(out=te[:], in_=te[:], func=AF.Exp,
                                                         bias=0.0, scale=1.0), sAC)
                st.wait("vector", sAC)
                st.emit("vector", lambda e: e.tensor_scalar(
                    tg1[:], te[:], 1.0, None, op0=ALU.add), sDV)
                st.wait("vector", sDV)
                st.emit("vector", lambda e: e.reciprocal(tg1[:], tg1[:]), sDV)
                st.wait("vector", sDV)
                st.emit("vector", lambda e: e.tensor_scalar(
                    tg2[:], tg1[:], -1.0, 1.0, op0=ALU.mult, op1=ALU.add), sDV)
                st.emit("vector", lambda e: e.tensor_scalar(
                    eq1[:], eq1[:], tg1[:], None, op0=ALU.mult), sDV)
                st.emit("vector", lambda e: e.tensor_scalar(
                    l2t[:], l2t[:], t2[:], None, op0=ALU.is_equal), sDV)
                st.wait("vector", sDV)
                st.emit("vector", lambda e: e.scalar_tensor_tensor(
                    w8t[:], l2t[:], tg2[:], eq1[:], op0=ALU.mult, op1=ALU.add), sDV)
                st.wait("vector", sDV)
                st.emit("vector", lambda e: e.tensor_tensor(
                    xaw[:].rearrange("p (e r) -> p e r", r=RLORA),
                    AP(psG, 0, [[ER, 128], [16, NE], [1, RLORA]]),
                    w8t[:].unsqueeze(2).broadcast_to([128, NE, RLORA]),
                    op=ALU.mult), sDV)
                dv_xaw = self.sDV.n

                st.wait("tensor", sDV, dv_xaw)
                st.emit("tensor", lambda e: e.transpose(
                    psT[0][0:128, 0:128], xaw[:], self.identb[:]), sPE)
                st.wait("vector", sPE)
                st.emit("vector", lambda e: e.tensor_copy(
                    xawT[:], psT[0][0:128, 0:128]), sDV)
                st.wait("tensor", sDV)
                for vs in range(4):
                    st.emit("tensor", lambda e, vs=vs: e.matmul(
                        pBig[0:128, vs * 512:(vs + 1) * 512], xawT[:],
                        self.BcatT[0:128, vs * 512:(vs + 1) * 512],
                        start=False, stop=True, skip_group_check=True),
                        sPE if vs == 3 else None)
                spe_ffn = self.sPE.n

                st.wait("vector", sPE, spe_ffn)
                st.emit("vector", lambda e: e.tensor_tensor(ysb[:], pBig[:], x1f[:],
                                                            op=ALU.add), sDV)
                dv_pbig = self.sDV.n
                if self.taps:
                    st.wait("gpsimd", sDV, self.sDV.n)
                    st.emit("gpsimd", lambda e, i=i: e.dma_start(
                        out=AP(self.dbg1, i * 128 * D, [[D, 128], [1, D]]),
                        in_=ysb[:]), self.sGP, 16)
                    st.emit("gpsimd", lambda e, i=i: e.dma_start(
                        out=AP(self.dbg3, i * 128 * ER, [[ER, 128], [1, ER]]),
                        in_=xaw[:]), self.sGP, 16)
                    st.wait("vector", self.sGP, self.sGP.n)
                st.wait("vector", sDV)
                for cst in range(4):
                    st.emit("vector", lambda e, cst=cst: e.bn_stats(
                        stat[0:128, cst * 6:cst * 6 + 6],
                        ysb[0:128, cst * 512:(cst + 1) * 512]), sDV)
                st.wait("vector", sDV)
                st.emit("vector", lambda e: e.bn_aggr(mv[:], stat[:]), sDV)
                st.wait("scalar", sDV, self.sDV.n)
                st.emit("scalar", lambda e: e.activation(out=rln[:], in_=mv[0:128, 1:2],
                                                         func=AF.Ln, bias=self.epsb[:],
                                                         scale=1.0), sAC)
                st.wait("scalar", sAC)
                st.emit("scalar", lambda e: e.activation(out=rln[:], in_=rln[:],
                                                         func=AF.Exp, bias=0.0,
                                                         scale=-0.5), sAC)
                st.wait("vector", sAC)
                st.emit("vector", lambda e: e.tensor_scalar(
                    ysb[:], ysb[:], mv[0:128, 0:1], rln[:], op0=ALU.subtract,
                    op1=ALU.mult), sDV)
                st.wait("sync", sDV)
                st.emit("sync", lambda e, i=i: e.dma_start(
                    out=AP(self.out, i * 128 * D, [[D, 128], [1, D]]), in_=ysb[:]),
                    sST, 16)

            st.wait("sync", sST)
            st.wait("gpsimd", sST, self.sST.n)
            st.flush(nc)


def _rotary_tables():
    j = np.arange(32, dtype=np.float64)
    f = (1.0 / 10000.0) ** ((2.0 * j) / 64.0)
    h = np.arange(H, dtype=np.float64)
    ang = h[:, None] * f[None, :]
    c, s = np.cos(ang), np.sin(ang)
    C = np.ones(D, dtype=np.float64)
    S = np.zeros(D, dtype=np.float64)
    for hh in range(H):
        C[128 * hh + 64:128 * hh + 96] = c[hh]
        C[128 * hh + 96:128 * hh + 128] = c[hh]
        S[128 * hh + 64:128 * hh + 96] = s[hh]
        S[128 * hh + 96:128 * hh + 128] = -s[hh]
    return (np.ascontiguousarray(np.broadcast_to(C.astype(np.float32), (128, D))),
            np.ascontiguousarray(np.broadcast_to(S.astype(np.float32), (128, D))))


def make_in_maps(inputs):
    C, S = _rotary_tables()
    x = np.asarray(inputs["L0_attn_x"], dtype=np.float32)
    f32c = lambda k: np.ascontiguousarray(np.asarray(inputs[k], dtype=np.float32))
    shared = {
        "w_dq": f32c("L0_attn_W_d_q"), "b_dq": f32c("L0_attn_b_d_q"),
        "qng": f32c("L0_attn_q_norm_g"), "qnb": f32c("L0_attn_q_norm_b"),
        "w_uq": f32c("L0_attn_W_u_q"), "b_uq": f32c("L0_attn_b_u_q"),
        "w_dkv": f32c("L0_attn_W_d_kv"), "b_dkv": f32c("L0_attn_b_d_kv"),
        "kvg": f32c("L0_attn_kv_norm_g"), "kvb": f32c("L0_attn_kv_norm_b"),
        "w_ukv": f32c("L0_attn_W_u_kv"), "b_ukv": f32c("L0_attn_b_u_kv"),
        "w_o": f32c("L0_attn_W_o"), "b_o": f32c("L0_attn_b_o"),
        "w_ffn": f32c("L0_ffn_W"), "b_ffn": f32c("L0_ffn_b"),
        "lora_a": f32c("L0_ffn_lora_A"), "lora_b": f32c("L0_ffn_lora_B"),
        "gate_w": f32c("L0_ffn_gate_w"), "gate_b": f32c("L0_ffn_gate_b"),
        "cosrep": C, "sinrep": S,
    }
    in_maps = []
    for c in range(8):
        b, rho = c // 4, c % 4
        tiles = [x[b, 512 * (4 * j + rho):512 * (4 * j + rho) + 512] for j in range(QNT)]
        m = dict(shared)
        m["x_own"] = np.ascontiguousarray(np.concatenate(tiles, axis=0))
        m["mthr"] = np.full((128, 1), -512.0 * rho, dtype=np.float32)
        in_maps.append(m)
    return in_maps


def assemble_out(results):
    y = np.empty((B, T, D), dtype=np.float32)
    for c in range(8):
        b, rho = c // 4, c % 4
        o = np.asarray(results[c]["out"])
        for j in range(QNT):
            y[b, 512 * (4 * j + rho):512 * (4 * j + rho) + 512] = o[512 * j:512 * j + 512]
    return y


def kernel(**inputs):
    mk = MK()
    nc = mk.build()
    in_maps = make_in_maps(inputs)
    res = run_bass_kernel_spmd(nc, in_maps, core_ids=list(range(8)))
    return assemble_out(res.results)


# version 10
# speedup vs baseline: 1.1658x; 1.0275x over previous
import sys
if "/opt/trn_rl_repo" not in sys.path:
    sys.path.insert(0, "/opt/trn_rl_repo")

from contextlib import ExitStack
import math
import numpy as np

import concourse.bass as bass
import concourse.mybir as mybir
from concourse.bass_utils import run_bass_kernel_spmd

F32 = mybir.dt.float32
BF16 = mybir.dt.bfloat16
AF = mybir.ActivationFunctionType
ALU = mybir.AluOpType
AX = mybir.AxisListType

B, T, D, H, DH = 2, 8192, 2048, 16, 128
DQ, DKV, DKR = 682, 1024, 64
DCKV = DKV + DKR
NCQ = 6
CQ_TAIL = DQ - 5 * 128
NCKV = 8
NCD = 16
TCORE = 2048
NT = 8
NSTRIP = 4
SW = 512
QG = 256
ATT_SCALE = 1.0 / math.sqrt(DH)
EPS = 1e-5
NE, RLORA = 8, 16
ER = NE * RLORA

QNT = 4
K_SBLK = H * DH * SW
V_SBLK = H * SW * DH
SBLK = K_SBLK + V_SBLK + 1024
KOFFs, VOFFs, NOFFs = 0, K_SBLK, K_SBLK + V_SBLK
CONTRIB = NSTRIP * SBLK


def AP(t, off, dims):
    return bass.AP(t, off, [list(d) for d in dims])


class Sem:
    __slots__ = ("h", "n")

    def __init__(self, h):
        self.h = h
        self.n = 0


class Streams:
    NAMES = ("sync", "tensor", "vector", "scalar", "gpsimd")

    def __init__(self):
        self.s = {n: [] for n in self.NAMES}

    def emit(self, eng, fn, sem=None, k=1):
        if sem is None:
            self.s[eng].append(fn)
            return None
        sem.n += k
        self.s[eng].append(lambda e, f=fn, sh=sem.h, kk=k: f(e).then_inc(sh, kk))
        return sem.n

    def wait(self, eng, sem, val=None):
        v = sem.n if val is None else val
        if v > 0:
            self.s[eng].append(lambda e, sh=sem.h, vv=v: e.wait_ge(sh, vv))

    def flush(self, nc):
        with nc.Block() as block:
            for n in self.NAMES:
                ops = self.s[n]
                if ops:
                    getattr(block, n)(lambda e, ops=ops: [f(e) for f in ops])
        self.s = {n: [] for n in self.NAMES}


class MK:
    def __init__(self, taps=(), stop_after=None):
        self.taps = list(taps)
        self.stop_after = stop_after
        self.nc = bass.Bass("TRN2", target_bir_lowering=False)

    def build(self):
        nc = self.nc

        def din(name, shape):
            return nc.dram_tensor(name, list(shape), F32, kind="ExternalInput")

        self.x_own = din("x_own", [TCORE, D])
        self.w_dq = din("w_dq", [DQ, D])
        self.b_dq = din("b_dq", [DQ])
        self.qng = din("qng", [DQ])
        self.qnb = din("qnb", [DQ])
        self.w_uq = din("w_uq", [D, DQ])
        self.b_uq = din("b_uq", [D])
        self.w_dkv = din("w_dkv", [DCKV, D])
        self.b_dkv = din("b_dkv", [DCKV])
        self.kvg = din("kvg", [DKV])
        self.kvb = din("kvb", [DKV])
        self.w_ukv = din("w_ukv", [DKV + D, DKV])
        self.b_ukv = din("b_ukv", [DKV + D])
        self.w_o = din("w_o", [D, D])
        self.b_o = din("b_o", [D])
        self.w_ffn = din("w_ffn", [D, D])
        self.b_ffn = din("b_ffn", [D])
        self.lora_a = din("lora_a", [NE, RLORA, D])
        self.lora_b = din("lora_b", [NE, D, RLORA])
        self.gate_w = din("gate_w", [NE, D])
        self.gate_b = din("gate_b", [NE])
        self.cosrep = din("cosrep", [128, D])
        self.sinrep = din("sinrep", [128, D])
        self.mthr = din("mthr", [128, 1])

        self.out = nc.dram_tensor("out", [TCORE, D], F32, kind="ExternalOutput")

        self.contrib = nc.dram_tensor("contrib", [CONTRIB], BF16)
        self.gathered = nc.dram_tensor("gathered", [4 * CONTRIB], BF16)
        self.q_spill = nc.dram_tensor("q_spill", [H * QNT * DH * SW], BF16)
        self.ot_spill = nc.dram_tensor("ot_spill", [H * QNT * DH * SW], BF16)
        self.wo_sp = nc.dram_tensor("wo_sp", [D * D], BF16)
        self.wffn_sp = nc.dram_tensor("wffn_sp", [D * D], BF16)
        self.acat_sp = nc.dram_tensor("acat_sp", [ER * D], BF16)
        self.bcat_sp = nc.dram_tensor("bcat_sp", [ER * D], BF16)
        self.gate_sp = nc.dram_tensor("gate_sp", [128 * NCD * NE], BF16)
        self.dbg1 = nc.dram_tensor("dbg1", [TCORE * D], BF16)
        self.dbg2 = nc.dram_tensor("dbg2", [TCORE * D], BF16)
        self.dbg3 = nc.dram_tensor("dbg3", [TCORE * ER], BF16)

        es = ExitStack()
        self.es = es

        def sb(name, shape, dt=F32, side="left"):
            return es.enter_context(nc.sbuf_tensor(name, list(shape), dt, side=side))
        self.sb = sb

        with es:
            self.identb = sb("identb", [128, 128], BF16)
            self.identf = sb("identf", [128, 128], F32)
            self.ones_bf = sb("ones_bf", [128, 1], BF16)
            self.onesA = sb("onesA", [128, 1], BF16)
            self.onesB = sb("onesB", [128, 1], BF16)
            self.onesrow_bf = sb("onesrow_bf", [1, 128], BF16)
            self.onesrow_f = sb("onesrow_f", [1, 128], F32)
            self.epsb = sb("epsb", [128, 1], F32)
            self.bdq_c = sb("bdq_c", [128, NCQ], F32)
            self.buq_c = sb("buq_c", [128, NCD], F32)
            self.bdkv_c = sb("bdkv_c", [128, 9], F32)
            self.buk_c = sb("buk_c", [128, NCKV], F32)
            self.qng_c = sb("qng_c", [128, NCQ], F32)
            self.qnb_c = sb("qnb_c", [128, NCQ], F32)
            self.kvg_c = sb("kvg_c", [128, NCKV], F32)
            self.kvb_c = sb("kvb_c", [128, NCKV], F32)
            self.bukvv_r = sb("bukvv_r", [1, D], BF16)
            self.maxq2 = sb("maxq2", [1, H * QNT], F32)
            self.maxk2 = sb("maxk2", [1, H], F32)
            self.negm_bc = sb("negm_bc", [128, QNT * H], F32)

            for nm in ("sLD", "sGP", "sGE", "sPE", "sDV", "sAC", "sST", "sCC",
                       "sLDa", "sLDb", "sQTa", "sQTb", "sKNa", "sKNb", "sKR",
                       "sXa", "sXb", "sKVa", "sKVb", "sQBa", "sQBb", "sOTa", "sOTb",
                       "sWL", "sOLa", "sOLb", "sXT"):
                setattr(self, nm, Sem(es.enter_context(nc.semaphore(nm))))

            cs = ExitStack()
            self.cosr = cs.enter_context(nc.sbuf_tensor("cosr", [128, D], F32, side="right"))
            self.sinr = cs.enter_context(nc.sbuf_tensor("sinr", [128, D], F32, side="right"))
            with nc.named_scope('p0c'):
                self.p0_consts()
            wa = ExitStack()
            self.WdqT = wa.enter_context(nc.sbuf_tensor("WdqT", [128, NCD * DQ], BF16))
            self.WuqT = wa.enter_context(nc.sbuf_tensor("WuqT", [128, NCQ * D], BF16))
            self.WdkvT = wa.enter_context(nc.sbuf_tensor("WdkvT", [128, NCD * DCKV], BF16))
            self.WukT = wa.enter_context(nc.sbuf_tensor("WukT", [128, NCKV * DKV], BF16))
            self.WuvT = wa.enter_context(nc.sbuf_tensor("WuvT", [128, NCKV * D], BF16))
            with nc.named_scope('p0w'):
                self.p0_weights()
            cs.close()
            if self.stop_after == "p0":
                wa.close()
                self._emit_taps()
                return nc
            with nc.named_scope('p1'):
                self.p1()
            wa.close()
            self.mask = sb("mask", [128, 16 * SW], BF16)
            self.bo_r = sb("bo_r", [1, D], BF16)
            self.bffn_r = sb("bffn_r", [1, D], BF16)
            self.gateb_r = sb("gateb_r", [1, NE], BF16)
            self.WoT = sb("WoT", [128, NCD * D], BF16)
            self.WffnT = sb("WffnT", [128, NCD * D], BF16)
            self.AcatT = sb("AcatT", [128, NCD * ER], BF16)
            self.gateT = sb("gateT", [128, NCD * NE], BF16)
            self.BcatT = sb("BcatT", [128, D], BF16)
            with nc.named_scope('ag'):
                self.ag()
            if self.stop_after == "p1":
                self._emit_taps()
                return nc
            with nc.named_scope('p2'):
                self.p2()
            if self.stop_after == "p2":
                self._emit_taps()
                return nc
            with nc.named_scope('p3'):
                self.p3()
            self._emit_taps()
        return nc

    def _emit_taps(self):
        if not self.taps:
            return
        st = Streams()
        tmap = {"contrib": (self.contrib, CONTRIB),
                "gathered": (self.gathered, 4 * CONTRIB),
                "q_spill": (self.q_spill, H * NT * DH * QG),
                "ot_spill": (self.ot_spill, H * NT * DH * QG),
                "wo_sp": (self.wo_sp, D * D),
                "wffn_sp": (self.wffn_sp, D * D),
                "acat_sp": (self.acat_sp, ER * D),
                "bcat_sp": (self.bcat_sp, ER * D),
                "gate_sp": (self.gate_sp, 128 * NCD * NE),
                "dbg1": (self.dbg1, TCORE * D), "dbg2": (self.dbg2, TCORE * D),
                "dbg3": (self.dbg3, TCORE * ER)}
        for t in self.taps:
            th, size = tmap[t]
            tout = self.nc.dram_tensor("tap_" + t, [size], BF16, kind="ExternalOutput")
            per = size // 128
            st.emit("sync", lambda e, a=tout, b=th, p=per: e.dma_start(
                out=AP(a, 0, [[p, 128], [1, p]]), in_=AP(b, 0, [[p, 128], [1, p]])),
                self.sST, 16)
        st.wait("sync", self.sST)
        st.flush(self.nc)

    def p0_consts(self):
        nc = self.nc
        sLD, sGP, sGE, sDV, sAC = self.sLD, self.sGP, self.sGE, self.sDV, self.sAC
        st = Streams()
        with ExitStack() as p0:
            vrow = p0.enter_context(nc.sbuf_tensor("p0vrow", [1, DKV + D], F32, side="right"))
            vrow2 = p0.enter_context(nc.sbuf_tensor("p0vrow2", [1, D], F32, side="right"))
            rscr = p0.enter_context(nc.sbuf_tensor("p0rscr", [1, D], F32, side="right"))

            st.emit("gpsimd", lambda e: e.iota(self.identf[:], [[1, 128]], channel_multiplier=-1,
                                               allow_small_or_imprecise_dtypes=True), sGE)
            st.emit("gpsimd", lambda e: e.memset(self.ones_bf[:], 1.0), sGE)
            st.emit("gpsimd", lambda e: e.memset(self.onesrow_bf[:], 1.0), sGE)
            st.emit("gpsimd", lambda e: e.memset(self.onesrow_f[:], 1.0), sGE)
            st.emit("gpsimd", lambda e: e.memset(self.epsb[:], EPS), sGE)
            st.emit("gpsimd", lambda e: e.iota(self.onesA[:], [[0, 1]], channel_multiplier=1,
                                               allow_small_or_imprecise_dtypes=True), sGE)
            st.emit("sync", lambda e: e.dma_start(out=self.cosr[:], in_=self.cosrep.ap()), sLD, 16)
            st.emit("sync", lambda e: e.dma_start(out=self.sinr[:], in_=self.sinrep.ap()), sLD, 16)
            st.wait("vector", sGE)
            st.wait("vector", sLD)
            st.emit("vector", lambda e: e.tensor_scalar(self.identf[:], self.identf[:], 0.0, None,
                                                        op0=ALU.is_equal), sDV)
            st.emit("vector", lambda e: e.tensor_scalar(self.onesB[:], self.onesA[:], 63.5, None,
                                                        op0=ALU.is_gt), sDV)
            st.wait("vector", sDV)
            st.emit("vector", lambda e: e.tensor_scalar(self.onesA[:], self.onesA[:], 63.5, None,
                                                        op0=ALU.is_lt), sDV)
            st.wait("vector", sDV)
            st.emit("vector", lambda e: e.tensor_copy(self.identb[:], self.identf[:]), sDV)

            def vec_load(dram, n):
                st.wait("sync", sGP)
                st.wait("sync", sDV)
                st.emit("sync", lambda e, d=dram, nn=n: e.dma_start(
                    out=vrow[0:1, 0:nn], in_=AP(d, 0, [[nn, 1], [1, nn]])), sLD, 16)

            def col_store(dst, n, nch, src=None, src_off=0):
                s = vrow if src is None else src
                st.wait("gpsimd", sLD)
                st.wait("gpsimd", sDV)
                st.wait("gpsimd", self.sGE)
                for c in range(nch):
                    w = min(128, n - c * 128)
                    st.emit("gpsimd", lambda e, dd=dst, cc=c, ww=w, so=src_off, ss=s: e.dma_start(
                        out=dd[0:ww, cc:cc + 1],
                        in_=ss[0:1, so + cc * 128:so + cc * 128 + ww]), sGP, 16)

            for dram, dst in ((self.b_dq, self.bdq_c), (self.qng, self.qng_c),
                              (self.qnb, self.qnb_c)):
                st.emit("gpsimd", lambda e, dd=dst: e.memset(dd[:], 0.0), sGE)
                vec_load(dram, DQ)
                col_store(dst, DQ, NCQ)
            st.emit("gpsimd", lambda e: e.memset(self.bdkv_c[:], 0.0), sGE)
            vec_load(self.b_dkv, DCKV)
            col_store(self.bdkv_c, DCKV, 9)
            for dram, dst in ((self.kvg, self.kvg_c), (self.kvb, self.kvb_c)):
                vec_load(dram, DKV)
                col_store(dst, DKV, NCKV)
            vec_load(self.b_ukv, DKV + D)
            st.wait("gpsimd", sLD)
            for co in range(NCKV):
                st.emit("gpsimd", lambda e, cc=co: e.dma_start(
                    out=self.buk_c[0:64, cc:cc + 1],
                    in_=vrow[0:1, 384 * cc:384 * cc + 64]), sGP, 16)
                st.emit("gpsimd", lambda e, cc=co: e.dma_start(
                    out=self.buk_c[64:128, cc:cc + 1],
                    in_=vrow[0:1, 384 * cc + 192:384 * cc + 192 + 64]), sGP, 16)
            st.wait("vector", sLD)
            st.emit("vector", lambda e: e.tensor_copy(
                self.bukvv_r[:], AP(vrow, 64, [[DKV + D, 1], [192, H], [1, 128]])), sDV)

            rc = lambda t, w, p=1: AP(t, 64, [[w, p], [128, H], [1, 64]])
            rp = lambda t, w, p=1: AP(t, 96, [[w, p], [128, H], [-32, 2], [1, 32]])
            vec_load(self.b_uq, D)
            st.wait("vector", sLD)
            st.emit("vector", lambda e: e.tensor_copy(vrow2[0:1, :], vrow[0:1, 0:D]), sDV)
            st.wait("vector", sDV)
            st.emit("vector", lambda e: e.tensor_tensor(
                rc(vrow2, D), rc(vrow, DKV + D), rc(self.cosr, D), op=ALU.mult), sDV)
            st.emit("vector", lambda e: e.tensor_tensor(
                rc(rscr, D), rp(vrow, DKV + D), rc(self.sinr, D), op=ALU.mult), sDV)
            st.wait("vector", sDV)
            st.emit("vector", lambda e: e.tensor_tensor(
                rc(vrow2, D), rc(vrow2, D), rc(rscr, D), op=ALU.add), sDV)
            col_store(self.buq_c, D, NCD, src=vrow2)
            st.wait("sync", sGP)
            st.flush(nc)

    def p0_weights(self):
        nc = self.nc
        sLD, sGP, sPE, sDV = self.sLD, self.sGP, self.sPE, self.sDV
        st = Streams()
        with ExitStack() as p0:
            stage = [p0.enter_context(nc.sbuf_tensor("p0st%d" % i, [128, D], F32, side="right"))
                     for i in range(2)]
            rot = p0.enter_context(nc.sbuf_tensor("p0rot", [128, D], F32, side="right"))
            rots = p0.enter_context(nc.sbuf_tensor("p0rots", [128, D], F32, side="right"))
            rots2 = p0.enter_context(nc.sbuf_tensor("p0rots2", [128, D], F32, side="right"))
            spb = p0.enter_context(nc.sbuf_tensor("p0spb", [128, D], BF16, side="right"))
            ptr = [p0.enter_context(nc.psum_tensor("p0ps%d" % i, [128, 512], F32))
                   for i in range(2)]
            stage_free = [0, 0]
            ps_free = [0, 0]
            sLDs = [self.sLDa, self.sLDb]
            cur = [0]
            trn = [0]

            def load_stripe(dram, off_elems, rh, cols):
                i = cur[0] % 2
                cur[0] += 1
                b = stage[i]
                st.wait("sync", sPE, stage_free[i])
                st.emit("sync", lambda e, d=dram, ro=off_elems, r=rh, C=cols, bb=b: e.dma_start(
                    out=bb[0:r, 0:C], in_=AP(d, ro, [[C, r], [1, C]])), sLDs[i], 16)
                return i

            def tr(i, c0, rh, cw, evac):
                k = trn[0] % 2
                trn[0] += 1
                p = ptr[k]
                st.wait("tensor", sLDs[i], sLDs[i].n)
                st.wait("tensor", sDV, ps_free[k])
                st.emit("tensor", lambda e, bb=stage[i], pp=p, r=rh, cc=c0, w=cw: e.transpose(
                    pp[0:w, 0:r], bb[0:r, cc:cc + w], self.identf[0:r, 0:r]), sPE)
                st.wait("vector", sPE)
                evac(p)
                ps_free[k] = sDV.n
                stage_free[i] = sPE.n

            def cp_evac(tgt, coff, rh, cw=128):
                def f(p):
                    st.emit("vector", lambda e, pp=p, t=tgt, co=coff, r=rh, w=cw: e.tensor_copy(
                        t[0:w, co:co + r], pp[0:w, 0:r]), sDV)
                return f

            def tr4(i, c0, rh, tgt, coff, blk):
                k = trn[0] % 2
                trn[0] += 1
                p = ptr[k]
                st.wait("tensor", sLDs[i], sLDs[i].n)
                st.wait("tensor", sDV, ps_free[k])
                for j in range(4):
                    st.emit("tensor", lambda e, bb=stage[i], pp=p, r=rh,
                            cc=c0 + 128 * j, j=j: e.transpose(
                                pp[0:128, j * 128:j * 128 + r], bb[0:r, cc:cc + 128],
                                self.identf[0:r, 0:r]), sPE if j == 3 else None)
                st.wait("vector", sPE)
                tw = tgt.shape[1]
                st.emit("vector", lambda e, pp=p, t=tgt, co=coff, r=rh, b=blk, w=tw:
                        e.tensor_copy(AP(t, co, [[w, 128], [b, 4], [1, r]]),
                                      AP(pp, 0, [[512, 128], [128, 4], [1, r]])), sDV)
                ps_free[k] = sDV.n
                stage_free[i] = sPE.n

            def simple_w(dram, R, C, tgt, blk):
                for ro in range((R + 127) // 128):
                    rh = min(128, R - ro * 128)
                    i = load_stripe(dram, ro * 128 * C, rh, C)
                    for cc4 in range(C // 512):
                        tr4(i, cc4 * 512, rh, tgt, (cc4 * 4) * blk + ro * 128, blk)

            simple_w(self.w_dq, DQ, D, self.WdqT, DQ)
            simple_w(self.w_dkv, DCKV, D, self.WdkvT, DCKV)
            for hh in range(H):
                i = load_stripe(self.w_ukv, (192 * hh) * DKV, 64, DKV)
                for cc4 in range(NCKV // 4):
                    tr4(i, cc4 * 512, 64, self.WukT, (cc4 * 4) * DKV + 64 * hh, DKV)
            for hh in range(H):
                i = load_stripe(self.w_ukv, (192 * hh + 64) * DKV, 128, DKV)
                for cc4 in range(NCKV // 4):
                    tr4(i, cc4 * 512, 128, self.WuvT, (cc4 * 4) * D + 128 * hh, D)

            rc = lambda t, w, p: AP(t, 64, [[w, p], [128, H], [1, 64]])
            rp = lambda t, w, p: AP(t, 96, [[w, p], [128, H], [-32, 2], [1, 32]])
            for ccq in range(NCQ):
                cw = min(128, DQ - ccq * 128)
                for ro in range(NCD):
                    i = load_stripe(self.w_uq, ro * 128 * DQ, 128, DQ)
                    tr(i, ccq * 128, 128, cw, cp_evac(rot, ro * 128, 128, cw))
                st.wait("vector", sDV)
                base = ccq * D
                st.emit("vector", lambda e, w=cw, b=base: e.tensor_copy(
                    AP(self.WuqT, b, [[NCQ * D, w], [128, H], [1, 64]]),
                    AP(rot, 0, [[D, w], [128, H], [1, 64]])), sDV)
                st.emit("vector", lambda e, w=cw: e.tensor_tensor(
                    rc(rots, D, w), rc(rot, D, w), rc(self.cosr, D, w), op=ALU.mult), sDV)
                st.emit("vector", lambda e, w=cw: e.tensor_tensor(
                    rc(rots2, D, w), rp(rot, D, w), rc(self.sinr, D, w), op=ALU.mult), sDV)
                st.wait("vector", sDV)
                st.emit("vector", lambda e, w=cw, b=base: e.tensor_tensor(
                    AP(self.WuqT, b + 64, [[NCQ * D, w], [128, H], [1, 64]]),
                    rc(rots, D, w), rc(rots2, D, w), op=ALU.add), sDV)
                st.wait("vector", sDV)

            def spill_stripes(dram, sp, nro, spill_ap_fn):
                for ro in range(nro):
                    i = load_stripe(dram, ro * 128 * D, 128, D)
                    st.wait("vector", sGP)
                    for cc4 in range(NCD // 4):
                        tr4(i, cc4 * 512, 128, spb, cc4 * 512, 128)
                    st.wait("gpsimd", sDV)
                    st.emit("gpsimd", lambda e, s=sp, r=ro, f=spill_ap_fn: e.dma_start(
                        out=f(s, r), in_=spb[:]), sGP, 16)

            wsp_ap = lambda s, r: AP(s, r * 128, [[NCD * D, 128], [D, NCD], [1, 128]])
            spill_stripes(self.w_o, self.wo_sp, NCD, wsp_ap)
            spill_stripes(self.w_ffn, self.wffn_sp, NCD, wsp_ap)

            i = load_stripe(self.lora_a, 0, 128, D)
            st.wait("vector", sGP)
            for cc4 in range(NCD // 4):
                tr4(i, cc4 * 512, 128, spb, cc4 * 512, 128)
            st.wait("gpsimd", sDV)
            st.emit("gpsimd", lambda e: e.dma_start(
                out=AP(self.acat_sp, 0, [[NCD * 128, 128], [1, NCD * 128]]), in_=spb[:]), sGP, 16)

            i = load_stripe(self.gate_w, 0, NE, D)
            st.wait("vector", sGP)
            for cc in range(NCD):
                tr(i, cc * 128, NE, 128, cp_evac(spb, cc * NE, NE))
            st.wait("gpsimd", sDV)
            st.emit("gpsimd", lambda e: e.dma_start(
                out=AP(self.gate_sp, 0, [[NCD * NE, 128], [1, NCD * NE]]),
                in_=spb[0:128, 0:NCD * NE]), sGP, 16)

            st.wait("vector", sGP)
            for co in range(NCD):
                ii = cur[0] % 2
                cur[0] += 1
                b = stage[ii]
                st.wait("sync", sPE, stage_free[ii])
                st.emit("sync", lambda e, c0=co * 128, bb=b: e.dma_start(
                    out=bb[0:128, 0:ER],
                    in_=AP(self.lora_b, c0 * RLORA,
                           [[RLORA, 128], [D * RLORA, NE], [1, RLORA]])), sLDs[ii], 16)
                tr(ii, 0, 128, 128, cp_evac(spb, co * 128, 128))
            st.wait("gpsimd", sDV)
            st.emit("gpsimd", lambda e: e.dma_start(
                out=AP(self.bcat_sp, 0, [[NCD * 128, 128], [1, NCD * 128]]),
                in_=spb[0:128, 0:NCD * 128]), sGP, 16)
            st.wait("gpsimd", sGP)
            st.flush(nc)

    def p1(self):
        nc = self.nc
        sLD, sGP, sPE, sDV, sAC, sST = self.sLD, self.sGP, self.sPE, self.sDV, self.sAC, self.sST
        st = Streams()
        with ExitStack() as p1:
            def rsb(name, shape, dt=F32):
                return p1.enter_context(nc.sbuf_tensor(name, list(shape), dt, side="right"))

            xT = rsb("xT", [128, NCD * SW], BF16)
            kvn = rsb("kvn", [128, NCKV * SW], BF16)
            raw = rsb("raw", [128, NCKV * SW], BF16)
            cqraw = rsb("cqraw", [128, NCQ * SW], BF16)
            cqn = cqraw
            krr = rsb("krr", [64, SW], BF16)
            xv = rsb("xv", [128, 4 * D], BF16)
            cqr, ckvr = cqraw, raw
            xbf, vsb = xv, xv
            sq = [rsb("sq%d" % i, [128, SW], BF16) for i in range(2)]
            sqkr = rsb("sqkr", [64, SW], BF16)
            _s2 = rsb("s2", [128, SW], F32)
            s2 = [_s2, _s2]
            qT = [rsb("qT%d" % i, [128, SW], BF16) for i in range(2)]
            knr = [rsb("knr%d" % i, [128, SW], BF16) for i in range(2)]
            murow = rsb("murow", [1, SW], F32)
            rrow = rsb("rrow", [1, SW], F32)
            varow = rsb("varow", [1, SW], F32)
            kr2row = rsb("kr2row", [1, SW], BF16)
            tiny = rsb("tiny", [1, 8], F32)
            nrmbf = rsb("nrmbf", [1, 1024], BF16)

            pA = [p1.enter_context(nc.psum_tensor("p1a%d" % i, [128, 512], F32))
                  for i in range(2)]
            pS = p1.enter_context(nc.psum_tensor("p1s", [1, 1024], F32))
            pS2 = p1.enter_context(nc.psum_tensor("p1s2", [1, 1024], F32))
            pMu = p1.enter_context(nc.psum_tensor("p1mu", [128, 512], F32))
            pR = p1.enter_context(nc.psum_tensor("p1r", [128, 512], F32))

            self._pa_free = [0, 0]
            self._pa_i = 0
            self._tr_free = [0, 0]
            p1pend = [None]

            def pend_flush():
                if p1pend[0] is not None:
                    p1pend[0]()
                    p1pend[0] = None

            def pend_set(fn):
                pend_flush()
                p1pend[0] = fn
            sQT = [self.sQTa, self.sQTb]
            sKN = [self.sKNa, self.sKNb]
            sKR = self.sKR

            def proj(nchunks, stat_fn, mov, mov_base_fn, out_rows, evac, kdims=None,
                     extra=None):
                k = self._pa_i % 2
                self._pa_i += 1
                p = pA[k]
                st.wait("tensor", sDV, self._pa_free[k])
                for c in range(nchunks):
                    kd = 128 if kdims is None else kdims[c]
                    last = (c == nchunks - 1) and extra is None
                    st.emit("tensor",
                            lambda e, pp=p, s=stat_fn(c), m=mov, mb=mov_base_fn(c), orr=out_rows,
                            r=kd, c0=(c == 0), st_=last: e.matmul(
                                pp[0:orr, 0:SW], s, m[0:r, mb:mb + SW], start=c0, stop=st_,
                                skip_group_check=True),
                            sPE if last else None)
                    if c == 1:
                        pend_flush()
                if extra is not None:
                    extra(p)
                st.wait("vector", sPE)
                evac(p)
                self._pa_free[k] = sDV.n
                return p

            for s in range(NSTRIP):
                st.wait("gpsimd", sPE)
                st.wait("gpsimd", sST)
                st.wait("tensor", sDV)
                sX = [self.sXa, self.sXb]
                for half in range(2):
                    st.emit("gpsimd", lambda e, ss=s, hf=half: e.dma_start(
                        out=xbf[0:128, 0:2 * D] if hf == 0 else xbf[0:128, 2 * D:4 * D],
                        in_=AP(self.x_own, (ss * SW + hf * 256) * D,
                               [[D, 128], [128 * D, 2], [1, D]])), sX[half], 16)
                    st.wait("tensor", sX[half], sX[half].n)
                    for bank in range(4):
                        k = bank % 2
                        p = pMu if k == 0 else pR
                        st.wait("tensor", sDV, self._tr_free[k])
                        for j in range(8):
                            t_idx = bank * 8 + j
                            cx, i = t_idx // 2, t_idx % 2
                            st.emit("tensor", lambda e, pp=p, j_=j, cx_=cx, i_=i, hf=half:
                                    e.transpose(
                                        pp.bitcast(BF16)[0:128, j_ * 128:j_ * 128 + 128],
                                        xbf[0:128, (2 * hf + i_) * D + cx_ * 128:
                                            (2 * hf + i_) * D + cx_ * 128 + 128],
                                        self.identb[:]), sPE if j == 7 else None)
                        st.wait("vector", sPE)
                        st.emit("vector", lambda e, pp=p, b=bank, hf=half: e.tensor_copy(
                            AP(xT, (b * 4) * SW + hf * 256,
                               [[NCD * SW, 128], [SW, 4], [1, 256]]),
                            AP(pp.bitcast(BF16), 0, [[1024, 128], [256, 4], [1, 256]])), sDV)
                        self._tr_free[k] = sDV.n

                st.wait("tensor", sDV)
                for co in range(NCQ):
                    w = 128 if co < 5 else CQ_TAIL

                    def evac_cq(p, co=co, w=w):
                        st.emit("vector", lambda e, pp=p: e.tensor_scalar(
                            cqr[0:w, co * SW:(co + 1) * SW], pp[0:w, 0:SW],
                            self.bdq_c[0:w, co:co + 1], None, op0=ALU.add), sDV)
                        st.wait("vector", sDV)
                        st.emit("vector", lambda e: e.tensor_tensor(
                            sq[co % 2][0:w, 0:SW], cqr[0:w, co * SW:(co + 1) * SW],
                            cqr[0:w, co * SW:(co + 1) * SW], op=ALU.mult), sDV)

                    proj(NCD, lambda c, co=co, w=w:
                         self.WdqT[0:128, c * DQ + co * 128:c * DQ + co * 128 + w],
                         xT, lambda c: c * SW, w, evac_cq)

                    def stat_cq(co=co, w=w, dvn=None):
                        def f():
                            st.wait("tensor", sDV)
                            st.emit("tensor", lambda e: e.matmul(
                                pS[0:1, 0:512], self.ones_bf[0:w, :],
                                cqr[0:w, co * SW:(co + 1) * SW],
                                start=(co == 0), stop=(co == NCQ - 1),
                                skip_group_check=True), None)
                            st.emit("tensor", lambda e: e.matmul(
                                pS[0:1, 512:1024], self.ones_bf[0:w, :],
                                sq[co % 2][0:w, 0:SW],
                                start=(co == 0), stop=(co == NCQ - 1),
                                skip_group_check=True),
                                sPE if co == NCQ - 1 else None)
                        return f
                    pend_set(stat_cq())

                def ckv_section(s=s):
                  for co in range(9):
                    w = 128 if co < 8 else 64

                    def evac_ckv(p, co=co, w=w):
                        tgt = (ckvr, co * SW) if co < 8 else (krr, 0)
                        if co == 8:
                            st.wait("vector", sKR, sKR.n)
                        st.emit("vector", lambda e, pp=p, tg=tgt: e.tensor_scalar(
                            tg[0][0:w, tg[1]:tg[1] + SW], pp[0:w, 0:SW],
                            self.bdkv_c[0:w, co:co + 1], None, op0=ALU.add), sDV)
                        st.wait("vector", sDV)
                        dst_sq = sq[co % 2] if co < 8 else sqkr
                        st.emit("vector", lambda e, tg=tgt, dsq=dst_sq: e.tensor_tensor(
                            dsq[0:w, 0:SW], tg[0][0:w, tg[1]:tg[1] + SW],
                            tg[0][0:w, tg[1]:tg[1] + SW], op=ALU.mult), sDV)

                    proj(NCD, lambda c, co=co, w=w:
                         self.WdkvT[0:128, c * DCKV + co * 128:c * DCKV + co * 128 + w],
                         xT, lambda c: c * SW, w, evac_ckv)
                    if co < 8:
                        def stat_ckv(co=co):
                            def f():
                                st.wait("tensor", sDV)
                                st.emit("tensor", lambda e: e.matmul(
                                    pS2[0:1, 0:512], self.ones_bf[:],
                                    ckvr[0:128, co * SW:(co + 1) * SW],
                                    start=(co == 0), stop=(co == 7),
                                    skip_group_check=True), None)
                                st.emit("tensor", lambda e: e.matmul(
                                    pS2[0:1, 512:1024], self.ones_bf[:], sq[co % 2][:],
                                    start=(co == 0), stop=(co == 7),
                                    skip_group_check=True),
                                    sPE if co == 7 else None)
                            return f
                        pend_set(stat_ckv())

                ckv_section()
                pend_flush()
                self._ln(st, pS, pMu, pR, murow, rrow, varow, DQ, NCQ, cqr, cqn, s2,
                         self.qng_c, self.qnb_c, CQ_TAIL, SW)
                dv_cqn = self.sDV.n

                pend_flush()
                self._ln(st, pS2, pMu, pR, murow, rrow, varow, DKV, NCKV, ckvr, kvn, s2,
                         self.kvg_c, self.kvb_c, 128, SW)

                st.wait("tensor", sDV, dv_cqn)
                kdims_q = [128] * 5 + [CQ_TAIL]
                for h in range(H):
                    def evac_q(p, h=h):
                        qb = qT[h % 2]
                        st.wait("vector", sQT[h % 2], sQT[h % 2].n)
                        st.emit("vector", lambda e, pp=p: e.tensor_scalar(
                            qb[:], pp[0:128, 0:SW], self.buq_c[0:128, h:h + 1], None,
                            op0=ALU.add), sDV)
                        st.wait("vector", sDV)
                        st.emit("vector", lambda e: e.tensor_tensor(
                            sq[h % 2][:], qb[:], qb[:], op=ALU.mult), sDV)

                    proj(NCQ, lambda c, h=h:
                         self.WuqT[0:kdims_q[c], c * D + h * 128:c * D + h * 128 + 128],
                         cqn, lambda c: c * SW, 128, evac_q, kdims=kdims_q)
                    st.wait("sync", sDV)
                    st.emit("sync", lambda e, h=h, ss=s: e.dma_start(
                        out=AP(self.q_spill, (h * QNT + ss) * DH * SW, [[SW, 128], [1, SW]]),
                        in_=qT[h % 2][0:128, 0:SW]), sQT[h % 2], 16)
                    def stat_q(h=h, ss=s):
                        def f():
                            st.wait("tensor", sDV)
                            st.emit("tensor", lambda e: e.matmul(
                                pS[0:1, 0:512], self.ones_bf[:], sq[h % 2][:],
                                start=True, stop=True, skip_group_check=True), sPE)
                            st.wait("vector", sPE)
                            st.emit("vector", lambda e: e.tensor_reduce(
                                self.maxq2[0:1, h * QNT + ss:h * QNT + ss + 1],
                                pS[0:1, 0:SW], axis=AX.X, op=ALU.max), sDV)
                            st.wait("tensor", sDV)
                        return f
                    pend_set(stat_q())

                st.wait("tensor", sDV)
                st.emit("tensor", lambda e: e.matmul(
                    pS2[0:1, 0:512], self.ones_bf[0:64, :],
                    sqkr[0:64, 0:SW], start=True, stop=True, skip_group_check=True), sPE)
                st.wait("vector", sPE)
                st.emit("vector", lambda e: e.tensor_copy(kr2row[0:1, :], pS2[0:1, 0:512]), sDV)
                st.wait("tensor", sDV)

                st.wait("sync", sDV)
                for h in range(H):
                    st.emit("sync", lambda e, h=h, ss=s: e.dma_start(
                        out=AP(self.contrib, ss * SBLK + (h * DH + 64) * SW,
                               [[SW, 64], [1, SW]]),
                        in_=krr[0:64, 0:SW]), sKR, 16)

                st.wait("tensor", sDV)
                for co in range(NCKV):
                    def evac_k(p, co=co):
                        kb = knr[co % 2]
                        st.wait("vector", sKN[co % 2], sKN[co % 2].n)
                        st.emit("vector", lambda e, pp=p: e.tensor_scalar(
                            kb[:], pp[0:128, 0:SW], self.buk_c[0:128, co:co + 1], None,
                            op0=ALU.add), sDV)
                        st.wait("vector", sDV)
                        st.emit("vector", lambda e: e.tensor_tensor(
                            sq[co % 2][:], kb[:], kb[:], op=ALU.mult), sDV)

                    proj(NCKV, lambda c, co=co:
                         self.WukT[0:128, c * DKV + co * 128:c * DKV + co * 128 + 128],
                         kvn, lambda c: c * SW, 128, evac_k)
                    st.wait("sync", sDV)
                    for hf in range(2):
                        h = 2 * co + hf
                        st.emit("sync", lambda e, h=h, hf=hf, co=co, ss=s: e.dma_start(
                            out=AP(self.contrib, ss * SBLK + h * DH * SW,
                                   [[SW, 64], [1, SW]]),
                            in_=knr[co % 2][hf * 64:hf * 64 + 64, 0:SW]), sKN[co % 2], 16)
                    st.wait("tensor", sDV)
                    st.emit("tensor", lambda e, co=co: e.matmul(
                        pS2[0:1, 0:512], self.onesA[:], sq[co % 2][:],
                        start=True, stop=True, skip_group_check=True), None)
                    st.emit("tensor", lambda e, co=co: e.matmul(
                        pS2[0:1, 512:1024], self.onesB[:], sq[co % 2][:],
                        start=True, stop=True, skip_group_check=True), sPE)
                    st.wait("vector", sPE)
                    st.emit("vector", lambda e: e.tensor_tensor(
                        varow[0:1, 0:SW], AP(pS2, 0, [[1024, 1], [1, SW]]),
                        kr2row[0:1, 0:SW], op=ALU.add), sDV)
                    st.emit("vector", lambda e: e.tensor_tensor(
                        murow[0:1, 0:SW], AP(pS2, 512, [[1024, 1], [1, SW]]),
                        kr2row[0:1, 0:SW], op=ALU.add), sDV)
                    st.wait("vector", sDV)
                    st.emit("vector", lambda e: e.tensor_reduce(
                        tiny[0:1, 0:1], varow[0:1, 0:SW], axis=AX.X, op=ALU.max), sDV)
                    st.emit("vector", lambda e: e.tensor_reduce(
                        tiny[0:1, 1:2], murow[0:1, 0:SW], axis=AX.X, op=ALU.max), sDV)
                    st.wait("vector", sDV)
                    for hf in range(2):
                        h = 2 * co + hf
                        st.emit("vector", lambda e, h=h, hf=hf: e.tensor_copy(
                            self.maxk2[0:1, h:h + 1], tiny[0:1, hf:hf + 1]), sDV)
                    st.wait("tensor", sDV)

                for i in range(4):
                    for vs in range(4):
                        def evac_v(p, i=i, vs=vs):
                            st.emit("vector", lambda e, pp=p: e.tensor_copy(
                                vsb[0:128, i * D + vs * 512:i * D + vs * 512 + 512],
                                pp[0:128, 0:512]), sDV)

                        def extra_bias(p, vs=vs):
                            st.emit("tensor", lambda e, pp=p: e.matmul(
                                pp[0:128, 0:512], self.onesrow_bf[0:1, :],
                                self.bukvv_r[0:1, vs * 512:(vs + 1) * 512],
                                start=False, stop=True, skip_group_check=True), sPE)

                        proj(NCKV, lambda c, i=i:
                             kvn[0:128, c * SW + i * 128:c * SW + i * 128 + 128],
                             self.WuvT, lambda c, vs=vs: c * D + vs * 512, 128,
                             evac_v, extra=extra_bias)
                st.wait("sync", sDV)
                for h in range(H):
                    st.emit("sync", lambda e, h=h, ss=s: e.dma_start(
                        out=AP(self.contrib, ss * SBLK + VOFFs + h * SW * DH,
                               [[DH, 128], [128 * DH, 4], [1, DH]]),
                        in_=AP(vsb, h * 128, [[4 * D, 128], [D, 4], [1, DH]])), sST, 16)

                if s == 0:
                    st.emit("gpsimd", lambda e: e.memset(nrmbf[:], 0.0), self.sGE)
                st.wait("vector", sDV)
                st.wait("vector", self.sGE)
                st.wait("vector", sST)
                st.emit("vector", lambda e: e.tensor_scalar(
                    nrmbf[0:1, 0:16], self.maxk2[0:1, :], 1.02, None, op0=ALU.mult), sDV)
                st.wait("sync", sDV)
                st.emit("sync", lambda e, ss=s: e.dma_start(
                    out=AP(self.contrib, ss * SBLK + NOFFs, [[1024, 1], [1, 1024]]),
                    in_=nrmbf[0:1, :]), sST, 16)
                for sx in (sST, sKR, sKN[0], sKN[1]):
                    st.wait("gpsimd", sx, sx.n)
                st.emit("gpsimd", lambda e, ss=s: e.collective_compute(
                    "AllGather", ALU.bypass,
                    replica_groups=[[0, 1, 2, 3], [4, 5, 6, 7]],
                    ins=[AP(self.contrib, ss * SBLK, [[SBLK, 1], [1, SBLK]])],
                    outs=[AP(self.gathered, ss * 4 * SBLK,
                             [[4 * SBLK, 1], [1, 4 * SBLK]])]), self.sCC)

            st.wait("sync", sST)
            st.wait("gpsimd", sGP)
            for sx in (sQT[0], sQT[1],):
                st.wait("gpsimd", sx, sx.n)
            st.flush(nc)

    def _ln(self, st, pS, pMu, pR, murow, rrow, varow, DIM, NCH, raw, dst, s2, g_c, b_c,
            tail, width):
        sPE, sDV, sAC = self.sPE, self.sDV, self.sAC
        inv = 1.0 / DIM
        st.wait("vector", sPE)
        st.emit("vector", lambda e: e.tensor_scalar(
            murow[0:1, 0:width], pS[0:1, 0:512], inv, None, op0=ALU.mult), sDV)
        st.emit("vector", lambda e: e.tensor_scalar(
            varow[0:1, 0:width], pS[0:1, 512:1024], inv, None, op0=ALU.mult), sDV)
        st.wait("vector", sDV)
        st.emit("vector", lambda e: e.tensor_tensor(
            rrow[0:1, 0:width], murow[0:1, 0:width], murow[0:1, 0:width], op=ALU.mult), sDV)
        st.wait("vector", sDV)
        st.emit("vector", lambda e: e.tensor_tensor(
            varow[0:1, 0:width], varow[0:1, 0:width], rrow[0:1, 0:width], op=ALU.subtract), sDV)
        st.wait("scalar", sDV)
        st.emit("scalar", lambda e: e.activation(out=rrow[0:1, 0:width], in_=varow[0:1, 0:width],
                                                 func=AF.Ln, bias=self.epsb[0:1, :],
                                                 scale=1.0), sAC)
        st.wait("scalar", sAC)
        st.emit("scalar", lambda e: e.activation(out=rrow[0:1, 0:width], in_=rrow[0:1, 0:width],
                                                 func=AF.Exp, bias=0.0, scale=-0.5), sAC)
        st.wait("tensor", sAC)
        st.wait("tensor", sDV)
        st.emit("tensor", lambda e: e.matmul(pMu[0:128, 0:width], self.onesrow_f[0:1, :],
                                             murow[0:1, 0:width], start=True, stop=True,
                                             skip_group_check=True), None)
        st.emit("tensor", lambda e: e.matmul(pR[0:128, 0:width], self.onesrow_f[0:1, :],
                                             rrow[0:1, 0:width], start=True, stop=True,
                                             skip_group_check=True), sPE)
        st.wait("vector", sPE)
        for c in range(NCH):
            w = 128 if c < NCH - 1 else tail
            st.emit("vector", lambda e, c=c, w=w: e.tensor_tensor(
                s2[c % 2][0:w, 0:width], raw[0:w, c * width:(c + 1) * width],
                pMu[0:w, 0:width], op=ALU.subtract), sDV)
            st.wait("vector", sDV)
            st.emit("vector", lambda e, c=c, w=w: e.tensor_tensor(
                s2[c % 2][0:w, 0:width], s2[c % 2][0:w, 0:width], pR[0:w, 0:width],
                op=ALU.mult), sDV)
            st.wait("vector", sDV)
            st.emit("vector", lambda e, c=c, w=w: e.tensor_scalar(
                dst[0:w, c * width:(c + 1) * width], s2[c % 2][0:w, 0:width],
                g_c[0:w, c:c + 1], b_c[0:w, c:c + 1], op0=ALU.mult, op1=ALU.add), sDV)
            st.wait("vector", sDV)

    def ag(self):
        nc = self.nc
        st = Streams()
        sGE, sDV, sLD = self.sGE, self.sDV, self.sLD
        with ExitStack() as ages:
            scr = ages.enter_context(nc.sbuf_tensor("agscr", [128, 16 * SW], F32, side="right"))
            mthr_t = ages.enter_context(nc.sbuf_tensor("agmthr", [128, 1], F32, side="right"))
            vr = ages.enter_context(nc.sbuf_tensor("agvr", [1, D], F32, side="right"))
            st.emit("gpsimd", lambda e: e.iota(scr[:], [[-128, 16], [1, SW]],
                                               channel_multiplier=-1,
                                               allow_small_or_imprecise_dtypes=True), sGE)
            st.emit("sync", lambda e: e.dma_start(out=mthr_t[:], in_=self.mthr.ap()), sLD, 16)
            st.wait("vector", sGE)
            st.wait("vector", sLD)
            st.emit("vector", lambda e: e.tensor_scalar(self.mask[:], scr[:], mthr_t[:, 0:1],
                                                        None, op0=ALU.is_ge), sDV)
            for dram, dst, n in ((self.b_o, self.bo_r, D), (self.b_ffn, self.bffn_r, D),
                                 (self.gate_b, self.gateb_r, NE)):
                st.wait("sync", sDV)
                st.emit("sync", lambda e, d=dram, nn=n: e.dma_start(
                    out=vr[0:1, 0:nn], in_=AP(d, 0, [[nn, 1], [1, nn]])), sLD, 16)
                st.wait("vector", sLD)
                st.emit("vector", lambda e, dd=dst, nn=n: e.tensor_copy(
                    dd[0:1, 0:nn], vr[0:1, 0:nn]), sDV)
            st.wait("gpsimd", sDV)
            st.flush(nc)

    def p2(self):
        nc = self.nc
        sPE, sDV, sAC, sCC = self.sPE, self.sDV, self.sAC, self.sCC
        sWL = self.sWL
        sKV = [self.sKVa, self.sKVb]
        sQB = [self.sQBa, self.sQBb]
        sOT = [self.sOTa, self.sOTb]
        st = Streams()
        with ExitStack() as p2:
            def rsb(name, shape, dt=BF16):
                return p2.enter_context(nc.sbuf_tensor(name, list(shape), dt, side="right"))

            qbuf = [rsb("qbuf%d" % i, [128, SW]) for i in range(2)]
            Kbuf = [rsb("Kbuf%d" % i, [128, 1024]) for i in range(2)]
            Vbuf = [rsb("Vbuf%d" % i, [128, 1024]) for i in range(2)]
            pt = [rsb("pt%d" % i, [128, 1024]) for i in range(2)]
            ot_sb = [rsb("ot%d" % i, [128, SW]) for i in range(2)]
            recip = p2.enter_context(nc.sbuf_tensor("recip", [1, SW], F32, side="right"))
            recip_bc = p2.enter_context(nc.sbuf_tensor("recip_bc", [128, SW], F32, side="right"))
            nrm4 = p2.enter_context(nc.sbuf_tensor("nrm4", [1, 64], BF16, side="right"))
            nrm4f = p2.enter_context(nc.sbuf_tensor("nrm4f", [1, 64], F32, side="right"))
            mkrun = p2.enter_context(nc.sbuf_tensor("mkrun", [1, H], F32, side="right"))
            prodr = p2.enter_context(nc.sbuf_tensor("prodr", [1, H], F32, side="right"))
            psS = [p2.enter_context(nc.psum_tensor("p2s%d" % i, [128, 1024], F32))
                   for i in range(2)]
            psO = [p2.enter_context(nc.psum_tensor("p2o%d" % i, [128, SW], F32))
                   for i in range(2)]
            psR = p2.enter_context(nc.psum_tensor("p2r", [1, SW], F32))
            psB = p2.enter_context(nc.psum_tensor("p2b", [128, SW], F32))

            for sp, tgt, cols in ((self.wo_sp, self.WoT, NCD * D),
                                  (self.wffn_sp, self.WffnT, NCD * D),
                                  (self.acat_sp, self.AcatT, NCD * ER),
                                  (self.gate_sp, self.gateT, NCD * NE),
                                  (self.bcat_sp, self.BcatT, D)):
                st.emit("gpsimd", lambda e, s=sp, t=tgt, cc=cols: e.dma_start(
                    out=t[0:128, 0:cc], in_=AP(s, 0, [[cc, 128], [1, cc]])), sWL, 16)

            self._p2_send = [None]
            pe_qfree = [0, 0]
            pe_kvfree = [0, 0]
            ac_psfree = [0, 0]
            pe_ptfree = [0, 0]
            dv_osfree = [0, 0]
            ac_rfree = 0
            self._p2_dvb = [0]

            for kcls in range(1, QNT + 1):
                st.wait("gpsimd", sCC, kcls)
                st.emit("gpsimd", lambda e, k=kcls: e.dma_start(
                    out=nrm4[0:1, 0:64],
                    in_=AP(self.gathered, (k - 1) * 4 * SBLK + NOFFs,
                           [[64, 1], [SBLK, 4], [1, 16]])), self.sGP, 16)
                st.wait("vector", self.sGP)
                st.emit("vector", lambda e: e.tensor_copy(nrm4f[:], nrm4[:]), sDV)
                st.wait("vector", sDV)
                st.emit("vector", lambda e: e.tensor_reduce(
                    prodr[0:1, 0:H], AP(nrm4f, 0, [[64, 1], [1, 16], [16, 4]]),
                    axis=AX.X, op=ALU.max), sDV)
                st.wait("vector", sDV)
                if kcls == 1:
                    st.emit("vector", lambda e: e.tensor_copy(mkrun[:], prodr[0:1, 0:H]), sDV)
                else:
                    st.emit("vector", lambda e: e.tensor_tensor(
                        mkrun[:], mkrun[:], prodr[0:1, 0:H], op=ALU.max), sDV)
                st.wait("vector", sDV)
                st.emit("vector", lambda e, k=kcls: e.tensor_tensor(
                    prodr[0:1, 0:H],
                    AP(self.maxq2, k - 1, [[H * QNT, 1], [QNT, H]]),
                    mkrun[0:1, 0:H], op=ALU.mult), sDV)
                st.wait("scalar", sDV)
                st.emit("scalar", lambda e: e.activation(out=prodr[:], in_=prodr[:],
                                                         func=AF.Ln, bias=self.epsb[0:1, :],
                                                         scale=1.0), sAC)
                st.wait("scalar", sAC)
                st.emit("scalar", lambda e: e.activation(out=prodr[:], in_=prodr[:],
                                                         func=AF.Exp, bias=0.0,
                                                         scale=0.5), sAC)
                st.wait("vector", sAC)
                st.emit("vector", lambda e: e.tensor_scalar(
                    prodr[:], prodr[:], ATT_SCALE, 80.0, op0=ALU.mult, op1=ALU.min), sDV)
                st.wait("vector", sDV)
                st.emit("vector", lambda e: e.tensor_scalar(
                    prodr[:], prodr[:], -1.0, None, op0=ALU.mult), sDV)
                if self._p2_send[0] is not None:
                    self._p2_send[0]()
                    self._p2_send[0] = None
                st.wait("tensor", sDV)
                st.wait("tensor", sDV, self._p2_dvb[0])
                st.emit("tensor", lambda e: e.matmul(
                    psB[0:128, 0:H], self.onesrow_f[0:1, :], prodr[0:1, 0:H],
                    start=True, stop=True, skip_group_check=True), sPE)
                st.wait("vector", sPE)
                st.emit("vector", lambda e, k=kcls: e.tensor_copy(
                    self.negm_bc[0:128, (k - 1) * H:k * H], psB[0:128, 0:H]), sDV)
                self._p2_dvb[0] = self.sDV.n
                st.wait("tensor", sDV)

                for h in range(H):
                    sidx = (kcls - 1) * H + h
                    psw = sidx % 2
                    nblk = 2 * kcls
                    npair = 4 * nblk

                    st.wait("sync", sPE, pe_qfree[psw])
                    st.emit("sync", lambda e, h=h, k=kcls, b=qbuf[psw]: e.dma_start(
                        out=b[:], in_=AP(self.q_spill, (h * QNT + k - 1) * DH * SW,
                                         [[SW, 128], [1, SW]])), sQB[psw], 16)
                    qld = sQB[psw].n

                    kvld = [0] * nblk

                    def emit_load(blk, h=h):
                        pb = blk % 2
                        strip, rank0 = blk // 2, (2 * blk) % 4
                        base = strip * 4 * SBLK + rank0 * SBLK
                        st.wait("sync", sPE, pe_kvfree[pb])
                        st.emit("sync", lambda e, ba=base, b=Kbuf[pb]: e.dma_start(
                            out=AP(b, 0, [[1024, 128], [512, 2], [1, 512]]),
                            in_=AP(self.gathered, ba + KOFFs + h * DH * SW,
                                   [[SW, 128], [SBLK, 2], [1, 512]])), sKV[pb], 16)
                        for pc in range(2):
                            st.emit("sync", lambda e, ba=base, b=Vbuf[pb], pc=pc:
                                    e.dma_start(
                                        out=AP(b, pc * 512,
                                               [[1024, 128], [128, 4], [1, 128]]),
                                        in_=AP(self.gathered,
                                               ba + pc * SBLK + VOFFs + h * SW * DH,
                                               [[DH, 128], [128 * DH, 4], [1, DH]])),
                                    sKV[pb], 16)
                        kvld[blk] = sKV[pb].n

                    emit_load(0)
                    if nblk > 1:
                        emit_load(1)

                    pend = [None]

                    def flush_rpv():
                        if pend[0] is not None:
                            pend[0]()
                            pend[0] = None

                    def flush_sweepend():
                        if self._p2_send[0] is not None:
                            self._p2_send[0]()
                            self._p2_send[0] = None

                    for g in range(npair):
                        blk, gb = g // 4, g % 4
                        pb = blk % 2
                        gp = g % 2
                        masked = (blk >= nblk - 2)
                        first = (g == 0)
                        last = (g == npair - 1)

                        if gb == 0:
                            st.wait("tensor", sKV[pb], kvld[blk])
                        if first:
                            st.wait("tensor", sQB[psw], qld)
                        st.wait("tensor", sAC, ac_psfree[gp])
                        for j in range(2):
                            cc = 2 * gb + j
                            st.emit("tensor", lambda e, pS=psS[gp], Kb=Kbuf[pb],
                                    qb=qbuf[psw], cc=cc, j=j: e.matmul(
                                        pS[0:128, j * SW:(j + 1) * SW],
                                        Kb[0:128, cc * 128:cc * 128 + 128], qb[:],
                                        start=True, stop=True, skip_group_check=True),
                                    sPE if j == 1 else None)
                        spe_s = sPE.n

                        st.wait("scalar", sPE, spe_s)
                        st.wait("scalar", sPE, pe_ptfree[gp])
                        st.emit("scalar", lambda e, pS=psS[gp], p=pt[gp], sidx=sidx:
                                e.activation(out=p[:], in_=pS[0:128, 0:1024], func=AF.Exp,
                                             bias=self.negm_bc[0:128, sidx:sidx + 1],
                                             scale=ATT_SCALE), sAC)
                        ac_psfree[gp] = sAC.n
                        ac_exp = sAC.n

                        dv_mask = None
                        if masked:
                            jp = g - (npair - 8)
                            st.wait("vector", sAC, ac_exp)
                            st.emit("vector", lambda e, p=pt[gp], jp=jp: e.tensor_tensor(
                                p[:], p[:], self.mask[0:128, jp * 1024:(jp + 1) * 1024],
                                op=ALU.mult), sDV)
                            dv_mask = sDV.n

                        flush_rpv()
                        if g == 0:
                            flush_sweepend()

                        def mk_rpv(g=g, gp=gp, pb=pb, ac_exp=ac_exp, dv_mask=dv_mask,
                                   first=first, last=last, psw=psw):
                            def emit():
                                if dv_mask is not None:
                                    st.wait("tensor", sDV, dv_mask)
                                else:
                                    st.wait("tensor", sAC, ac_exp)
                                if first:
                                    st.wait("tensor", sDV, dv_osfree[psw])
                                    st.wait("tensor", sAC, ac_rfree)
                                for j in range(2):
                                    cc = 2 * (g % 4) + j
                                    st.emit("tensor", lambda e, p=pt[gp], j=j: e.matmul(
                                        psR[0:1, :], self.ones_bf[:],
                                        p[0:128, j * SW:(j + 1) * SW],
                                        start=(first and j == 0), stop=(last and j == 1),
                                        skip_group_check=True), None)
                                    st.emit("tensor", lambda e, p=pt[gp], Vb=Vbuf[pb],
                                            cc=cc, j=j: e.matmul(
                                                psO[psw][:],
                                                Vb[0:128, cc * 128:cc * 128 + 128],
                                                p[0:128, j * SW:(j + 1) * SW],
                                                start=(first and j == 0),
                                                stop=(last and j == 1),
                                                skip_group_check=True),
                                            sPE if j == 1 else None)
                                pe_ptfree[gp] = sPE.n
                                if g % 4 == 3 or last:
                                    pe_kvfree[pb] = sPE.n
                                if last:
                                    pe_qfree[psw] = sPE.n
                            return emit

                        pend[0] = mk_rpv()
                        if g >= 1 and (g - 1) % 4 == 3:
                            nxt = (g - 1) // 4 + 2
                            if nxt < nblk:
                                emit_load(nxt)
                    flush_rpv()
                    spe_sweep = sPE.n

                    st.wait("scalar", sPE, spe_sweep)
                    st.emit("scalar", lambda e: e.activation(
                        out=recip[0:1, :], in_=psR[0:1, :], func=AF.Ln,
                        bias=0.0, scale=1.0), sAC)
                    ac_rfree = sAC.n
                    st.wait("scalar", sAC)
                    st.emit("scalar", lambda e: e.activation(
                        out=recip[0:1, :], in_=recip[0:1, :], func=AF.Exp,
                        bias=0.0, scale=-1.0), sAC)
                    ac_rec = sAC.n

                    def mk_sweepend(h=h, kcls=kcls, psw=psw, ac_rec=ac_rec):
                        def emit():
                            nonlocal_dv = None
                            st.wait("tensor", sAC, ac_rec)
                            st.wait("tensor", sDV, self._p2_dvb[0])
                            st.emit("tensor", lambda e: e.matmul(
                                psB[:], self.onesrow_f[0:1, :], recip[0:1, :],
                                start=True, stop=True, skip_group_check=True), sPE)
                            spe_bc = sPE.n
                            st.wait("vector", sPE, spe_bc)
                            st.emit("vector", lambda e: e.tensor_copy(
                                recip_bc[:], psB[:]), sDV)
                            self._p2_dvb[0] = self.sDV.n
                            st.wait("vector", sDV, self._p2_dvb[0])
                            st.wait("vector", sOT[psw], sOT[psw].n)
                            st.emit("vector", lambda e, pO=psO[psw], o=ot_sb[psw]:
                                    e.tensor_tensor(o[:], pO[:], recip_bc[:],
                                                    op=ALU.mult), sDV)
                            dv_osfree[psw] = self.sDV.n
                            st.wait("gpsimd", sDV, dv_osfree[psw])
                            st.emit("gpsimd", lambda e, h=h, k=kcls, o=ot_sb[psw]:
                                    e.dma_start(
                                        out=AP(self.ot_spill, (h * QNT + k - 1) * DH * SW,
                                               [[SW, 128], [1, SW]]),
                                        in_=o[:]), sOT[psw], 16)
                        return emit

                    self._p2_send[0] = mk_sweepend()

            if self._p2_send[0] is not None:
                self._p2_send[0]()
                self._p2_send[0] = None
            st.wait("gpsimd", sOT[0], sOT[0].n)
            st.wait("gpsimd", sOT[1], sOT[1].n)
            st.wait("gpsimd", sWL, sWL.n)
            st.flush(nc)

    def p3(self):
        nc = self.nc
        sPE, sDV, sAC, sST = self.sPE, self.sDV, self.sAC, self.sST
        sOL = [self.sOLa, self.sOLb]
        sXT = self.sXT
        st = Streams()
        with ExitStack() as p3:
            def rsb(name, shape, dt=F32):
                return p3.enter_context(nc.sbuf_tensor(name, list(shape), dt, side="right"))

            otb = [rsb("otb%d" % i, [128, NCD * 128], BF16) for i in range(2)]
            xt = rsb("xt", [128, D])
            ysb = rsb("ysb", [128, D])
            x1f = rsb("x1f", [128, D])
            x1b = rsb("x1b", [128, D], BF16)
            x1T = rsb("x1T", [128, NCD * 128], BF16)
            xaw = rsb("xaw", [128, ER], BF16)
            xawT = rsb("xawT", [128, ER], BF16)
            stat = rsb("stat", [128, 24])
            mv = rsb("mv", [128, 2])
            rln = rsb("rln", [128, 1])
            eq1 = rsb("eq1", [128, 8])
            l2t = rsb("l2t", [128, 8])
            w8t = rsb("w8t", [128, 8])
            t1 = rsb("t1", [128, 1])
            t2 = rsb("t2", [128, 1])
            te = rsb("te", [128, 1])
            tg1 = rsb("tg1", [128, 1])
            tg2 = rsb("tg2", [128, 1])
            pBig = p3.enter_context(nc.psum_tensor("p3big", [128, D], F32))
            psG = p3.enter_context(nc.psum_tensor("p3g", [128, ER], F32))
            psL = p3.enter_context(nc.psum_tensor("p3l", [128, NE], F32))
            psT = [p3.enter_context(nc.psum_tensor("p3t%d" % i, [128, 1024], BF16))
                   for i in range(2)]

            pe_otfree = [0, 0]
            dv_pbig = 0
            dv_xt = 0

            st.wait("tensor", self.sWL, self.sWL.n)
            for i in range(NT * 2):
                par = i % 2
                lt, hf = i // 2, i % 2

                st.wait("sync", sPE, pe_otfree[par])
                for h in range(H):
                    st.emit("sync", lambda e, h=h, i=i, b=otb[par]: e.dma_start(
                        out=b[0:128, h * 128:h * 128 + 128],
                        in_=AP(self.ot_spill, (h * QNT + i // 4) * DH * SW + (i % 4) * 128,
                               [[SW, 128], [1, 128]])), sOL[par], 16)
                st.wait("sync", sDV, dv_xt)
                st.emit("sync", lambda e, i=i: e.dma_start(
                    out=xt[:], in_=AP(self.x_own, i * 128 * D, [[D, 128], [1, D]])),
                    sXT, 16)
                xtld = self.sXT.n

                st.wait("tensor", sOL[par], sOL[par].n)
                st.wait("tensor", sDV, dv_pbig)
                for cd in range(NCD):
                    for vs in range(4):
                        st.emit("tensor", lambda e, cd=cd, vs=vs, b=otb[par]: e.matmul(
                            pBig[0:128, vs * 512:(vs + 1) * 512],
                            b[0:128, cd * 128:cd * 128 + 128],
                            self.WoT[0:128, cd * D + vs * 512:cd * D + vs * 512 + 512],
                            start=(cd == 0), stop=False, skip_group_check=True), None)
                for vs in range(4):
                    st.emit("tensor", lambda e, vs=vs: e.matmul(
                        pBig[0:128, vs * 512:(vs + 1) * 512], self.onesrow_bf[0:1, :],
                        self.bo_r[0:1, vs * 512:(vs + 1) * 512],
                        start=False, stop=True, skip_group_check=True),
                        sPE if vs == 3 else None)
                spe_wo = self.sPE.n
                pe_otfree[par] = spe_wo

                st.wait("vector", sPE, spe_wo)
                st.wait("vector", sXT, xtld)
                st.wait("vector", sST, self.sST.n)
                st.emit("vector", lambda e: e.tensor_tensor(ysb[:], pBig[:], xt[:],
                                                            op=ALU.add), sDV)
                dv_xt = self.sDV.n
                st.wait("vector", sDV)
                for cst in range(4):
                    st.emit("vector", lambda e, cst=cst: e.bn_stats(
                        stat[0:128, cst * 6:cst * 6 + 6],
                        ysb[0:128, cst * 512:(cst + 1) * 512]), sDV)
                st.wait("vector", sDV)
                st.emit("vector", lambda e: e.bn_aggr(mv[:], stat[:]), sDV)
                st.wait("scalar", sDV, self.sDV.n)
                st.emit("scalar", lambda e: e.activation(out=rln[:], in_=mv[0:128, 1:2],
                                                         func=AF.Ln, bias=self.epsb[:],
                                                         scale=1.0), sAC)
                st.wait("scalar", sAC)
                st.emit("scalar", lambda e: e.activation(out=rln[:], in_=rln[:],
                                                         func=AF.Exp, bias=0.0,
                                                         scale=-0.5), sAC)
                st.wait("vector", sAC)
                st.emit("vector", lambda e: e.tensor_scalar(
                    x1f[:], ysb[:], mv[0:128, 0:1], rln[:], op0=ALU.subtract,
                    op1=ALU.mult), sDV)
                st.wait("vector", sDV)
                st.emit("vector", lambda e: e.tensor_copy(x1b[:], x1f[:]), sDV)
                dv_x1 = self.sDV.n
                if self.taps:
                    st.wait("gpsimd", sDV, self.sDV.n)
                    st.emit("gpsimd", lambda e, i=i: e.dma_start(
                        out=AP(self.dbg2, i * 128 * D, [[D, 128], [1, D]]),
                        in_=x1b[:]), self.sGP, 16)
                    st.wait("vector", self.sGP, self.sGP.n)

                st.wait("tensor", sDV, dv_x1)
                for bank in range(2):
                    for j in range(8):
                        cd = bank * 8 + j
                        st.emit("tensor", lambda e, cd=cd, j=j, p=psT[bank]: e.transpose(
                            p[0:128, j * 128:j * 128 + 128],
                            x1b[0:128, cd * 128:cd * 128 + 128], self.identb[:]),
                            sPE if j == 7 else None)
                    st.wait("vector", sPE)
                    st.emit("vector", lambda e, bank=bank, p=psT[bank]: e.tensor_copy(
                        x1T[0:128, bank * 1024:(bank + 1) * 1024], p[:]), sDV)
                dv_x1T = self.sDV.n

                st.wait("tensor", sDV, dv_x1T)
                for cd in range(NCD):
                    for vs in range(4):
                        st.emit("tensor", lambda e, cd=cd, vs=vs: e.matmul(
                            pBig[0:128, vs * 512:(vs + 1) * 512],
                            x1T[0:128, cd * 128:cd * 128 + 128],
                            self.WffnT[0:128, cd * D + vs * 512:cd * D + vs * 512 + 512],
                            start=(cd == 0), stop=False, skip_group_check=True), None)
                    st.emit("tensor", lambda e, cd=cd: e.matmul(
                        psL[0:128, 0:NE], x1T[0:128, cd * 128:cd * 128 + 128],
                        self.gateT[0:128, cd * NE:(cd + 1) * NE],
                        start=(cd == 0), stop=False, skip_group_check=True), None)
                    st.emit("tensor", lambda e, cd=cd: e.matmul(
                        psG[0:128, 0:ER], x1T[0:128, cd * 128:cd * 128 + 128],
                        self.AcatT[0:128, cd * ER:(cd + 1) * ER],
                        start=(cd == 0), stop=(cd == NCD - 1), skip_group_check=True), None)
                for vs in range(4):
                    st.emit("tensor", lambda e, vs=vs: e.matmul(
                        pBig[0:128, vs * 512:(vs + 1) * 512], self.onesrow_bf[0:1, :],
                        self.bffn_r[0:1, vs * 512:(vs + 1) * 512],
                        start=False, stop=False, skip_group_check=True), None)
                st.emit("tensor", lambda e: e.matmul(
                    psL[0:128, 0:NE], self.onesrow_bf[0:1, :], self.gateb_r[0:1, :],
                    start=False, stop=True, skip_group_check=True), sPE)
                spe_gx = self.sPE.n

                st.wait("vector", sPE, spe_gx)
                st.emit("vector", lambda e: e.tensor_reduce(
                    t1[:], psL[0:128, 0:NE], axis=AX.X, op=ALU.max), sDV)
                st.wait("vector", sDV)
                st.emit("vector", lambda e: e.tensor_scalar(
                    eq1[:], psL[0:128, 0:NE], t1[:], None, op0=ALU.is_equal), sDV)
                st.wait("vector", sDV)
                st.emit("vector", lambda e: e.scalar_tensor_tensor(
                    l2t[:], eq1[:], -1e30, psL[0:128, 0:NE], op0=ALU.mult,
                    op1=ALU.add), sDV)
                st.wait("vector", sDV)
                st.emit("vector", lambda e: e.tensor_reduce(
                    t2[:], l2t[:], axis=AX.X, op=ALU.max), sDV)
                st.wait("vector", sDV)
                st.emit("vector", lambda e: e.tensor_tensor(
                    te[:], t2[:], t1[:], op=ALU.subtract), sDV)
                st.wait("scalar", sDV, self.sDV.n)
                st.emit("scalar", lambda e: e.activation(out=te[:], in_=te[:], func=AF.Exp,
                                                         bias=0.0, scale=1.0), sAC)
                st.wait("vector", sAC)
                st.emit("vector", lambda e: e.tensor_scalar(
                    tg1[:], te[:], 1.0, None, op0=ALU.add), sDV)
                st.wait("vector", sDV)
                st.emit("vector", lambda e: e.reciprocal(tg1[:], tg1[:]), sDV)
                st.wait("vector", sDV)
                st.emit("vector", lambda e: e.tensor_scalar(
                    tg2[:], tg1[:], -1.0, 1.0, op0=ALU.mult, op1=ALU.add), sDV)
                st.emit("vector", lambda e: e.tensor_scalar(
                    eq1[:], eq1[:], tg1[:], None, op0=ALU.mult), sDV)
                st.emit("vector", lambda e: e.tensor_scalar(
                    l2t[:], l2t[:], t2[:], None, op0=ALU.is_equal), sDV)
                st.wait("vector", sDV)
                st.emit("vector", lambda e: e.scalar_tensor_tensor(
                    w8t[:], l2t[:], tg2[:], eq1[:], op0=ALU.mult, op1=ALU.add), sDV)
                st.wait("vector", sDV)
                st.emit("vector", lambda e: e.tensor_tensor(
                    xaw[:].rearrange("p (e r) -> p e r", r=RLORA),
                    AP(psG, 0, [[ER, 128], [16, NE], [1, RLORA]]),
                    w8t[:].unsqueeze(2).broadcast_to([128, NE, RLORA]),
                    op=ALU.mult), sDV)
                dv_xaw = self.sDV.n

                st.wait("tensor", sDV, dv_xaw)
                st.emit("tensor", lambda e: e.transpose(
                    psT[0][0:128, 0:128], xaw[:], self.identb[:]), sPE)
                st.wait("vector", sPE)
                st.emit("vector", lambda e: e.tensor_copy(
                    xawT[:], psT[0][0:128, 0:128]), sDV)
                st.wait("tensor", sDV)
                for vs in range(4):
                    st.emit("tensor", lambda e, vs=vs: e.matmul(
                        pBig[0:128, vs * 512:(vs + 1) * 512], xawT[:],
                        self.BcatT[0:128, vs * 512:(vs + 1) * 512],
                        start=False, stop=True, skip_group_check=True),
                        sPE if vs == 3 else None)
                spe_ffn = self.sPE.n

                st.wait("vector", sPE, spe_ffn)
                st.emit("vector", lambda e: e.tensor_tensor(ysb[:], pBig[:], x1f[:],
                                                            op=ALU.add), sDV)
                dv_pbig = self.sDV.n
                if self.taps:
                    st.wait("gpsimd", sDV, self.sDV.n)
                    st.emit("gpsimd", lambda e, i=i: e.dma_start(
                        out=AP(self.dbg1, i * 128 * D, [[D, 128], [1, D]]),
                        in_=ysb[:]), self.sGP, 16)
                    st.emit("gpsimd", lambda e, i=i: e.dma_start(
                        out=AP(self.dbg3, i * 128 * ER, [[ER, 128], [1, ER]]),
                        in_=xaw[:]), self.sGP, 16)
                    st.wait("vector", self.sGP, self.sGP.n)
                st.wait("vector", sDV)
                for cst in range(4):
                    st.emit("vector", lambda e, cst=cst: e.bn_stats(
                        stat[0:128, cst * 6:cst * 6 + 6],
                        ysb[0:128, cst * 512:(cst + 1) * 512]), sDV)
                st.wait("vector", sDV)
                st.emit("vector", lambda e: e.bn_aggr(mv[:], stat[:]), sDV)
                st.wait("scalar", sDV, self.sDV.n)
                st.emit("scalar", lambda e: e.activation(out=rln[:], in_=mv[0:128, 1:2],
                                                         func=AF.Ln, bias=self.epsb[:],
                                                         scale=1.0), sAC)
                st.wait("scalar", sAC)
                st.emit("scalar", lambda e: e.activation(out=rln[:], in_=rln[:],
                                                         func=AF.Exp, bias=0.0,
                                                         scale=-0.5), sAC)
                st.wait("vector", sAC)
                st.emit("vector", lambda e: e.tensor_scalar(
                    ysb[:], ysb[:], mv[0:128, 0:1], rln[:], op0=ALU.subtract,
                    op1=ALU.mult), sDV)
                st.wait("sync", sDV)
                st.emit("sync", lambda e, i=i: e.dma_start(
                    out=AP(self.out, i * 128 * D, [[D, 128], [1, D]]), in_=ysb[:]),
                    sST, 16)

            st.wait("sync", sST)
            st.wait("gpsimd", sST, self.sST.n)
            st.flush(nc)


def _rotary_tables():
    j = np.arange(32, dtype=np.float64)
    f = (1.0 / 10000.0) ** ((2.0 * j) / 64.0)
    h = np.arange(H, dtype=np.float64)
    ang = h[:, None] * f[None, :]
    c, s = np.cos(ang), np.sin(ang)
    C = np.ones(D, dtype=np.float64)
    S = np.zeros(D, dtype=np.float64)
    for hh in range(H):
        C[128 * hh + 64:128 * hh + 96] = c[hh]
        C[128 * hh + 96:128 * hh + 128] = c[hh]
        S[128 * hh + 64:128 * hh + 96] = s[hh]
        S[128 * hh + 96:128 * hh + 128] = -s[hh]
    return (np.ascontiguousarray(np.broadcast_to(C.astype(np.float32), (128, D))),
            np.ascontiguousarray(np.broadcast_to(S.astype(np.float32), (128, D))))


def make_in_maps(inputs):
    C, S = _rotary_tables()
    x = np.asarray(inputs["L0_attn_x"], dtype=np.float32)
    f32c = lambda k: np.ascontiguousarray(np.asarray(inputs[k], dtype=np.float32))
    shared = {
        "w_dq": f32c("L0_attn_W_d_q"), "b_dq": f32c("L0_attn_b_d_q"),
        "qng": f32c("L0_attn_q_norm_g"), "qnb": f32c("L0_attn_q_norm_b"),
        "w_uq": f32c("L0_attn_W_u_q"), "b_uq": f32c("L0_attn_b_u_q"),
        "w_dkv": f32c("L0_attn_W_d_kv"), "b_dkv": f32c("L0_attn_b_d_kv"),
        "kvg": f32c("L0_attn_kv_norm_g"), "kvb": f32c("L0_attn_kv_norm_b"),
        "w_ukv": f32c("L0_attn_W_u_kv"), "b_ukv": f32c("L0_attn_b_u_kv"),
        "w_o": f32c("L0_attn_W_o"), "b_o": f32c("L0_attn_b_o"),
        "w_ffn": f32c("L0_ffn_W"), "b_ffn": f32c("L0_ffn_b"),
        "lora_a": f32c("L0_ffn_lora_A"), "lora_b": f32c("L0_ffn_lora_B"),
        "gate_w": f32c("L0_ffn_gate_w"), "gate_b": f32c("L0_ffn_gate_b"),
        "cosrep": C, "sinrep": S,
    }
    in_maps = []
    for c in range(8):
        b, rho = c // 4, c % 4
        tiles = [x[b, 512 * (4 * j + rho):512 * (4 * j + rho) + 512] for j in range(QNT)]
        m = dict(shared)
        m["x_own"] = np.ascontiguousarray(np.concatenate(tiles, axis=0))
        m["mthr"] = np.full((128, 1), -512.0 * rho, dtype=np.float32)
        in_maps.append(m)
    return in_maps


def assemble_out(results):
    y = np.empty((B, T, D), dtype=np.float32)
    for c in range(8):
        b, rho = c // 4, c % 4
        o = np.asarray(results[c]["out"])
        for j in range(QNT):
            y[b, 512 * (4 * j + rho):512 * (4 * j + rho) + 512] = o[512 * j:512 * j + 512]
    return y


def kernel(**inputs):
    mk = MK()
    nc = mk.build()
    in_maps = make_in_maps(inputs)
    res = run_bass_kernel_spmd(nc, in_maps, core_ids=list(range(8)))
    return assemble_out(res.results)


# version 11
# speedup vs baseline: 1.2752x; 1.0938x over previous
import sys
if "/opt/trn_rl_repo" not in sys.path:
    sys.path.insert(0, "/opt/trn_rl_repo")

from contextlib import ExitStack
import math
import numpy as np

import concourse.bass as bass
import concourse.mybir as mybir
from concourse.bass_utils import run_bass_kernel_spmd

F32 = mybir.dt.float32
BF16 = mybir.dt.bfloat16
AF = mybir.ActivationFunctionType
ALU = mybir.AluOpType
AX = mybir.AxisListType

B, T, D, H, DH = 2, 8192, 2048, 16, 128
DQ, DKV, DKR = 682, 1024, 64
DCKV = DKV + DKR
NCQ = 6
CQ_TAIL = DQ - 5 * 128
NCKV = 8
NCD = 16
TCORE = 2048
NT = 8
NSTRIP = 4
SW = 512
QG = 256
ATT_SCALE = 1.0 / math.sqrt(DH)
EPS = 1e-5
NE, RLORA = 8, 16
ER = NE * RLORA

QNT = 4
K_SBLK = H * DH * SW
V_SBLK = H * SW * DH
SBLK = K_SBLK + V_SBLK + 1024
KOFFs, VOFFs, NOFFs = 0, K_SBLK, K_SBLK + V_SBLK
CONTRIB = NSTRIP * SBLK


def AP(t, off, dims):
    return bass.AP(t, off, [list(d) for d in dims])


class Sem:
    __slots__ = ("h", "n")

    def __init__(self, h):
        self.h = h
        self.n = 0


class Streams:
    NAMES = ("sync", "tensor", "vector", "scalar", "gpsimd")

    def __init__(self):
        self.s = {n: [] for n in self.NAMES}

    def emit(self, eng, fn, sem=None, k=1):
        if sem is None:
            self.s[eng].append(fn)
            return None
        sem.n += k
        self.s[eng].append(lambda e, f=fn, sh=sem.h, kk=k: f(e).then_inc(sh, kk))
        return sem.n

    def wait(self, eng, sem, val=None):
        v = sem.n if val is None else val
        if v > 0:
            self.s[eng].append(lambda e, sh=sem.h, vv=v: e.wait_ge(sh, vv))

    def flush(self, nc):
        with nc.Block() as block:
            for n in self.NAMES:
                ops = self.s[n]
                if ops:
                    getattr(block, n)(lambda e, ops=ops: [f(e) for f in ops])
        self.s = {n: [] for n in self.NAMES}


class MK:
    def __init__(self, taps=(), stop_after=None):
        self.taps = list(taps)
        self.stop_after = stop_after
        self.nc = bass.Bass("TRN2", target_bir_lowering=False)

    def build(self):
        nc = self.nc

        def din(name, shape):
            return nc.dram_tensor(name, list(shape), F32, kind="ExternalInput")

        self.x_own = din("x_own", [TCORE, D])
        self.w_dq = din("w_dq", [DQ, D])
        self.b_dq = din("b_dq", [DQ])
        self.qng = din("qng", [DQ])
        self.qnb = din("qnb", [DQ])
        self.w_uq = din("w_uq", [D, DQ])
        self.b_uq = din("b_uq", [D])
        self.w_dkv = din("w_dkv", [DCKV, D])
        self.b_dkv = din("b_dkv", [DCKV])
        self.kvg = din("kvg", [DKV])
        self.kvb = din("kvb", [DKV])
        self.w_ukv = din("w_ukv", [DKV + D, DKV])
        self.b_ukv = din("b_ukv", [DKV + D])
        self.w_o = din("w_o", [D, D])
        self.b_o = din("b_o", [D])
        self.w_ffn = din("w_ffn", [D, D])
        self.b_ffn = din("b_ffn", [D])
        self.lora_a = din("lora_a", [NE, RLORA, D])
        self.lora_b = din("lora_b", [NE, D, RLORA])
        self.gate_w = din("gate_w", [NE, D])
        self.gate_b = din("gate_b", [NE])
        self.cosrep = din("cosrep", [128, D])
        self.sinrep = din("sinrep", [128, D])
        self.mthr = din("mthr", [128, 1])

        self.out = nc.dram_tensor("out", [TCORE, D], F32, kind="ExternalOutput")

        self.contrib = nc.dram_tensor("contrib", [CONTRIB], BF16)
        self.gathered = nc.dram_tensor("gathered", [4 * CONTRIB], BF16)
        self.q_spill = nc.dram_tensor("q_spill", [H * QNT * DH * SW], BF16)
        self.ot_spill = nc.dram_tensor("ot_spill", [H * QNT * DH * SW], BF16)
        self.wo_sp = nc.dram_tensor("wo_sp", [D * D], BF16)
        self.wffn_sp = nc.dram_tensor("wffn_sp", [D * D], BF16)
        self.acat_sp = nc.dram_tensor("acat_sp", [ER * D], BF16)
        self.bcat_sp = nc.dram_tensor("bcat_sp", [ER * D], BF16)
        self.gate_sp = nc.dram_tensor("gate_sp", [128 * NCD * NE], BF16)
        self.dbg1 = nc.dram_tensor("dbg1", [TCORE * D], BF16)
        self.dbg2 = nc.dram_tensor("dbg2", [TCORE * D], BF16)
        self.dbg3 = nc.dram_tensor("dbg3", [TCORE * ER], BF16)

        es = ExitStack()
        self.es = es

        def sb(name, shape, dt=F32, side="left"):
            return es.enter_context(nc.sbuf_tensor(name, list(shape), dt, side=side))
        self.sb = sb

        with es:
            self.identb = sb("identb", [128, 128], BF16)
            self.identf = sb("identf", [128, 128], F32)
            self.ones_bf = sb("ones_bf", [128, 1], BF16)
            self.onesA = sb("onesA", [128, 1], BF16)
            self.onesB = sb("onesB", [128, 1], BF16)
            self.onesrow_bf = sb("onesrow_bf", [1, 128], BF16)
            self.onesrow_f = sb("onesrow_f", [1, 128], F32)
            self.epsb = sb("epsb", [128, 1], F32)
            self.bdq_c = sb("bdq_c", [128, NCQ], F32)
            self.buq_c = sb("buq_c", [128, NCD], F32)
            self.bdkv_c = sb("bdkv_c", [128, 9], F32)
            self.buk_c = sb("buk_c", [128, NCKV], F32)
            self.qng_c = sb("qng_c", [128, NCQ], F32)
            self.qnb_c = sb("qnb_c", [128, NCQ], F32)
            self.kvg_c = sb("kvg_c", [128, NCKV], F32)
            self.kvb_c = sb("kvb_c", [128, NCKV], F32)
            self.bukvv_r = sb("bukvv_r", [1, D], BF16)
            self.maxq2 = sb("maxq2", [1, H * QNT], F32)
            self.maxk2 = sb("maxk2", [1, H], F32)
            self.negm_bc = sb("negm_bc", [128, QNT * H], F32)

            for nm in ("sLD", "sGP", "sGE", "sPE", "sDV", "sAC", "sST", "sCC",
                       "sLDa", "sLDb", "sQTa", "sQTb", "sKNa", "sKNb", "sKR",
                       "sXa", "sXb", "sKVa", "sKVb", "sQBa", "sQBb", "sOTa", "sOTb",
                       "sWL", "sOLa", "sOLb", "sXT"):
                setattr(self, nm, Sem(es.enter_context(nc.semaphore(nm))))

            cs = ExitStack()
            self.cosr = cs.enter_context(nc.sbuf_tensor("cosr", [128, D], F32, side="right"))
            self.sinr = cs.enter_context(nc.sbuf_tensor("sinr", [128, D], F32, side="right"))
            with nc.named_scope('p0c'):
                self.p0_consts()
            wa = ExitStack()
            self.WdqT = wa.enter_context(nc.sbuf_tensor("WdqT", [128, NCD * DQ], BF16))
            self.WuqT = wa.enter_context(nc.sbuf_tensor("WuqT", [128, NCQ * D], BF16))
            self.WdkvT = wa.enter_context(nc.sbuf_tensor("WdkvT", [128, NCD * DCKV], BF16))
            self.WukT = wa.enter_context(nc.sbuf_tensor("WukT", [128, NCKV * DKV], BF16))
            self.WuvT = wa.enter_context(nc.sbuf_tensor("WuvT", [128, NCKV * D], BF16))
            with nc.named_scope('p0w'):
                self.p0_weights()
            cs.close()
            if self.stop_after == "p0":
                wa.close()
                self._emit_taps()
                return nc
            with nc.named_scope('p1'):
                self.p1()
            wa.close()
            self.mask = sb("mask", [128, 16 * SW], BF16)
            self.bo_r = sb("bo_r", [1, D], BF16)
            self.bffn_r = sb("bffn_r", [1, D], BF16)
            self.gateb_r = sb("gateb_r", [1, NE], BF16)
            self.WoT = sb("WoT", [128, NCD * D], BF16)
            self.WffnT = sb("WffnT", [128, NCD * D], BF16)
            self.AcatT = sb("AcatT", [128, NCD * ER], BF16)
            self.gateT = sb("gateT", [128, NCD * NE], BF16)
            self.BcatT = sb("BcatT", [128, D], BF16)
            with nc.named_scope('ag'):
                self.ag()
            if self.stop_after == "p1":
                self._emit_taps()
                return nc
            with nc.named_scope('p2'):
                self.p2()
            if self.stop_after == "p2":
                self._emit_taps()
                return nc
            with nc.named_scope('p3'):
                self.p3()
            self._emit_taps()
        return nc

    def _emit_taps(self):
        if not self.taps:
            return
        st = Streams()
        tmap = {"contrib": (self.contrib, CONTRIB),
                "gathered": (self.gathered, 4 * CONTRIB),
                "q_spill": (self.q_spill, H * NT * DH * QG),
                "ot_spill": (self.ot_spill, H * NT * DH * QG),
                "wo_sp": (self.wo_sp, D * D),
                "wffn_sp": (self.wffn_sp, D * D),
                "acat_sp": (self.acat_sp, ER * D),
                "bcat_sp": (self.bcat_sp, ER * D),
                "gate_sp": (self.gate_sp, 128 * NCD * NE),
                "dbg1": (self.dbg1, TCORE * D), "dbg2": (self.dbg2, TCORE * D),
                "dbg3": (self.dbg3, TCORE * ER)}
        for t in self.taps:
            th, size = tmap[t]
            tout = self.nc.dram_tensor("tap_" + t, [size], BF16, kind="ExternalOutput")
            per = size // 128
            st.emit("sync", lambda e, a=tout, b=th, p=per: e.dma_start(
                out=AP(a, 0, [[p, 128], [1, p]]), in_=AP(b, 0, [[p, 128], [1, p]])),
                self.sST, 16)
        st.wait("sync", self.sST)
        st.flush(self.nc)

    def p0_consts(self):
        nc = self.nc
        sLD, sGP, sGE, sDV, sAC = self.sLD, self.sGP, self.sGE, self.sDV, self.sAC
        st = Streams()
        with ExitStack() as p0:
            vrow = p0.enter_context(nc.sbuf_tensor("p0vrow", [1, DKV + D], F32, side="right"))
            vrow2 = p0.enter_context(nc.sbuf_tensor("p0vrow2", [1, D], F32, side="right"))
            rscr = p0.enter_context(nc.sbuf_tensor("p0rscr", [1, D], F32, side="right"))

            st.emit("gpsimd", lambda e: e.iota(self.identf[:], [[1, 128]], channel_multiplier=-1,
                                               allow_small_or_imprecise_dtypes=True), sGE)
            st.emit("gpsimd", lambda e: e.memset(self.ones_bf[:], 1.0), sGE)
            st.emit("gpsimd", lambda e: e.memset(self.onesrow_bf[:], 1.0), sGE)
            st.emit("gpsimd", lambda e: e.memset(self.onesrow_f[:], 1.0), sGE)
            st.emit("gpsimd", lambda e: e.memset(self.epsb[:], EPS), sGE)
            st.emit("gpsimd", lambda e: e.iota(self.onesA[:], [[0, 1]], channel_multiplier=1,
                                               allow_small_or_imprecise_dtypes=True), sGE)
            st.emit("sync", lambda e: e.dma_start(out=self.cosr[:], in_=self.cosrep.ap()), sLD, 16)
            st.emit("sync", lambda e: e.dma_start(out=self.sinr[:], in_=self.sinrep.ap()), sLD, 16)
            st.wait("vector", sGE)
            st.wait("vector", sLD)
            st.emit("vector", lambda e: e.tensor_scalar(self.identf[:], self.identf[:], 0.0, None,
                                                        op0=ALU.is_equal), sDV)
            st.emit("vector", lambda e: e.tensor_scalar(self.onesB[:], self.onesA[:], 63.5, None,
                                                        op0=ALU.is_gt), sDV)
            st.wait("vector", sDV)
            st.emit("vector", lambda e: e.tensor_scalar(self.onesA[:], self.onesA[:], 63.5, None,
                                                        op0=ALU.is_lt), sDV)
            st.wait("vector", sDV)
            st.emit("vector", lambda e: e.tensor_copy(self.identb[:], self.identf[:]), sDV)

            def vec_load(dram, n):
                st.wait("sync", sGP)
                st.wait("sync", sDV)
                st.emit("sync", lambda e, d=dram, nn=n: e.dma_start(
                    out=vrow[0:1, 0:nn], in_=AP(d, 0, [[nn, 1], [1, nn]])), sLD, 16)

            def col_store(dst, n, nch, src=None, src_off=0):
                s = vrow if src is None else src
                st.wait("gpsimd", sLD)
                st.wait("gpsimd", sDV)
                st.wait("gpsimd", self.sGE)
                for c in range(nch):
                    w = min(128, n - c * 128)
                    st.emit("gpsimd", lambda e, dd=dst, cc=c, ww=w, so=src_off, ss=s: e.dma_start(
                        out=dd[0:ww, cc:cc + 1],
                        in_=ss[0:1, so + cc * 128:so + cc * 128 + ww]), sGP, 16)

            for dram, dst in ((self.b_dq, self.bdq_c), (self.qng, self.qng_c),
                              (self.qnb, self.qnb_c)):
                st.emit("gpsimd", lambda e, dd=dst: e.memset(dd[:], 0.0), sGE)
                vec_load(dram, DQ)
                col_store(dst, DQ, NCQ)
            st.emit("gpsimd", lambda e: e.memset(self.bdkv_c[:], 0.0), sGE)
            vec_load(self.b_dkv, DCKV)
            col_store(self.bdkv_c, DCKV, 9)
            for dram, dst in ((self.kvg, self.kvg_c), (self.kvb, self.kvb_c)):
                vec_load(dram, DKV)
                col_store(dst, DKV, NCKV)
            vec_load(self.b_ukv, DKV + D)
            st.wait("gpsimd", sLD)
            for co in range(NCKV):
                st.emit("gpsimd", lambda e, cc=co: e.dma_start(
                    out=self.buk_c[0:64, cc:cc + 1],
                    in_=vrow[0:1, 384 * cc:384 * cc + 64]), sGP, 16)
                st.emit("gpsimd", lambda e, cc=co: e.dma_start(
                    out=self.buk_c[64:128, cc:cc + 1],
                    in_=vrow[0:1, 384 * cc + 192:384 * cc + 192 + 64]), sGP, 16)
            st.wait("vector", sLD)
            st.emit("vector", lambda e: e.tensor_copy(
                self.bukvv_r[:], AP(vrow, 64, [[DKV + D, 1], [192, H], [1, 128]])), sDV)

            rc = lambda t, w, p=1: AP(t, 64, [[w, p], [128, H], [1, 64]])
            rp = lambda t, w, p=1: AP(t, 96, [[w, p], [128, H], [-32, 2], [1, 32]])
            vec_load(self.b_uq, D)
            st.wait("vector", sLD)
            st.emit("vector", lambda e: e.tensor_copy(vrow2[0:1, :], vrow[0:1, 0:D]), sDV)
            st.wait("vector", sDV)
            st.emit("vector", lambda e: e.tensor_tensor(
                rc(vrow2, D), rc(vrow, DKV + D), rc(self.cosr, D), op=ALU.mult), sDV)
            st.emit("vector", lambda e: e.tensor_tensor(
                rc(rscr, D), rp(vrow, DKV + D), rc(self.sinr, D), op=ALU.mult), sDV)
            st.wait("vector", sDV)
            st.emit("vector", lambda e: e.tensor_tensor(
                rc(vrow2, D), rc(vrow2, D), rc(rscr, D), op=ALU.add), sDV)
            col_store(self.buq_c, D, NCD, src=vrow2)
            st.wait("sync", sGP)
            st.flush(nc)

    def p0_weights(self):
        nc = self.nc
        sLD, sGP, sPE, sDV = self.sLD, self.sGP, self.sPE, self.sDV
        st = Streams()
        with ExitStack() as p0:
            stage = [p0.enter_context(nc.sbuf_tensor("p0st%d" % i, [128, D], F32, side="right"))
                     for i in range(2)]
            rot = p0.enter_context(nc.sbuf_tensor("p0rot", [128, D], F32, side="right"))
            rots = p0.enter_context(nc.sbuf_tensor("p0rots", [128, D], F32, side="right"))
            rots2 = p0.enter_context(nc.sbuf_tensor("p0rots2", [128, D], F32, side="right"))
            spb = p0.enter_context(nc.sbuf_tensor("p0spb", [128, D], BF16, side="right"))
            ptr = [p0.enter_context(nc.psum_tensor("p0ps%d" % i, [128, 512], F32))
                   for i in range(2)]
            stage_free = [0, 0]
            ps_free = [0, 0]
            sLDs = [self.sLDa, self.sLDb]
            cur = [0]
            trn = [0]

            def load_stripe(dram, off_elems, rh, cols):
                i = cur[0] % 2
                cur[0] += 1
                b = stage[i]
                st.wait("sync", sPE, stage_free[i])
                st.emit("sync", lambda e, d=dram, ro=off_elems, r=rh, C=cols, bb=b: e.dma_start(
                    out=bb[0:r, 0:C], in_=AP(d, ro, [[C, r], [1, C]])), sLDs[i], 16)
                return i

            def tr(i, c0, rh, cw, evac):
                k = trn[0] % 2
                trn[0] += 1
                p = ptr[k]
                st.wait("tensor", sLDs[i], sLDs[i].n)
                st.wait("tensor", sDV, ps_free[k])
                st.emit("tensor", lambda e, bb=stage[i], pp=p, r=rh, cc=c0, w=cw: e.transpose(
                    pp[0:w, 0:r], bb[0:r, cc:cc + w], self.identf[0:r, 0:r]), sPE)
                st.wait("vector", sPE)
                evac(p)
                ps_free[k] = sDV.n
                stage_free[i] = sPE.n

            def cp_evac(tgt, coff, rh, cw=128):
                def f(p):
                    st.emit("vector", lambda e, pp=p, t=tgt, co=coff, r=rh, w=cw: e.tensor_copy(
                        t[0:w, co:co + r], pp[0:w, 0:r]), sDV)
                return f

            def tr4(i, c0, rh, tgt, coff, blk):
                k = trn[0] % 2
                trn[0] += 1
                p = ptr[k]
                st.wait("tensor", sLDs[i], sLDs[i].n)
                st.wait("tensor", sDV, ps_free[k])
                for j in range(4):
                    st.emit("tensor", lambda e, bb=stage[i], pp=p, r=rh,
                            cc=c0 + 128 * j, j=j: e.transpose(
                                pp[0:128, j * 128:j * 128 + r], bb[0:r, cc:cc + 128],
                                self.identf[0:r, 0:r]), sPE if j == 3 else None)
                st.wait("vector", sPE)
                tw = tgt.shape[1]
                st.emit("vector", lambda e, pp=p, t=tgt, co=coff, r=rh, b=blk, w=tw:
                        e.tensor_copy(AP(t, co, [[w, 128], [b, 4], [1, r]]),
                                      AP(pp, 0, [[512, 128], [128, 4], [1, r]])), sDV)
                ps_free[k] = sDV.n
                stage_free[i] = sPE.n

            def simple_w(dram, R, C, tgt, blk):
                for ro in range((R + 127) // 128):
                    rh = min(128, R - ro * 128)
                    i = load_stripe(dram, ro * 128 * C, rh, C)
                    for cc4 in range(C // 512):
                        tr4(i, cc4 * 512, rh, tgt, (cc4 * 4) * blk + ro * 128, blk)

            simple_w(self.w_dq, DQ, D, self.WdqT, DQ)
            simple_w(self.w_dkv, DCKV, D, self.WdkvT, DCKV)
            for hh in range(H):
                i = load_stripe(self.w_ukv, (192 * hh) * DKV, 64, DKV)
                for cc4 in range(NCKV // 4):
                    tr4(i, cc4 * 512, 64, self.WukT, (cc4 * 4) * DKV + 64 * hh, DKV)
            for hh in range(H):
                i = load_stripe(self.w_ukv, (192 * hh + 64) * DKV, 128, DKV)
                for cc4 in range(NCKV // 4):
                    tr4(i, cc4 * 512, 128, self.WuvT, (cc4 * 4) * D + 128 * hh, D)

            rc = lambda t, w, p: AP(t, 64, [[w, p], [128, H], [1, 64]])
            rp = lambda t, w, p: AP(t, 96, [[w, p], [128, H], [-32, 2], [1, 32]])
            for ccq in range(NCQ):
                cw = min(128, DQ - ccq * 128)
                for ro in range(NCD):
                    i = load_stripe(self.w_uq, ro * 128 * DQ, 128, DQ)
                    tr(i, ccq * 128, 128, cw, cp_evac(rot, ro * 128, 128, cw))
                st.wait("vector", sDV)
                base = ccq * D
                st.emit("vector", lambda e, w=cw, b=base: e.tensor_copy(
                    AP(self.WuqT, b, [[NCQ * D, w], [128, H], [1, 64]]),
                    AP(rot, 0, [[D, w], [128, H], [1, 64]])), sDV)
                st.emit("vector", lambda e, w=cw: e.tensor_tensor(
                    rc(rots, D, w), rc(rot, D, w), rc(self.cosr, D, w), op=ALU.mult), sDV)
                st.emit("vector", lambda e, w=cw: e.tensor_tensor(
                    rc(rots2, D, w), rp(rot, D, w), rc(self.sinr, D, w), op=ALU.mult), sDV)
                st.wait("vector", sDV)
                st.emit("vector", lambda e, w=cw, b=base: e.tensor_tensor(
                    AP(self.WuqT, b + 64, [[NCQ * D, w], [128, H], [1, 64]]),
                    rc(rots, D, w), rc(rots2, D, w), op=ALU.add), sDV)
                st.wait("vector", sDV)

            def spill_stripes(dram, sp, nro, spill_ap_fn):
                for ro in range(nro):
                    i = load_stripe(dram, ro * 128 * D, 128, D)
                    st.wait("vector", sGP)
                    for cc4 in range(NCD // 4):
                        tr4(i, cc4 * 512, 128, spb, cc4 * 512, 128)
                    st.wait("gpsimd", sDV)
                    st.emit("gpsimd", lambda e, s=sp, r=ro, f=spill_ap_fn: e.dma_start(
                        out=f(s, r), in_=spb[:]), sGP, 16)

            wsp_ap = lambda s, r: AP(s, r * 128, [[NCD * D, 128], [D, NCD], [1, 128]])
            spill_stripes(self.w_o, self.wo_sp, NCD, wsp_ap)
            spill_stripes(self.w_ffn, self.wffn_sp, NCD, wsp_ap)

            i = load_stripe(self.lora_a, 0, 128, D)
            st.wait("vector", sGP)
            for cc4 in range(NCD // 4):
                tr4(i, cc4 * 512, 128, spb, cc4 * 512, 128)
            st.wait("gpsimd", sDV)
            st.emit("gpsimd", lambda e: e.dma_start(
                out=AP(self.acat_sp, 0, [[NCD * 128, 128], [1, NCD * 128]]), in_=spb[:]), sGP, 16)

            i = load_stripe(self.gate_w, 0, NE, D)
            st.wait("vector", sGP)
            for cc in range(NCD):
                tr(i, cc * 128, NE, 128, cp_evac(spb, cc * NE, NE))
            st.wait("gpsimd", sDV)
            st.emit("gpsimd", lambda e: e.dma_start(
                out=AP(self.gate_sp, 0, [[NCD * NE, 128], [1, NCD * NE]]),
                in_=spb[0:128, 0:NCD * NE]), sGP, 16)

            st.wait("vector", sGP)
            for co in range(NCD):
                ii = cur[0] % 2
                cur[0] += 1
                b = stage[ii]
                st.wait("sync", sPE, stage_free[ii])
                st.emit("sync", lambda e, c0=co * 128, bb=b: e.dma_start(
                    out=bb[0:128, 0:ER],
                    in_=AP(self.lora_b, c0 * RLORA,
                           [[RLORA, 128], [D * RLORA, NE], [1, RLORA]])), sLDs[ii], 16)
                tr(ii, 0, 128, 128, cp_evac(spb, co * 128, 128))
            st.wait("gpsimd", sDV)
            st.emit("gpsimd", lambda e: e.dma_start(
                out=AP(self.bcat_sp, 0, [[NCD * 128, 128], [1, NCD * 128]]),
                in_=spb[0:128, 0:NCD * 128]), sGP, 16)
            st.wait("gpsimd", sGP)
            st.flush(nc)

    def p1(self):
        nc = self.nc
        sLD, sGP, sPE, sDV, sAC, sST = self.sLD, self.sGP, self.sPE, self.sDV, self.sAC, self.sST
        st = Streams()
        with ExitStack() as p1:
            def rsb(name, shape, dt=F32):
                return p1.enter_context(nc.sbuf_tensor(name, list(shape), dt, side="right"))

            xT = rsb("xT", [128, NCD * SW], BF16)
            kvn = rsb("kvn", [128, NCKV * SW], BF16)
            raw = rsb("raw", [128, NCKV * SW], BF16)
            cqraw = rsb("cqraw", [128, NCQ * SW], BF16)
            cqn = cqraw
            krr = rsb("krr", [64, SW], BF16)
            xv = rsb("xv", [128, 4 * D], BF16)
            cqr, ckvr = cqraw, raw
            xbf, vsb = xv, xv
            sq = [rsb("sq%d" % i, [128, SW], BF16) for i in range(2)]
            sqkr = rsb("sqkr", [64, SW], BF16)
            _s2 = rsb("s2", [128, SW], F32)
            s2 = [_s2, _s2]
            qT = [rsb("qT%d" % i, [128, SW], BF16) for i in range(2)]
            knr = [rsb("knr%d" % i, [128, SW], BF16) for i in range(2)]
            murow = rsb("murow", [1, SW], F32)
            rrow = rsb("rrow", [1, SW], F32)
            varow = rsb("varow", [1, SW], F32)
            kr2row = rsb("kr2row", [1, SW], BF16)
            tiny = rsb("tiny", [1, 8], F32)
            nrmbf = rsb("nrmbf", [1, 1024], BF16)

            pA = [p1.enter_context(nc.psum_tensor("p1a%d" % i, [128, 512], F32))
                  for i in range(2)]
            pS = p1.enter_context(nc.psum_tensor("p1s", [1, 1024], F32))
            pS2 = p1.enter_context(nc.psum_tensor("p1s2", [1, 1024], F32))
            pMu = p1.enter_context(nc.psum_tensor("p1mu", [128, 512], F32))
            pR = p1.enter_context(nc.psum_tensor("p1r", [128, 512], F32))

            self._pa_free = [0, 0]
            self._pa_i = 0
            self._tr_free = [0, 0]
            p1pend = [None]

            def pend_flush():
                if p1pend[0] is not None:
                    p1pend[0]()
                    p1pend[0] = None

            def pend_set(fn):
                pend_flush()
                p1pend[0] = fn
            sQT = [self.sQTa, self.sQTb]
            sKN = [self.sKNa, self.sKNb]
            sKR = self.sKR

            def proj(nchunks, stat_fn, mov, mov_base_fn, out_rows, evac, kdims=None,
                     extra=None):
                k = self._pa_i % 2
                self._pa_i += 1
                p = pA[k]
                st.wait("tensor", sDV, self._pa_free[k])
                for c in range(nchunks):
                    kd = 128 if kdims is None else kdims[c]
                    last = (c == nchunks - 1) and extra is None
                    st.emit("tensor",
                            lambda e, pp=p, s=stat_fn(c), m=mov, mb=mov_base_fn(c), orr=out_rows,
                            r=kd, c0=(c == 0), st_=last: e.matmul(
                                pp[0:orr, 0:SW], s, m[0:r, mb:mb + SW], start=c0, stop=st_,
                                skip_group_check=True),
                            sPE if last else None)
                    if c == 1:
                        pend_flush()
                if extra is not None:
                    extra(p)
                st.wait("vector", sPE)
                evac(p)
                self._pa_free[k] = sDV.n
                return p

            for s in range(NSTRIP):
                st.wait("gpsimd", sPE)
                st.wait("gpsimd", sST)
                st.wait("tensor", sDV)
                sX = [self.sXa, self.sXb]
                for half in range(2):
                    st.emit("gpsimd", lambda e, ss=s, hf=half: e.dma_start(
                        out=xbf[0:128, 0:2 * D] if hf == 0 else xbf[0:128, 2 * D:4 * D],
                        in_=AP(self.x_own, (ss * SW + hf * 256) * D,
                               [[D, 128], [128 * D, 2], [1, D]])), sX[half], 16)
                    st.wait("tensor", sX[half], sX[half].n)
                    for bank in range(4):
                        k = bank % 2
                        p = pMu if k == 0 else pR
                        st.wait("tensor", sDV, self._tr_free[k])
                        for j in range(8):
                            t_idx = bank * 8 + j
                            cx, i = t_idx // 2, t_idx % 2
                            st.emit("tensor", lambda e, pp=p, j_=j, cx_=cx, i_=i, hf=half:
                                    e.transpose(
                                        pp.bitcast(BF16)[0:128, j_ * 128:j_ * 128 + 128],
                                        xbf[0:128, (2 * hf + i_) * D + cx_ * 128:
                                            (2 * hf + i_) * D + cx_ * 128 + 128],
                                        self.identb[:]), sPE if j == 7 else None)
                        st.wait("vector", sPE)
                        st.emit("vector", lambda e, pp=p, b=bank, hf=half: e.tensor_copy(
                            AP(xT, (b * 4) * SW + hf * 256,
                               [[NCD * SW, 128], [SW, 4], [1, 256]]),
                            AP(pp.bitcast(BF16), 0, [[1024, 128], [256, 4], [1, 256]])), sDV)
                        self._tr_free[k] = sDV.n

                st.wait("tensor", sDV)
                for co in range(NCQ):
                    w = 128 if co < 5 else CQ_TAIL

                    def evac_cq(p, co=co, w=w):
                        st.emit("vector", lambda e, pp=p: e.tensor_scalar(
                            cqr[0:w, co * SW:(co + 1) * SW], pp[0:w, 0:SW],
                            self.bdq_c[0:w, co:co + 1], None, op0=ALU.add), sDV)
                        st.wait("vector", sDV)
                        st.emit("vector", lambda e: e.tensor_tensor(
                            sq[co % 2][0:w, 0:SW], cqr[0:w, co * SW:(co + 1) * SW],
                            cqr[0:w, co * SW:(co + 1) * SW], op=ALU.mult), sDV)

                    proj(NCD, lambda c, co=co, w=w:
                         self.WdqT[0:128, c * DQ + co * 128:c * DQ + co * 128 + w],
                         xT, lambda c: c * SW, w, evac_cq)

                    def stat_cq(co=co, w=w, dvn=None):
                        def f():
                            st.wait("tensor", sDV)
                            st.emit("tensor", lambda e: e.matmul(
                                pS[0:1, 0:512], self.ones_bf[0:w, :],
                                cqr[0:w, co * SW:(co + 1) * SW],
                                start=(co == 0), stop=(co == NCQ - 1),
                                skip_group_check=True), None)
                            st.emit("tensor", lambda e: e.matmul(
                                pS[0:1, 512:1024], self.ones_bf[0:w, :],
                                sq[co % 2][0:w, 0:SW],
                                start=(co == 0), stop=(co == NCQ - 1),
                                skip_group_check=True),
                                sPE if co == NCQ - 1 else None)
                        return f
                    pend_set(stat_cq())

                def ckv_section(s=s):
                  for co in range(9):
                    w = 128 if co < 8 else 64

                    def evac_ckv(p, co=co, w=w):
                        tgt = (ckvr, co * SW) if co < 8 else (krr, 0)
                        if co == 8:
                            st.wait("vector", sKR, sKR.n)
                        st.emit("vector", lambda e, pp=p, tg=tgt: e.tensor_scalar(
                            tg[0][0:w, tg[1]:tg[1] + SW], pp[0:w, 0:SW],
                            self.bdkv_c[0:w, co:co + 1], None, op0=ALU.add), sDV)
                        st.wait("vector", sDV)
                        dst_sq = sq[co % 2] if co < 8 else sqkr
                        st.emit("vector", lambda e, tg=tgt, dsq=dst_sq: e.tensor_tensor(
                            dsq[0:w, 0:SW], tg[0][0:w, tg[1]:tg[1] + SW],
                            tg[0][0:w, tg[1]:tg[1] + SW], op=ALU.mult), sDV)

                    proj(NCD, lambda c, co=co, w=w:
                         self.WdkvT[0:128, c * DCKV + co * 128:c * DCKV + co * 128 + w],
                         xT, lambda c: c * SW, w, evac_ckv)
                    if co < 8:
                        def stat_ckv(co=co):
                            def f():
                                st.wait("tensor", sDV)
                                st.emit("tensor", lambda e: e.matmul(
                                    pS2[0:1, 0:512], self.ones_bf[:],
                                    ckvr[0:128, co * SW:(co + 1) * SW],
                                    start=(co == 0), stop=(co == 7),
                                    skip_group_check=True), None)
                                st.emit("tensor", lambda e: e.matmul(
                                    pS2[0:1, 512:1024], self.ones_bf[:], sq[co % 2][:],
                                    start=(co == 0), stop=(co == 7),
                                    skip_group_check=True),
                                    sPE if co == 7 else None)
                            return f
                        pend_set(stat_ckv())

                ckv_section()
                pend_flush()
                self._ln(st, pS, pMu, pR, murow, rrow, varow, DQ, NCQ, cqr, cqn, s2,
                         self.qng_c, self.qnb_c, CQ_TAIL, SW)
                dv_cqn = self.sDV.n

                pend_flush()
                self._ln(st, pS2, pMu, pR, murow, rrow, varow, DKV, NCKV, ckvr, kvn, s2,
                         self.kvg_c, self.kvb_c, 128, SW)

                st.wait("tensor", sDV, dv_cqn)
                kdims_q = [128] * 5 + [CQ_TAIL]
                for h in range(H):
                    def evac_q(p, h=h):
                        qb = qT[h % 2]
                        st.wait("vector", sQT[h % 2], sQT[h % 2].n)
                        st.emit("vector", lambda e, pp=p: e.tensor_scalar(
                            qb[:], pp[0:128, 0:SW], self.buq_c[0:128, h:h + 1], None,
                            op0=ALU.add), sDV)
                        st.wait("vector", sDV)
                        st.emit("vector", lambda e: e.tensor_tensor(
                            sq[h % 2][:], qb[:], qb[:], op=ALU.mult), sDV)

                    proj(NCQ, lambda c, h=h:
                         self.WuqT[0:kdims_q[c], c * D + h * 128:c * D + h * 128 + 128],
                         cqn, lambda c: c * SW, 128, evac_q, kdims=kdims_q)
                    st.wait("sync", sDV)
                    st.emit("sync", lambda e, h=h, ss=s: e.dma_start(
                        out=AP(self.q_spill, (h * QNT + ss) * DH * SW, [[SW, 128], [1, SW]]),
                        in_=qT[h % 2][0:128, 0:SW]), sQT[h % 2], 16)
                    def stat_q(h=h, ss=s):
                        def f():
                            st.wait("tensor", sDV)
                            st.emit("tensor", lambda e: e.matmul(
                                pS[0:1, 0:512], self.ones_bf[:], sq[h % 2][:],
                                start=True, stop=True, skip_group_check=True), sPE)
                            st.wait("vector", sPE)
                            st.emit("vector", lambda e: e.tensor_reduce(
                                self.maxq2[0:1, h * QNT + ss:h * QNT + ss + 1],
                                pS[0:1, 0:SW], axis=AX.X, op=ALU.max), sDV)
                        return f
                    pend_set(stat_q())

                st.wait("tensor", sDV)
                st.emit("tensor", lambda e: e.matmul(
                    pS2[0:1, 0:512], self.ones_bf[0:64, :],
                    sqkr[0:64, 0:SW], start=True, stop=True, skip_group_check=True), sPE)
                st.wait("vector", sPE)
                st.emit("vector", lambda e: e.tensor_copy(kr2row[0:1, :], pS2[0:1, 0:512]), sDV)
                st.wait("tensor", sDV)

                st.wait("sync", sDV)
                for h in range(H):
                    st.emit("sync", lambda e, h=h, ss=s: e.dma_start(
                        out=AP(self.contrib, ss * SBLK + (h * DH + 64) * SW,
                               [[SW, 64], [1, SW]]),
                        in_=krr[0:64, 0:SW]), sKR, 16)

                st.wait("tensor", sDV)
                for co in range(NCKV):
                    def evac_k(p, co=co):
                        kb = knr[co % 2]
                        st.wait("vector", sKN[co % 2], sKN[co % 2].n)
                        st.emit("vector", lambda e, pp=p: e.tensor_scalar(
                            kb[:], pp[0:128, 0:SW], self.buk_c[0:128, co:co + 1], None,
                            op0=ALU.add), sDV)
                        st.wait("vector", sDV)
                        st.emit("vector", lambda e: e.tensor_tensor(
                            sq[co % 2][:], kb[:], kb[:], op=ALU.mult), sDV)

                    proj(NCKV, lambda c, co=co:
                         self.WukT[0:128, c * DKV + co * 128:c * DKV + co * 128 + 128],
                         kvn, lambda c: c * SW, 128, evac_k)
                    st.wait("sync", sDV)
                    for hf in range(2):
                        h = 2 * co + hf
                        st.emit("sync", lambda e, h=h, hf=hf, co=co, ss=s: e.dma_start(
                            out=AP(self.contrib, ss * SBLK + h * DH * SW,
                                   [[SW, 64], [1, SW]]),
                            in_=knr[co % 2][hf * 64:hf * 64 + 64, 0:SW]), sKN[co % 2], 16)
                    def knorm(co=co):
                        def f():
                            st.wait("tensor", sDV)
                            st.emit("tensor", lambda e: e.matmul(
                                pS2[0:1, 0:512], self.onesA[:], sq[co % 2][:],
                                start=True, stop=True, skip_group_check=True), None)
                            st.emit("tensor", lambda e: e.matmul(
                                pS2[0:1, 512:1024], self.onesB[:], sq[co % 2][:],
                                start=True, stop=True, skip_group_check=True), sPE)
                            st.wait("vector", sPE)
                            st.emit("vector", lambda e: e.tensor_tensor(
                                varow[0:1, 0:SW], AP(pS2, 0, [[1024, 1], [1, SW]]),
                                kr2row[0:1, 0:SW], op=ALU.add), sDV)
                            st.emit("vector", lambda e: e.tensor_tensor(
                                murow[0:1, 0:SW], AP(pS2, 512, [[1024, 1], [1, SW]]),
                                kr2row[0:1, 0:SW], op=ALU.add), sDV)
                            st.wait("vector", sDV)
                            st.emit("vector", lambda e: e.tensor_reduce(
                                tiny[0:1, 0:1], varow[0:1, 0:SW], axis=AX.X,
                                op=ALU.max), sDV)
                            st.emit("vector", lambda e: e.tensor_reduce(
                                tiny[0:1, 1:2], murow[0:1, 0:SW], axis=AX.X,
                                op=ALU.max), sDV)
                            st.wait("vector", sDV)
                            for hf in range(2):
                                h = 2 * co + hf
                                st.emit("vector", lambda e, h=h, hf=hf: e.tensor_copy(
                                    self.maxk2[0:1, h:h + 1], tiny[0:1, hf:hf + 1]), sDV)
                        return f
                    pend_set(knorm())

                for i in range(4):
                    for vs in range(4):
                        def evac_v(p, i=i, vs=vs):
                            st.emit("vector", lambda e, pp=p: e.tensor_copy(
                                vsb[0:128, i * D + vs * 512:i * D + vs * 512 + 512],
                                pp[0:128, 0:512]), sDV)

                        def extra_bias(p, vs=vs):
                            st.emit("tensor", lambda e, pp=p: e.matmul(
                                pp[0:128, 0:512], self.onesrow_bf[0:1, :],
                                self.bukvv_r[0:1, vs * 512:(vs + 1) * 512],
                                start=False, stop=True, skip_group_check=True), sPE)

                        proj(NCKV, lambda c, i=i:
                             kvn[0:128, c * SW + i * 128:c * SW + i * 128 + 128],
                             self.WuvT, lambda c, vs=vs: c * D + vs * 512, 128,
                             evac_v, extra=extra_bias)
                st.wait("sync", sDV)
                for h in range(H):
                    st.emit("sync", lambda e, h=h, ss=s: e.dma_start(
                        out=AP(self.contrib, ss * SBLK + VOFFs + h * SW * DH,
                               [[DH, 128], [128 * DH, 4], [1, DH]]),
                        in_=AP(vsb, h * 128, [[4 * D, 128], [D, 4], [1, DH]])), sST, 16)

                if s == 0:
                    st.emit("gpsimd", lambda e: e.memset(nrmbf[:], 0.0), self.sGE)
                st.wait("vector", sDV)
                st.wait("vector", self.sGE)
                st.wait("vector", sST)
                st.emit("vector", lambda e: e.tensor_scalar(
                    nrmbf[0:1, 0:16], self.maxk2[0:1, :], 1.02, None, op0=ALU.mult), sDV)
                st.wait("sync", sDV)
                st.emit("sync", lambda e, ss=s: e.dma_start(
                    out=AP(self.contrib, ss * SBLK + NOFFs, [[1024, 1], [1, 1024]]),
                    in_=nrmbf[0:1, :]), sST, 16)
                for sx in (sST, sKR, sKN[0], sKN[1]):
                    st.wait("gpsimd", sx, sx.n)
                st.emit("gpsimd", lambda e, ss=s: e.collective_compute(
                    "AllGather", ALU.bypass,
                    replica_groups=[[0, 1, 2, 3], [4, 5, 6, 7]],
                    ins=[AP(self.contrib, ss * SBLK, [[SBLK, 1], [1, SBLK]])],
                    outs=[AP(self.gathered, ss * 4 * SBLK,
                             [[4 * SBLK, 1], [1, 4 * SBLK]])]), self.sCC)

            st.wait("sync", sST)
            st.wait("gpsimd", sGP)
            for sx in (sQT[0], sQT[1],):
                st.wait("gpsimd", sx, sx.n)
            st.flush(nc)

    def _ln(self, st, pS, pMu, pR, murow, rrow, varow, DIM, NCH, raw, dst, s2, g_c, b_c,
            tail, width):
        sPE, sDV, sAC = self.sPE, self.sDV, self.sAC
        inv = 1.0 / DIM
        st.wait("vector", sPE)
        st.emit("vector", lambda e: e.tensor_scalar(
            murow[0:1, 0:width], pS[0:1, 0:512], inv, None, op0=ALU.mult), sDV)
        st.emit("vector", lambda e: e.tensor_scalar(
            varow[0:1, 0:width], pS[0:1, 512:1024], inv, None, op0=ALU.mult), sDV)
        st.wait("vector", sDV)
        st.emit("vector", lambda e: e.tensor_tensor(
            rrow[0:1, 0:width], murow[0:1, 0:width], murow[0:1, 0:width], op=ALU.mult), sDV)
        st.wait("vector", sDV)
        st.emit("vector", lambda e: e.tensor_tensor(
            varow[0:1, 0:width], varow[0:1, 0:width], rrow[0:1, 0:width], op=ALU.subtract), sDV)
        st.wait("scalar", sDV)
        st.emit("scalar", lambda e: e.activation(out=rrow[0:1, 0:width], in_=varow[0:1, 0:width],
                                                 func=AF.Ln, bias=self.epsb[0:1, :],
                                                 scale=1.0), sAC)
        st.wait("scalar", sAC)
        st.emit("scalar", lambda e: e.activation(out=rrow[0:1, 0:width], in_=rrow[0:1, 0:width],
                                                 func=AF.Exp, bias=0.0, scale=-0.5), sAC)
        st.wait("tensor", sAC)
        st.wait("tensor", sDV)
        st.emit("tensor", lambda e: e.matmul(pMu[0:128, 0:width], self.onesrow_f[0:1, :],
                                             murow[0:1, 0:width], start=True, stop=True,
                                             skip_group_check=True), None)
        st.emit("tensor", lambda e: e.matmul(pR[0:128, 0:width], self.onesrow_f[0:1, :],
                                             rrow[0:1, 0:width], start=True, stop=True,
                                             skip_group_check=True), sPE)
        st.wait("vector", sPE)
        for c in range(NCH):
            w = 128 if c < NCH - 1 else tail
            st.emit("vector", lambda e, c=c, w=w: e.tensor_tensor(
                s2[c % 2][0:w, 0:width], raw[0:w, c * width:(c + 1) * width],
                pMu[0:w, 0:width], op=ALU.subtract), sDV)
            st.wait("vector", sDV)
            st.emit("vector", lambda e, c=c, w=w: e.tensor_tensor(
                s2[c % 2][0:w, 0:width], s2[c % 2][0:w, 0:width], pR[0:w, 0:width],
                op=ALU.mult), sDV)
            st.wait("vector", sDV)
            st.emit("vector", lambda e, c=c, w=w: e.tensor_scalar(
                dst[0:w, c * width:(c + 1) * width], s2[c % 2][0:w, 0:width],
                g_c[0:w, c:c + 1], b_c[0:w, c:c + 1], op0=ALU.mult, op1=ALU.add), sDV)
            st.wait("vector", sDV)

    def ag(self):
        nc = self.nc
        st = Streams()
        sGE, sDV, sLD = self.sGE, self.sDV, self.sLD
        with ExitStack() as ages:
            scr = ages.enter_context(nc.sbuf_tensor("agscr", [128, 16 * SW], F32, side="right"))
            mthr_t = ages.enter_context(nc.sbuf_tensor("agmthr", [128, 1], F32, side="right"))
            vr = ages.enter_context(nc.sbuf_tensor("agvr", [1, D], F32, side="right"))
            st.emit("gpsimd", lambda e: e.iota(scr[:], [[-128, 16], [1, SW]],
                                               channel_multiplier=-1,
                                               allow_small_or_imprecise_dtypes=True), sGE)
            st.emit("sync", lambda e: e.dma_start(out=mthr_t[:], in_=self.mthr.ap()), sLD, 16)
            st.wait("vector", sGE)
            st.wait("vector", sLD)
            st.emit("vector", lambda e: e.tensor_scalar(self.mask[:], scr[:], mthr_t[:, 0:1],
                                                        None, op0=ALU.is_ge), sDV)
            for dram, dst, n in ((self.b_o, self.bo_r, D), (self.b_ffn, self.bffn_r, D),
                                 (self.gate_b, self.gateb_r, NE)):
                st.wait("sync", sDV)
                st.emit("sync", lambda e, d=dram, nn=n: e.dma_start(
                    out=vr[0:1, 0:nn], in_=AP(d, 0, [[nn, 1], [1, nn]])), sLD, 16)
                st.wait("vector", sLD)
                st.emit("vector", lambda e, dd=dst, nn=n: e.tensor_copy(
                    dd[0:1, 0:nn], vr[0:1, 0:nn]), sDV)
            st.wait("gpsimd", sDV)
            st.flush(nc)

    def p2(self):
        nc = self.nc
        sPE, sDV, sAC, sCC = self.sPE, self.sDV, self.sAC, self.sCC
        sWL = self.sWL
        sKV = [self.sKVa, self.sKVb]
        sQB = [self.sQBa, self.sQBb]
        sOT = [self.sOTa, self.sOTb]
        st = Streams()
        with ExitStack() as p2:
            def rsb(name, shape, dt=BF16):
                return p2.enter_context(nc.sbuf_tensor(name, list(shape), dt, side="right"))

            qbuf = [rsb("qbuf%d" % i, [128, SW]) for i in range(2)]
            Kbuf = [rsb("Kbuf%d" % i, [128, 1024]) for i in range(2)]
            Vbuf = [rsb("Vbuf%d" % i, [128, 1024]) for i in range(2)]
            pt = [rsb("pt%d" % i, [128, 1024]) for i in range(2)]
            ot_sb = [rsb("ot%d" % i, [128, SW]) for i in range(2)]
            recip = p2.enter_context(nc.sbuf_tensor("recip", [1, SW], F32, side="right"))
            recip_bc = p2.enter_context(nc.sbuf_tensor("recip_bc", [128, SW], F32, side="right"))
            nrm4 = p2.enter_context(nc.sbuf_tensor("nrm4", [1, 64], BF16, side="right"))
            nrm4f = p2.enter_context(nc.sbuf_tensor("nrm4f", [1, 64], F32, side="right"))
            mkrun = p2.enter_context(nc.sbuf_tensor("mkrun", [1, H], F32, side="right"))
            prodr = p2.enter_context(nc.sbuf_tensor("prodr", [1, H], F32, side="right"))
            psS = [p2.enter_context(nc.psum_tensor("p2s%d" % i, [128, 1024], F32))
                   for i in range(2)]
            psO = [p2.enter_context(nc.psum_tensor("p2o%d" % i, [128, SW], F32))
                   for i in range(2)]
            psR = p2.enter_context(nc.psum_tensor("p2r", [1, SW], F32))
            psB = p2.enter_context(nc.psum_tensor("p2b", [128, SW], F32))

            for sp, tgt, cols in ((self.wo_sp, self.WoT, NCD * D),
                                  (self.wffn_sp, self.WffnT, NCD * D),
                                  (self.acat_sp, self.AcatT, NCD * ER),
                                  (self.gate_sp, self.gateT, NCD * NE),
                                  (self.bcat_sp, self.BcatT, D)):
                st.emit("gpsimd", lambda e, s=sp, t=tgt, cc=cols: e.dma_start(
                    out=t[0:128, 0:cc], in_=AP(s, 0, [[cc, 128], [1, cc]])), sWL, 16)

            self._p2_send = [None]
            pe_qfree = [0, 0]
            pe_kvfree = [0, 0]
            ac_psfree = [0, 0]
            pe_ptfree = [0, 0]
            dv_osfree = [0, 0]
            ac_rfree = 0
            self._p2_dvb = [0]

            for kcls in range(1, QNT + 1):
                st.wait("gpsimd", sCC, kcls)
                st.emit("gpsimd", lambda e, k=kcls: e.dma_start(
                    out=nrm4[0:1, 0:64],
                    in_=AP(self.gathered, (k - 1) * 4 * SBLK + NOFFs,
                           [[64, 1], [SBLK, 4], [1, 16]])), self.sGP, 16)
                st.wait("vector", self.sGP)
                st.emit("vector", lambda e: e.tensor_copy(nrm4f[:], nrm4[:]), sDV)
                st.wait("vector", sDV)
                st.emit("vector", lambda e: e.tensor_reduce(
                    prodr[0:1, 0:H], AP(nrm4f, 0, [[64, 1], [1, 16], [16, 4]]),
                    axis=AX.X, op=ALU.max), sDV)
                st.wait("vector", sDV)
                if kcls == 1:
                    st.emit("vector", lambda e: e.tensor_copy(mkrun[:], prodr[0:1, 0:H]), sDV)
                else:
                    st.emit("vector", lambda e: e.tensor_tensor(
                        mkrun[:], mkrun[:], prodr[0:1, 0:H], op=ALU.max), sDV)
                st.wait("vector", sDV)
                st.emit("vector", lambda e, k=kcls: e.tensor_tensor(
                    prodr[0:1, 0:H],
                    AP(self.maxq2, k - 1, [[H * QNT, 1], [QNT, H]]),
                    mkrun[0:1, 0:H], op=ALU.mult), sDV)
                st.wait("scalar", sDV)
                st.emit("scalar", lambda e: e.activation(out=prodr[:], in_=prodr[:],
                                                         func=AF.Ln, bias=self.epsb[0:1, :],
                                                         scale=1.0), sAC)
                st.wait("scalar", sAC)
                st.emit("scalar", lambda e: e.activation(out=prodr[:], in_=prodr[:],
                                                         func=AF.Exp, bias=0.0,
                                                         scale=0.5), sAC)
                st.wait("vector", sAC)
                st.emit("vector", lambda e: e.tensor_scalar(
                    prodr[:], prodr[:], ATT_SCALE, 80.0, op0=ALU.mult, op1=ALU.min), sDV)
                st.wait("vector", sDV)
                st.emit("vector", lambda e: e.tensor_scalar(
                    prodr[:], prodr[:], -1.0, None, op0=ALU.mult), sDV)
                if self._p2_send[0] is not None:
                    self._p2_send[0]()
                    self._p2_send[0] = None
                st.wait("tensor", sDV)
                st.wait("tensor", sDV, self._p2_dvb[0])
                st.emit("tensor", lambda e: e.matmul(
                    psB[0:128, 0:H], self.onesrow_f[0:1, :], prodr[0:1, 0:H],
                    start=True, stop=True, skip_group_check=True), sPE)
                st.wait("vector", sPE)
                st.emit("vector", lambda e, k=kcls: e.tensor_copy(
                    self.negm_bc[0:128, (k - 1) * H:k * H], psB[0:128, 0:H]), sDV)
                self._p2_dvb[0] = self.sDV.n
                st.wait("tensor", sDV)

                for h in range(H):
                    sidx = (kcls - 1) * H + h
                    psw = sidx % 2
                    nblk = 2 * kcls
                    npair = 4 * nblk

                    st.wait("sync", sPE, pe_qfree[psw])
                    st.emit("sync", lambda e, h=h, k=kcls, b=qbuf[psw]: e.dma_start(
                        out=b[:], in_=AP(self.q_spill, (h * QNT + k - 1) * DH * SW,
                                         [[SW, 128], [1, SW]])), sQB[psw], 16)
                    qld = sQB[psw].n

                    kvld = [0] * nblk

                    def emit_load(blk, h=h):
                        pb = blk % 2
                        strip, rank0 = blk // 2, (2 * blk) % 4
                        base = strip * 4 * SBLK + rank0 * SBLK
                        st.wait("sync", sPE, pe_kvfree[pb])
                        st.emit("sync", lambda e, ba=base, b=Kbuf[pb]: e.dma_start(
                            out=AP(b, 0, [[1024, 128], [512, 2], [1, 512]]),
                            in_=AP(self.gathered, ba + KOFFs + h * DH * SW,
                                   [[SW, 128], [SBLK, 2], [1, 512]])), sKV[pb], 16)
                        for pc in range(2):
                            st.emit("sync", lambda e, ba=base, b=Vbuf[pb], pc=pc:
                                    e.dma_start(
                                        out=AP(b, pc * 512,
                                               [[1024, 128], [128, 4], [1, 128]]),
                                        in_=AP(self.gathered,
                                               ba + pc * SBLK + VOFFs + h * SW * DH,
                                               [[DH, 128], [128 * DH, 4], [1, DH]])),
                                    sKV[pb], 16)
                        kvld[blk] = sKV[pb].n

                    emit_load(0)
                    if nblk > 1:
                        emit_load(1)

                    pend = [None]

                    def flush_rpv():
                        if pend[0] is not None:
                            pend[0]()
                            pend[0] = None

                    def flush_sweepend():
                        if self._p2_send[0] is not None:
                            self._p2_send[0]()
                            self._p2_send[0] = None

                    for g in range(npair):
                        blk, gb = g // 4, g % 4
                        pb = blk % 2
                        gp = g % 2
                        masked = (blk >= nblk - 2)
                        first = (g == 0)
                        last = (g == npair - 1)

                        if gb == 0:
                            st.wait("tensor", sKV[pb], kvld[blk])
                        if first:
                            st.wait("tensor", sQB[psw], qld)
                        st.wait("tensor", sAC, ac_psfree[gp])
                        for j in range(2):
                            cc = 2 * gb + j
                            st.emit("tensor", lambda e, pS=psS[gp], Kb=Kbuf[pb],
                                    qb=qbuf[psw], cc=cc, j=j: e.matmul(
                                        pS[0:128, j * SW:(j + 1) * SW],
                                        Kb[0:128, cc * 128:cc * 128 + 128], qb[:],
                                        start=True, stop=True, skip_group_check=True),
                                    sPE if j == 1 else None)
                        spe_s = sPE.n

                        st.wait("scalar", sPE, spe_s)
                        st.wait("scalar", sPE, pe_ptfree[gp])
                        st.emit("scalar", lambda e, pS=psS[gp], p=pt[gp], sidx=sidx:
                                e.activation(out=p[:], in_=pS[0:128, 0:1024], func=AF.Exp,
                                             bias=self.negm_bc[0:128, sidx:sidx + 1],
                                             scale=ATT_SCALE), sAC)
                        ac_psfree[gp] = sAC.n
                        ac_exp = sAC.n

                        dv_mask = None
                        if masked:
                            jp = g - (npair - 8)
                            st.wait("vector", sAC, ac_exp)
                            st.emit("vector", lambda e, p=pt[gp], jp=jp: e.tensor_tensor(
                                p[:], p[:], self.mask[0:128, jp * 1024:(jp + 1) * 1024],
                                op=ALU.mult), sDV)
                            dv_mask = sDV.n

                        flush_rpv()
                        if g == 0:
                            flush_sweepend()

                        def mk_rpv(g=g, gp=gp, pb=pb, ac_exp=ac_exp, dv_mask=dv_mask,
                                   first=first, last=last, psw=psw):
                            def emit():
                                if dv_mask is not None:
                                    st.wait("tensor", sDV, dv_mask)
                                else:
                                    st.wait("tensor", sAC, ac_exp)
                                if first:
                                    st.wait("tensor", sDV, dv_osfree[psw])
                                    st.wait("tensor", sAC, ac_rfree)
                                for j in range(2):
                                    cc = 2 * (g % 4) + j
                                    st.emit("tensor", lambda e, p=pt[gp], j=j: e.matmul(
                                        psR[0:1, :], self.ones_bf[:],
                                        p[0:128, j * SW:(j + 1) * SW],
                                        start=(first and j == 0), stop=(last and j == 1),
                                        skip_group_check=True), None)
                                    st.emit("tensor", lambda e, p=pt[gp], Vb=Vbuf[pb],
                                            cc=cc, j=j: e.matmul(
                                                psO[psw][:],
                                                Vb[0:128, cc * 128:cc * 128 + 128],
                                                p[0:128, j * SW:(j + 1) * SW],
                                                start=(first and j == 0),
                                                stop=(last and j == 1),
                                                skip_group_check=True),
                                            sPE if j == 1 else None)
                                pe_ptfree[gp] = sPE.n
                                if g % 4 == 3 or last:
                                    pe_kvfree[pb] = sPE.n
                                if last:
                                    pe_qfree[psw] = sPE.n
                            return emit

                        pend[0] = mk_rpv()
                        if g >= 1 and (g - 1) % 4 == 3:
                            nxt = (g - 1) // 4 + 2
                            if nxt < nblk:
                                emit_load(nxt)
                    flush_rpv()
                    spe_sweep = sPE.n

                    st.wait("scalar", sPE, spe_sweep)
                    st.emit("scalar", lambda e: e.activation(
                        out=recip[0:1, :], in_=psR[0:1, :], func=AF.Ln,
                        bias=0.0, scale=1.0), sAC)
                    ac_rfree = sAC.n
                    st.wait("scalar", sAC)
                    st.emit("scalar", lambda e: e.activation(
                        out=recip[0:1, :], in_=recip[0:1, :], func=AF.Exp,
                        bias=0.0, scale=-1.0), sAC)
                    ac_rec = sAC.n

                    def mk_sweepend(h=h, kcls=kcls, psw=psw, ac_rec=ac_rec):
                        def emit():
                            nonlocal_dv = None
                            st.wait("tensor", sAC, ac_rec)
                            st.wait("tensor", sDV, self._p2_dvb[0])
                            st.emit("tensor", lambda e: e.matmul(
                                psB[:], self.onesrow_f[0:1, :], recip[0:1, :],
                                start=True, stop=True, skip_group_check=True), sPE)
                            spe_bc = sPE.n
                            st.wait("vector", sPE, spe_bc)
                            st.emit("vector", lambda e: e.tensor_copy(
                                recip_bc[:], psB[:]), sDV)
                            self._p2_dvb[0] = self.sDV.n
                            st.wait("vector", sDV, self._p2_dvb[0])
                            st.wait("vector", sOT[psw], sOT[psw].n)
                            st.emit("vector", lambda e, pO=psO[psw], o=ot_sb[psw]:
                                    e.tensor_tensor(o[:], pO[:], recip_bc[:],
                                                    op=ALU.mult), sDV)
                            dv_osfree[psw] = self.sDV.n
                            st.wait("gpsimd", sDV, dv_osfree[psw])
                            st.emit("gpsimd", lambda e, h=h, k=kcls, o=ot_sb[psw]:
                                    e.dma_start(
                                        out=AP(self.ot_spill, (h * QNT + k - 1) * DH * SW,
                                               [[SW, 128], [1, SW]]),
                                        in_=o[:]), sOT[psw], 16)
                        return emit

                    self._p2_send[0] = mk_sweepend()

            if self._p2_send[0] is not None:
                self._p2_send[0]()
                self._p2_send[0] = None
            st.wait("gpsimd", sOT[0], sOT[0].n)
            st.wait("gpsimd", sOT[1], sOT[1].n)
            st.wait("gpsimd", sWL, sWL.n)
            st.flush(nc)

    def p3(self):
        nc = self.nc
        sPE, sDV, sAC, sST = self.sPE, self.sDV, self.sAC, self.sST
        sOL = [self.sOLa, self.sOLb]
        sXT = self.sXT
        st = Streams()
        with ExitStack() as p3:
            def rsb(name, shape, dt=F32):
                return p3.enter_context(nc.sbuf_tensor(name, list(shape), dt, side="right"))

            otb = [rsb("otb%d" % i, [128, NCD * 128], BF16) for i in range(2)]
            xt = rsb("xt", [128, D])
            ysb = rsb("ysb", [128, D])
            x1f = rsb("x1f", [128, D])
            x1b = rsb("x1b", [128, D], BF16)
            x1T = rsb("x1T", [128, NCD * 128], BF16)
            xaw = rsb("xaw", [128, ER], BF16)
            xawT = rsb("xawT", [128, ER], BF16)
            stat = rsb("stat", [128, 24])
            mv = rsb("mv", [128, 2])
            rln = rsb("rln", [128, 1])
            eq1 = rsb("eq1", [128, 8])
            l2t = rsb("l2t", [128, 8])
            w8t = rsb("w8t", [128, 8])
            t1 = rsb("t1", [128, 1])
            t2 = rsb("t2", [128, 1])
            te = rsb("te", [128, 1])
            tg1 = rsb("tg1", [128, 1])
            tg2 = rsb("tg2", [128, 1])
            pBig = p3.enter_context(nc.psum_tensor("p3big", [128, D], F32))
            psG = p3.enter_context(nc.psum_tensor("p3g", [128, ER], F32))
            psL = p3.enter_context(nc.psum_tensor("p3l", [128, NE], F32))
            psT = [p3.enter_context(nc.psum_tensor("p3t%d" % i, [128, 1024], BF16))
                   for i in range(2)]

            pe_otfree = [0, 0]
            dv_pbig = 0
            dv_xt = [0]
            xtlds, olds = {}, {}

            def emit_loads(i):
                par = i % 2
                st.wait("sync", sPE, pe_otfree[par])
                for h in range(H):
                    st.emit("sync", lambda e, h=h, i=i, b=otb[par]: e.dma_start(
                        out=b[0:128, h * 128:h * 128 + 128],
                        in_=AP(self.ot_spill, (h * QNT + i // 4) * DH * SW + (i % 4) * 128,
                               [[SW, 128], [1, 128]])), sOL[par], 16)
                olds[i] = sOL[par].n
                st.wait("sync", sDV, dv_xt[0])
                st.emit("sync", lambda e, i=i: e.dma_start(
                    out=xt[:], in_=AP(self.x_own, i * 128 * D, [[D, 128], [1, D]])),
                    sXT, 16)
                xtlds[i] = self.sXT.n

            st.wait("tensor", self.sWL, self.sWL.n)
            emit_loads(0)
            for i in range(NT * 2):
                par = i % 2
                xtld = xtlds[i]

                st.wait("tensor", sOL[par], olds[i])
                st.wait("tensor", sDV, dv_pbig)
                for cd in range(NCD):
                    for vs in range(4):
                        st.emit("tensor", lambda e, cd=cd, vs=vs, b=otb[par]: e.matmul(
                            pBig[0:128, vs * 512:(vs + 1) * 512],
                            b[0:128, cd * 128:cd * 128 + 128],
                            self.WoT[0:128, cd * D + vs * 512:cd * D + vs * 512 + 512],
                            start=(cd == 0), stop=False, skip_group_check=True), None)
                for vs in range(4):
                    st.emit("tensor", lambda e, vs=vs: e.matmul(
                        pBig[0:128, vs * 512:(vs + 1) * 512], self.onesrow_bf[0:1, :],
                        self.bo_r[0:1, vs * 512:(vs + 1) * 512],
                        start=False, stop=True, skip_group_check=True),
                        sPE if vs == 3 else None)
                spe_wo = self.sPE.n
                pe_otfree[par] = spe_wo

                st.wait("vector", sPE, spe_wo)
                st.wait("vector", sXT, xtld)
                st.wait("vector", sST, self.sST.n)
                st.emit("vector", lambda e: e.tensor_tensor(ysb[:], pBig[:], xt[:],
                                                            op=ALU.add), sDV)
                dv_xt[0] = self.sDV.n
                if i + 1 < NT * 2:
                    emit_loads(i + 1)
                st.wait("vector", sDV)
                for cst in range(4):
                    st.emit("vector", lambda e, cst=cst: e.bn_stats(
                        stat[0:128, cst * 6:cst * 6 + 6],
                        ysb[0:128, cst * 512:(cst + 1) * 512]), sDV)
                st.wait("vector", sDV)
                st.emit("vector", lambda e: e.bn_aggr(mv[:], stat[:]), sDV)
                st.wait("scalar", sDV, self.sDV.n)
                st.emit("scalar", lambda e: e.activation(out=rln[:], in_=mv[0:128, 1:2],
                                                         func=AF.Ln, bias=self.epsb[:],
                                                         scale=1.0), sAC)
                st.wait("scalar", sAC)
                st.emit("scalar", lambda e: e.activation(out=rln[:], in_=rln[:],
                                                         func=AF.Exp, bias=0.0,
                                                         scale=-0.5), sAC)
                st.wait("vector", sAC)
                st.emit("vector", lambda e: e.tensor_scalar(
                    x1f[:], ysb[:], mv[0:128, 0:1], rln[:], op0=ALU.subtract,
                    op1=ALU.mult), sDV)
                st.wait("vector", sDV)
                st.emit("vector", lambda e: e.tensor_copy(x1b[:], x1f[:]), sDV)
                dv_x1 = self.sDV.n
                if self.taps:
                    st.wait("gpsimd", sDV, self.sDV.n)
                    st.emit("gpsimd", lambda e, i=i: e.dma_start(
                        out=AP(self.dbg2, i * 128 * D, [[D, 128], [1, D]]),
                        in_=x1b[:]), self.sGP, 16)
                    st.wait("vector", self.sGP, self.sGP.n)

                st.wait("tensor", sDV, dv_x1)
                for bank in range(2):
                    for j in range(8):
                        cd = bank * 8 + j
                        st.emit("tensor", lambda e, cd=cd, j=j, p=psT[bank]: e.transpose(
                            p[0:128, j * 128:j * 128 + 128],
                            x1b[0:128, cd * 128:cd * 128 + 128], self.identb[:]),
                            sPE if j == 7 else None)
                    st.wait("vector", sPE)
                    st.emit("vector", lambda e, bank=bank, p=psT[bank]: e.tensor_copy(
                        x1T[0:128, bank * 1024:(bank + 1) * 1024], p[:]), sDV)
                dv_x1T = self.sDV.n

                st.wait("tensor", sDV, dv_x1T)
                for cd in range(NCD):
                    st.emit("tensor", lambda e, cd=cd: e.matmul(
                        psL[0:128, 0:NE], x1T[0:128, cd * 128:cd * 128 + 128],
                        self.gateT[0:128, cd * NE:(cd + 1) * NE],
                        start=(cd == 0), stop=False, skip_group_check=True), None)
                    st.emit("tensor", lambda e, cd=cd: e.matmul(
                        psG[0:128, 0:ER], x1T[0:128, cd * 128:cd * 128 + 128],
                        self.AcatT[0:128, cd * ER:(cd + 1) * ER],
                        start=(cd == 0), stop=(cd == NCD - 1), skip_group_check=True), None)
                st.emit("tensor", lambda e: e.matmul(
                    psL[0:128, 0:NE], self.onesrow_bf[0:1, :], self.gateb_r[0:1, :],
                    start=False, stop=True, skip_group_check=True), sPE)
                spe_gx = self.sPE.n
                for cd in range(NCD):
                    for vs in range(4):
                        st.emit("tensor", lambda e, cd=cd, vs=vs: e.matmul(
                            pBig[0:128, vs * 512:(vs + 1) * 512],
                            x1T[0:128, cd * 128:cd * 128 + 128],
                            self.WffnT[0:128, cd * D + vs * 512:cd * D + vs * 512 + 512],
                            start=(cd == 0), stop=False, skip_group_check=True), None)
                for vs in range(4):
                    st.emit("tensor", lambda e, vs=vs: e.matmul(
                        pBig[0:128, vs * 512:(vs + 1) * 512], self.onesrow_bf[0:1, :],
                        self.bffn_r[0:1, vs * 512:(vs + 1) * 512],
                        start=False, stop=False, skip_group_check=True), None)

                st.wait("vector", sPE, spe_gx)
                st.emit("vector", lambda e: e.tensor_reduce(
                    t1[:], psL[0:128, 0:NE], axis=AX.X, op=ALU.max), sDV)
                st.wait("vector", sDV)
                st.emit("vector", lambda e: e.tensor_scalar(
                    eq1[:], psL[0:128, 0:NE], t1[:], None, op0=ALU.is_equal), sDV)
                st.wait("vector", sDV)
                st.emit("vector", lambda e: e.scalar_tensor_tensor(
                    l2t[:], eq1[:], -1e30, psL[0:128, 0:NE], op0=ALU.mult,
                    op1=ALU.add), sDV)
                st.wait("vector", sDV)
                st.emit("vector", lambda e: e.tensor_reduce(
                    t2[:], l2t[:], axis=AX.X, op=ALU.max), sDV)
                st.wait("vector", sDV)
                st.emit("vector", lambda e: e.tensor_tensor(
                    te[:], t2[:], t1[:], op=ALU.subtract), sDV)
                st.wait("scalar", sDV, self.sDV.n)
                st.emit("scalar", lambda e: e.activation(out=te[:], in_=te[:], func=AF.Exp,
                                                         bias=0.0, scale=1.0), sAC)
                st.wait("vector", sAC)
                st.emit("vector", lambda e: e.tensor_scalar(
                    tg1[:], te[:], 1.0, None, op0=ALU.add), sDV)
                st.wait("vector", sDV)
                st.emit("vector", lambda e: e.reciprocal(tg1[:], tg1[:]), sDV)
                st.wait("vector", sDV)
                st.emit("vector", lambda e: e.tensor_scalar(
                    tg2[:], tg1[:], -1.0, 1.0, op0=ALU.mult, op1=ALU.add), sDV)
                st.emit("vector", lambda e: e.tensor_scalar(
                    eq1[:], eq1[:], tg1[:], None, op0=ALU.mult), sDV)
                st.emit("vector", lambda e: e.tensor_scalar(
                    l2t[:], l2t[:], t2[:], None, op0=ALU.is_equal), sDV)
                st.wait("vector", sDV)
                st.emit("vector", lambda e: e.scalar_tensor_tensor(
                    w8t[:], l2t[:], tg2[:], eq1[:], op0=ALU.mult, op1=ALU.add), sDV)
                st.wait("vector", sDV)
                st.emit("vector", lambda e: e.tensor_tensor(
                    xaw[:].rearrange("p (e r) -> p e r", r=RLORA),
                    AP(psG, 0, [[ER, 128], [16, NE], [1, RLORA]]),
                    w8t[:].unsqueeze(2).broadcast_to([128, NE, RLORA]),
                    op=ALU.mult), sDV)
                dv_xaw = self.sDV.n

                st.wait("tensor", sDV, dv_xaw)
                st.emit("tensor", lambda e: e.transpose(
                    psT[0][0:128, 0:128], xaw[:], self.identb[:]), sPE)
                st.wait("vector", sPE)
                st.emit("vector", lambda e: e.tensor_copy(
                    xawT[:], psT[0][0:128, 0:128]), sDV)
                st.wait("tensor", sDV)
                for vs in range(4):
                    st.emit("tensor", lambda e, vs=vs: e.matmul(
                        pBig[0:128, vs * 512:(vs + 1) * 512], xawT[:],
                        self.BcatT[0:128, vs * 512:(vs + 1) * 512],
                        start=False, stop=True, skip_group_check=True),
                        sPE if vs == 3 else None)
                spe_ffn = self.sPE.n

                st.wait("vector", sPE, spe_ffn)
                st.emit("vector", lambda e: e.tensor_tensor(ysb[:], pBig[:], x1f[:],
                                                            op=ALU.add), sDV)
                dv_pbig = self.sDV.n
                if self.taps:
                    st.wait("gpsimd", sDV, self.sDV.n)
                    st.emit("gpsimd", lambda e, i=i: e.dma_start(
                        out=AP(self.dbg1, i * 128 * D, [[D, 128], [1, D]]),
                        in_=ysb[:]), self.sGP, 16)
                    st.emit("gpsimd", lambda e, i=i: e.dma_start(
                        out=AP(self.dbg3, i * 128 * ER, [[ER, 128], [1, ER]]),
                        in_=xaw[:]), self.sGP, 16)
                    st.wait("vector", self.sGP, self.sGP.n)
                st.wait("vector", sDV)
                for cst in range(4):
                    st.emit("vector", lambda e, cst=cst: e.bn_stats(
                        stat[0:128, cst * 6:cst * 6 + 6],
                        ysb[0:128, cst * 512:(cst + 1) * 512]), sDV)
                st.wait("vector", sDV)
                st.emit("vector", lambda e: e.bn_aggr(mv[:], stat[:]), sDV)
                st.wait("scalar", sDV, self.sDV.n)
                st.emit("scalar", lambda e: e.activation(out=rln[:], in_=mv[0:128, 1:2],
                                                         func=AF.Ln, bias=self.epsb[:],
                                                         scale=1.0), sAC)
                st.wait("scalar", sAC)
                st.emit("scalar", lambda e: e.activation(out=rln[:], in_=rln[:],
                                                         func=AF.Exp, bias=0.0,
                                                         scale=-0.5), sAC)
                st.wait("vector", sAC)
                st.emit("vector", lambda e: e.tensor_scalar(
                    ysb[:], ysb[:], mv[0:128, 0:1], rln[:], op0=ALU.subtract,
                    op1=ALU.mult), sDV)
                st.wait("sync", sDV)
                st.emit("sync", lambda e, i=i: e.dma_start(
                    out=AP(self.out, i * 128 * D, [[D, 128], [1, D]]), in_=ysb[:]),
                    sST, 16)

            st.wait("sync", sST)
            st.wait("gpsimd", sST, self.sST.n)
            st.flush(nc)


def _rotary_tables():
    j = np.arange(32, dtype=np.float64)
    f = (1.0 / 10000.0) ** ((2.0 * j) / 64.0)
    h = np.arange(H, dtype=np.float64)
    ang = h[:, None] * f[None, :]
    c, s = np.cos(ang), np.sin(ang)
    C = np.ones(D, dtype=np.float64)
    S = np.zeros(D, dtype=np.float64)
    for hh in range(H):
        C[128 * hh + 64:128 * hh + 96] = c[hh]
        C[128 * hh + 96:128 * hh + 128] = c[hh]
        S[128 * hh + 64:128 * hh + 96] = s[hh]
        S[128 * hh + 96:128 * hh + 128] = -s[hh]
    return (np.ascontiguousarray(np.broadcast_to(C.astype(np.float32), (128, D))),
            np.ascontiguousarray(np.broadcast_to(S.astype(np.float32), (128, D))))


def make_in_maps(inputs):
    C, S = _rotary_tables()
    x = np.asarray(inputs["L0_attn_x"], dtype=np.float32)
    f32c = lambda k: np.ascontiguousarray(np.asarray(inputs[k], dtype=np.float32))
    shared = {
        "w_dq": f32c("L0_attn_W_d_q"), "b_dq": f32c("L0_attn_b_d_q"),
        "qng": f32c("L0_attn_q_norm_g"), "qnb": f32c("L0_attn_q_norm_b"),
        "w_uq": f32c("L0_attn_W_u_q"), "b_uq": f32c("L0_attn_b_u_q"),
        "w_dkv": f32c("L0_attn_W_d_kv"), "b_dkv": f32c("L0_attn_b_d_kv"),
        "kvg": f32c("L0_attn_kv_norm_g"), "kvb": f32c("L0_attn_kv_norm_b"),
        "w_ukv": f32c("L0_attn_W_u_kv"), "b_ukv": f32c("L0_attn_b_u_kv"),
        "w_o": f32c("L0_attn_W_o"), "b_o": f32c("L0_attn_b_o"),
        "w_ffn": f32c("L0_ffn_W"), "b_ffn": f32c("L0_ffn_b"),
        "lora_a": f32c("L0_ffn_lora_A"), "lora_b": f32c("L0_ffn_lora_B"),
        "gate_w": f32c("L0_ffn_gate_w"), "gate_b": f32c("L0_ffn_gate_b"),
        "cosrep": C, "sinrep": S,
    }
    in_maps = []
    for c in range(8):
        b, rho = c // 4, c % 4
        tiles = [x[b, 512 * (4 * j + rho):512 * (4 * j + rho) + 512] for j in range(QNT)]
        m = dict(shared)
        m["x_own"] = np.ascontiguousarray(np.concatenate(tiles, axis=0))
        m["mthr"] = np.full((128, 1), -512.0 * rho, dtype=np.float32)
        in_maps.append(m)
    return in_maps


def assemble_out(results):
    y = np.empty((B, T, D), dtype=np.float32)
    for c in range(8):
        b, rho = c // 4, c % 4
        o = np.asarray(results[c]["out"])
        for j in range(QNT):
            y[b, 512 * (4 * j + rho):512 * (4 * j + rho) + 512] = o[512 * j:512 * j + 512]
    return y


def kernel(**inputs):
    mk = MK()
    nc = mk.build()
    in_maps = make_in_maps(inputs)
    res = run_bass_kernel_spmd(nc, in_maps, core_ids=list(range(8)))
    return assemble_out(res.results)


# version 12
# speedup vs baseline: 1.3570x; 1.0642x over previous
import sys
if "/opt/trn_rl_repo" not in sys.path:
    sys.path.insert(0, "/opt/trn_rl_repo")

from contextlib import ExitStack
import math
import numpy as np

import concourse.bass as bass
import concourse.mybir as mybir
from concourse.bass_utils import run_bass_kernel_spmd

F32 = mybir.dt.float32
BF16 = mybir.dt.bfloat16
AF = mybir.ActivationFunctionType
ALU = mybir.AluOpType
AX = mybir.AxisListType

B, T, D, H, DH = 2, 8192, 2048, 16, 128
DQ, DKV, DKR = 682, 1024, 64
DCKV = DKV + DKR
NCQ = 6
CQ_TAIL = DQ - 5 * 128
NCKV = 8
NCD = 16
TCORE = 2048
NT = 8
NSTRIP = 4
SW = 512
QG = 256
ATT_SCALE = 1.0 / math.sqrt(DH)
EPS = 1e-5
NE, RLORA = 8, 16
ER = NE * RLORA

QNT = 4
K_SBLK = H * DH * SW
V_SBLK = H * SW * DH
SBLK = K_SBLK + V_SBLK + 1024
KOFFs, VOFFs, NOFFs = 0, K_SBLK, K_SBLK + V_SBLK
CONTRIB = NSTRIP * SBLK


def AP(t, off, dims):
    return bass.AP(t, off, [list(d) for d in dims])


class Sem:
    __slots__ = ("h", "n")

    def __init__(self, h):
        self.h = h
        self.n = 0


class Streams:
    NAMES = ("sync", "tensor", "vector", "scalar", "gpsimd")

    def __init__(self):
        self.s = {n: [] for n in self.NAMES}

    def emit(self, eng, fn, sem=None, k=1):
        if sem is None:
            self.s[eng].append(fn)
            return None
        sem.n += k
        self.s[eng].append(lambda e, f=fn, sh=sem.h, kk=k: f(e).then_inc(sh, kk))
        return sem.n

    def wait(self, eng, sem, val=None):
        v = sem.n if val is None else val
        if v > 0:
            self.s[eng].append(lambda e, sh=sem.h, vv=v: e.wait_ge(sh, vv))

    def flush(self, nc):
        with nc.Block() as block:
            for n in self.NAMES:
                ops = self.s[n]
                if ops:
                    getattr(block, n)(lambda e, ops=ops: [f(e) for f in ops])
        self.s = {n: [] for n in self.NAMES}


class MK:
    def __init__(self, taps=(), stop_after=None):
        self.taps = list(taps)
        self.stop_after = stop_after
        self.nc = bass.Bass("TRN2", target_bir_lowering=False)

    def build(self):
        nc = self.nc

        def din(name, shape):
            return nc.dram_tensor(name, list(shape), F32, kind="ExternalInput")

        self.x_own = din("x_own", [TCORE, D])
        self.w_dq = din("w_dq", [DQ, D])
        self.b_dq = din("b_dq", [DQ])
        self.qng = din("qng", [DQ])
        self.qnb = din("qnb", [DQ])
        self.w_uq = din("w_uq", [D, DQ])
        self.b_uq = din("b_uq", [D])
        self.w_dkv = din("w_dkv", [DCKV, D])
        self.b_dkv = din("b_dkv", [DCKV])
        self.kvg = din("kvg", [DKV])
        self.kvb = din("kvb", [DKV])
        self.w_ukv = din("w_ukv", [DKV + D, DKV])
        self.b_ukv = din("b_ukv", [DKV + D])
        self.w_o = din("w_o", [D, D])
        self.b_o = din("b_o", [D])
        self.w_ffn = din("w_ffn", [D, D])
        self.b_ffn = din("b_ffn", [D])
        self.lora_a = din("lora_a", [NE, RLORA, D])
        self.lora_b = din("lora_b", [NE, D, RLORA])
        self.gate_w = din("gate_w", [NE, D])
        self.gate_b = din("gate_b", [NE])
        self.cosrep = din("cosrep", [128, D])
        self.sinrep = din("sinrep", [128, D])
        self.mthr = din("mthr", [128, 1])

        self.out = nc.dram_tensor("out", [TCORE, D], F32, kind="ExternalOutput")

        self.contrib = nc.dram_tensor("contrib", [CONTRIB], BF16)
        self.gathered = nc.dram_tensor("gathered", [4 * CONTRIB], BF16)
        self.q_spill = nc.dram_tensor("q_spill", [H * QNT * DH * SW], BF16)
        self.ot_spill = nc.dram_tensor("ot_spill", [H * QNT * DH * SW], BF16)
        self.wo_sp = nc.dram_tensor("wo_sp", [D * D], BF16)
        self.wffn_sp = nc.dram_tensor("wffn_sp", [D * D], BF16)
        self.acat_sp = nc.dram_tensor("acat_sp", [ER * D], BF16)
        self.bcat_sp = nc.dram_tensor("bcat_sp", [ER * D], BF16)
        self.gate_sp = nc.dram_tensor("gate_sp", [128 * NCD * NE], BF16)
        self.dbg1 = nc.dram_tensor("dbg1", [TCORE * D], BF16)
        self.dbg2 = nc.dram_tensor("dbg2", [TCORE * D], BF16)
        self.dbg3 = nc.dram_tensor("dbg3", [TCORE * ER], BF16)

        es = ExitStack()
        self.es = es

        def sb(name, shape, dt=F32, side="left"):
            return es.enter_context(nc.sbuf_tensor(name, list(shape), dt, side=side))
        self.sb = sb

        with es:
            self.identb = sb("identb", [128, 128], BF16)
            self.identf = sb("identf", [128, 128], F32)
            self.ones_bf = sb("ones_bf", [128, 1], BF16)
            self.onesA = sb("onesA", [128, 1], BF16)
            self.onesB = sb("onesB", [128, 1], BF16)
            self.onesrow_bf = sb("onesrow_bf", [1, 128], BF16)
            self.onesrow_f = sb("onesrow_f", [1, 128], F32)
            self.epsb = sb("epsb", [128, 1], F32)
            self.bdq_c = sb("bdq_c", [128, NCQ], F32)
            self.buq_c = sb("buq_c", [128, NCD], F32)
            self.bdkv_c = sb("bdkv_c", [128, 9], F32)
            self.buk_c = sb("buk_c", [128, NCKV], F32)
            self.qng_c = sb("qng_c", [128, NCQ], F32)
            self.qnb_c = sb("qnb_c", [128, NCQ], F32)
            self.kvg_c = sb("kvg_c", [128, NCKV], F32)
            self.kvb_c = sb("kvb_c", [128, NCKV], F32)
            self.bukvv_r = sb("bukvv_r", [1, D], BF16)
            self.maxq2 = sb("maxq2", [1, H * QNT], F32)
            self.maxk2 = sb("maxk2", [1, H], F32)
            self.negm_bc = sb("negm_bc", [128, QNT * H], F32)

            for nm in ("sLD", "sGP", "sGE", "sPE", "sDV", "sAC", "sST", "sCC",
                       "sLDa", "sLDb", "sLDc", "sQTa", "sQTb", "sKNa", "sKNb", "sKR",
                       "sXa", "sXb", "sKVa", "sKVb", "sQBa", "sQBb", "sOTa", "sOTb",
                       "sWL", "sOLa", "sOLb", "sXT"):
                setattr(self, nm, Sem(es.enter_context(nc.semaphore(nm))))

            cs = ExitStack()
            self.cosr = cs.enter_context(nc.sbuf_tensor("cosr", [128, D], F32, side="right"))
            self.sinr = cs.enter_context(nc.sbuf_tensor("sinr", [128, D], F32, side="right"))
            with nc.named_scope('p0c'):
                self.p0_consts()
            wa = ExitStack()
            self.WdqT = wa.enter_context(nc.sbuf_tensor("WdqT", [128, NCD * DQ], BF16))
            self.WuqT = wa.enter_context(nc.sbuf_tensor("WuqT", [128, NCQ * D], BF16))
            self.WdkvT = wa.enter_context(nc.sbuf_tensor("WdkvT", [128, NCD * DCKV], BF16))
            self.WukT = wa.enter_context(nc.sbuf_tensor("WukT", [128, NCKV * DKV], BF16))
            self.WuvT = wa.enter_context(nc.sbuf_tensor("WuvT", [128, NCKV * D], BF16))
            with nc.named_scope('p0w'):
                self.p0_weights()
            cs.close()
            if self.stop_after == "p0":
                wa.close()
                self._emit_taps()
                return nc
            with nc.named_scope('p1'):
                self.p1()
            wa.close()
            self.mask = sb("mask", [128, 16 * SW], BF16)
            self.bo_r = sb("bo_r", [1, D], BF16)
            self.bffn_r = sb("bffn_r", [1, D], BF16)
            self.gateb_r = sb("gateb_r", [1, NE], BF16)
            self.WoT = sb("WoT", [128, NCD * D], BF16)
            self.WffnT = sb("WffnT", [128, NCD * D], BF16)
            self.AcatT = sb("AcatT", [128, NCD * ER], BF16)
            self.gateT = sb("gateT", [128, NCD * NE], BF16)
            self.BcatT = sb("BcatT", [128, D], BF16)
            with nc.named_scope('ag'):
                self.ag()
            if self.stop_after == "p1":
                self._emit_taps()
                return nc
            with nc.named_scope('p2'):
                self.p2()
            if self.stop_after == "p2":
                self._emit_taps()
                return nc
            with nc.named_scope('p3'):
                self.p3()
            self._emit_taps()
        return nc

    def _emit_taps(self):
        if not self.taps:
            return
        st = Streams()
        tmap = {"contrib": (self.contrib, CONTRIB),
                "gathered": (self.gathered, 4 * CONTRIB),
                "q_spill": (self.q_spill, H * NT * DH * QG),
                "ot_spill": (self.ot_spill, H * NT * DH * QG),
                "wo_sp": (self.wo_sp, D * D),
                "wffn_sp": (self.wffn_sp, D * D),
                "acat_sp": (self.acat_sp, ER * D),
                "bcat_sp": (self.bcat_sp, ER * D),
                "gate_sp": (self.gate_sp, 128 * NCD * NE),
                "dbg1": (self.dbg1, TCORE * D), "dbg2": (self.dbg2, TCORE * D),
                "dbg3": (self.dbg3, TCORE * ER)}
        for t in self.taps:
            th, size = tmap[t]
            tout = self.nc.dram_tensor("tap_" + t, [size], BF16, kind="ExternalOutput")
            per = size // 128
            st.emit("sync", lambda e, a=tout, b=th, p=per: e.dma_start(
                out=AP(a, 0, [[p, 128], [1, p]]), in_=AP(b, 0, [[p, 128], [1, p]])),
                self.sST, 16)
        st.wait("sync", self.sST)
        st.flush(self.nc)

    def p0_consts(self):
        nc = self.nc
        sLD, sGP, sGE, sDV, sAC = self.sLD, self.sGP, self.sGE, self.sDV, self.sAC
        st = Streams()
        with ExitStack() as p0:
            vrow = p0.enter_context(nc.sbuf_tensor("p0vrow", [1, DKV + D], F32, side="right"))
            vrow2 = p0.enter_context(nc.sbuf_tensor("p0vrow2", [1, D], F32, side="right"))
            rscr = p0.enter_context(nc.sbuf_tensor("p0rscr", [1, D], F32, side="right"))

            st.emit("gpsimd", lambda e: e.iota(self.identf[:], [[1, 128]], channel_multiplier=-1,
                                               allow_small_or_imprecise_dtypes=True), sGE)
            st.emit("gpsimd", lambda e: e.memset(self.ones_bf[:], 1.0), sGE)
            st.emit("gpsimd", lambda e: e.memset(self.onesrow_bf[:], 1.0), sGE)
            st.emit("gpsimd", lambda e: e.memset(self.onesrow_f[:], 1.0), sGE)
            st.emit("gpsimd", lambda e: e.memset(self.epsb[:], EPS), sGE)
            st.emit("gpsimd", lambda e: e.iota(self.onesA[:], [[0, 1]], channel_multiplier=1,
                                               allow_small_or_imprecise_dtypes=True), sGE)
            st.emit("sync", lambda e: e.dma_start(out=self.cosr[:], in_=self.cosrep.ap()), sLD, 16)
            st.emit("sync", lambda e: e.dma_start(out=self.sinr[:], in_=self.sinrep.ap()), sLD, 16)
            st.wait("vector", sGE)
            st.wait("vector", sLD)
            st.emit("vector", lambda e: e.tensor_scalar(self.identf[:], self.identf[:], 0.0, None,
                                                        op0=ALU.is_equal), sDV)
            st.emit("vector", lambda e: e.tensor_scalar(self.onesB[:], self.onesA[:], 63.5, None,
                                                        op0=ALU.is_gt), sDV)
            st.wait("vector", sDV)
            st.emit("vector", lambda e: e.tensor_scalar(self.onesA[:], self.onesA[:], 63.5, None,
                                                        op0=ALU.is_lt), sDV)
            st.wait("vector", sDV)
            st.emit("vector", lambda e: e.tensor_copy(self.identb[:], self.identf[:]), sDV)

            def vec_load(dram, n):
                st.wait("sync", sGP)
                st.wait("sync", sDV)
                st.emit("sync", lambda e, d=dram, nn=n: e.dma_start(
                    out=vrow[0:1, 0:nn], in_=AP(d, 0, [[nn, 1], [1, nn]])), sLD, 16)

            def col_store(dst, n, nch, src=None, src_off=0):
                s = vrow if src is None else src
                st.wait("gpsimd", sLD)
                st.wait("gpsimd", sDV)
                st.wait("gpsimd", self.sGE)
                for c in range(nch):
                    w = min(128, n - c * 128)
                    st.emit("gpsimd", lambda e, dd=dst, cc=c, ww=w, so=src_off, ss=s: e.dma_start(
                        out=dd[0:ww, cc:cc + 1],
                        in_=ss[0:1, so + cc * 128:so + cc * 128 + ww]), sGP, 16)

            for dram, dst in ((self.b_dq, self.bdq_c), (self.qng, self.qng_c),
                              (self.qnb, self.qnb_c)):
                st.emit("gpsimd", lambda e, dd=dst: e.memset(dd[:], 0.0), sGE)
                vec_load(dram, DQ)
                col_store(dst, DQ, NCQ)
            st.emit("gpsimd", lambda e: e.memset(self.bdkv_c[:], 0.0), sGE)
            vec_load(self.b_dkv, DCKV)
            col_store(self.bdkv_c, DCKV, 9)
            for dram, dst in ((self.kvg, self.kvg_c), (self.kvb, self.kvb_c)):
                vec_load(dram, DKV)
                col_store(dst, DKV, NCKV)
            vec_load(self.b_ukv, DKV + D)
            st.wait("gpsimd", sLD)
            for co in range(NCKV):
                st.emit("gpsimd", lambda e, cc=co: e.dma_start(
                    out=self.buk_c[0:64, cc:cc + 1],
                    in_=vrow[0:1, 384 * cc:384 * cc + 64]), sGP, 16)
                st.emit("gpsimd", lambda e, cc=co: e.dma_start(
                    out=self.buk_c[64:128, cc:cc + 1],
                    in_=vrow[0:1, 384 * cc + 192:384 * cc + 192 + 64]), sGP, 16)
            st.wait("vector", sLD)
            st.emit("vector", lambda e: e.tensor_copy(
                self.bukvv_r[:], AP(vrow, 64, [[DKV + D, 1], [192, H], [1, 128]])), sDV)

            rc = lambda t, w, p=1: AP(t, 64, [[w, p], [128, H], [1, 64]])
            rp = lambda t, w, p=1: AP(t, 96, [[w, p], [128, H], [-32, 2], [1, 32]])
            vec_load(self.b_uq, D)
            st.wait("vector", sLD)
            st.emit("vector", lambda e: e.tensor_copy(vrow2[0:1, :], vrow[0:1, 0:D]), sDV)
            st.wait("vector", sDV)
            st.emit("vector", lambda e: e.tensor_tensor(
                rc(vrow2, D), rc(vrow, DKV + D), rc(self.cosr, D), op=ALU.mult), sDV)
            st.emit("vector", lambda e: e.tensor_tensor(
                rc(rscr, D), rp(vrow, DKV + D), rc(self.sinr, D), op=ALU.mult), sDV)
            st.wait("vector", sDV)
            st.emit("vector", lambda e: e.tensor_tensor(
                rc(vrow2, D), rc(vrow2, D), rc(rscr, D), op=ALU.add), sDV)
            col_store(self.buq_c, D, NCD, src=vrow2)
            st.wait("sync", sGP)
            st.flush(nc)

    def p0_weights(self):
        nc = self.nc
        sLD, sGP, sPE, sDV = self.sLD, self.sGP, self.sPE, self.sDV
        st = Streams()
        with ExitStack() as p0:
            stage = [p0.enter_context(nc.sbuf_tensor("p0st%d" % i, [128, D], F32, side="right"))
                     for i in range(3)]
            rotc = p0.enter_context(nc.sbuf_tensor("p0rotc", [128, 256], F32, side="right"))
            rots_ = p0.enter_context(nc.sbuf_tensor("p0rots", [128, 256], F32, side="right"))
            spb = p0.enter_context(nc.sbuf_tensor("p0spb", [128, D], BF16, side="right"))
            spb2 = p0.enter_context(nc.sbuf_tensor("p0spb2", [128, D], BF16, side="right"))
            ptr = [p0.enter_context(nc.psum_tensor("p0ps%d" % i, [128, 512], F32))
                   for i in range(2)]
            stage_free = [0, 0, 0]
            ps_free = [0, 0]
            sLDs = [self.sLDa, self.sLDb, self.sLDc]
            cur = [0]
            trn = [0]

            def load_stripe(dram, off_elems, rh, cols):
                i = cur[0] % 3
                q = "sync" if cur[0] % 2 == 0 else "scalar"
                cur[0] += 1
                b = stage[i]
                st.wait(q, sPE, stage_free[i])
                st.emit(q, lambda e, d=dram, ro=off_elems, r=rh, C=cols, bb=b: e.dma_start(
                    out=bb[0:r, 0:C], in_=AP(d, ro, [[C, r], [1, C]])), sLDs[i], 16)
                return i

            def tr(i, c0, rh, cw, evac):
                k = trn[0] % 2
                trn[0] += 1
                p = ptr[k]
                st.wait("tensor", sLDs[i], sLDs[i].n)
                st.wait("tensor", sDV, ps_free[k])
                st.emit("tensor", lambda e, bb=stage[i], pp=p, r=rh, cc=c0, w=cw: e.transpose(
                    pp[0:w, 0:r], bb[0:r, cc:cc + w], self.identf[0:r, 0:r]), sPE)
                st.wait("vector", sPE)
                evac(p)
                ps_free[k] = sDV.n
                stage_free[i] = sPE.n

            def cp_evac(tgt, coff, rh, cw=128):
                def f(p):
                    st.emit("vector", lambda e, pp=p, t=tgt, co=coff, r=rh, w=cw: e.tensor_copy(
                        t[0:w, co:co + r], pp[0:w, 0:r]), sDV)
                return f

            def tr4(i, c0, rh, tgt, coff, blk):
                k = trn[0] % 2
                trn[0] += 1
                p = ptr[k]
                st.wait("tensor", sLDs[i], sLDs[i].n)
                st.wait("tensor", sDV, ps_free[k])
                for j in range(4):
                    st.emit("tensor", lambda e, bb=stage[i], pp=p, r=rh,
                            cc=c0 + 128 * j, j=j: e.transpose(
                                pp[0:128, j * 128:j * 128 + r], bb[0:r, cc:cc + 128],
                                self.identf[0:r, 0:r]), sPE if j == 3 else None)
                st.wait("vector", sPE)
                tw = tgt.shape[1]
                st.emit("vector", lambda e, pp=p, t=tgt, co=coff, r=rh, b=blk, w=tw:
                        e.tensor_copy(AP(t, co, [[w, 128], [b, 4], [1, r]]),
                                      AP(pp, 0, [[512, 128], [128, 4], [1, r]])), sDV)
                ps_free[k] = sDV.n
                stage_free[i] = sPE.n

            def simple_w(dram, R, C, tgt, blk):
                for ro in range((R + 127) // 128):
                    rh = min(128, R - ro * 128)
                    i = load_stripe(dram, ro * 128 * C, rh, C)
                    for cc4 in range(C // 512):
                        tr4(i, cc4 * 512, rh, tgt, (cc4 * 4) * blk + ro * 128, blk)

            simple_w(self.w_dq, DQ, D, self.WdqT, DQ)
            simple_w(self.w_dkv, DCKV, D, self.WdkvT, DCKV)
            for hh in range(H):
                i = load_stripe(self.w_ukv, (192 * hh) * DKV, 64, DKV)
                for cc4 in range(NCKV // 4):
                    tr4(i, cc4 * 512, 64, self.WukT, (cc4 * 4) * DKV + 64 * hh, DKV)
            for hh in range(H):
                i = load_stripe(self.w_ukv, (192 * hh + 64) * DKV, 128, DKV)
                for cc4 in range(NCKV // 4):
                    tr4(i, cc4 * 512, 128, self.WuvT, (cc4 * 4) * D + 128 * hh, D)

            WQW = NCQ * D

            def trn_rot(i, c0, cw, ntile, ro):
                k = trn[0] % 2
                trn[0] += 1
                p = ptr[k]
                st.wait("tensor", sLDs[i], sLDs[i].n)
                st.wait("tensor", sDV, ps_free[k])
                for j in range(ntile):
                    st.emit("tensor", lambda e, bb=stage[i], pp=p, cc=c0 + 128 * j,
                            j=j, w=cw: e.transpose(
                                pp[0:w, j * 128:j * 128 + 128], bb[0:128, cc:cc + w],
                                self.identf[:]), sPE if j == ntile - 1 else None)
                st.wait("vector", sPE)
                ccq0 = c0 // 128
                st.emit("vector", lambda e, pp=p, b=ccq0 * D + ro * 128, n=ntile, w=cw:
                        e.tensor_copy(
                            AP(self.WuqT, b, [[WQW, w], [D, n], [1, 64]]),
                            AP(pp, 0, [[512, w], [128, n], [1, 64]])), sDV)
                st.emit("vector", lambda e, pp=p, n=ntile, w=cw, ro=ro: e.tensor_tensor(
                    rotc[0:w, 0:n * 64],
                    AP(pp, 64, [[512, w], [128, n], [1, 64]]),
                    AP(self.cosr, ro * 128 + 64, [[D, w], [0, n], [1, 64]]),
                    op=ALU.mult), sDV)
                st.emit("vector", lambda e, pp=p, n=ntile, w=cw, ro=ro: e.tensor_tensor(
                    rots_[0:w, 0:n * 64],
                    AP(pp, 96, [[512, w], [128, n], [-32, 2], [1, 32]]),
                    AP(self.sinr, ro * 128 + 64, [[D, w], [0, n], [1, 64]]),
                    op=ALU.mult), sDV)
                st.wait("vector", sDV)
                st.emit("vector", lambda e, b=ccq0 * D + ro * 128 + 64, n=ntile, w=cw:
                        e.tensor_tensor(
                            AP(self.WuqT, b, [[WQW, w], [D, n], [1, 64]]),
                            rotc[0:w, 0:n * 64], rots_[0:w, 0:n * 64], op=ALU.add), sDV)
                ps_free[k] = sDV.n
                stage_free[i] = sPE.n

            for ro in range(NCD):
                i = load_stripe(self.w_uq, ro * 128 * DQ, 128, DQ)
                trn_rot(i, 0, 128, 4, ro)
                trn_rot(i, 512, 128, 1, ro)
                trn_rot(i, 640, 42, 1, ro)

            spbs, spg = [spb, spb2], [0, 0]

            def spill_stripes(dram, sp, nro, spill_ap_fn):
                for ro in range(nro):
                    i = load_stripe(dram, ro * 128 * D, 128, D)
                    sp_ = spbs[ro % 2]
                    st.wait("vector", sGP, spg[ro % 2])
                    for cc4 in range(NCD // 4):
                        tr4(i, cc4 * 512, 128, sp_, cc4 * 512, 128)
                    st.wait("gpsimd", sDV)
                    st.emit("gpsimd", lambda e, s=sp, r=ro, f=spill_ap_fn, b=sp_:
                            e.dma_start(out=f(s, r), in_=b[:]), sGP, 16)
                    spg[ro % 2] = sGP.n

            wsp_ap = lambda s, r: AP(s, r * 128, [[NCD * D, 128], [D, NCD], [1, 128]])
            spill_stripes(self.w_o, self.wo_sp, NCD, wsp_ap)
            spill_stripes(self.w_ffn, self.wffn_sp, NCD, wsp_ap)

            i = load_stripe(self.lora_a, 0, 128, D)
            st.wait("vector", sGP)
            for cc4 in range(NCD // 4):
                tr4(i, cc4 * 512, 128, spb, cc4 * 512, 128)
            st.wait("gpsimd", sDV)
            st.emit("gpsimd", lambda e: e.dma_start(
                out=AP(self.acat_sp, 0, [[NCD * 128, 128], [1, NCD * 128]]), in_=spb[:]), sGP, 16)

            i = load_stripe(self.gate_w, 0, NE, D)
            st.wait("vector", sGP)
            for cc in range(NCD):
                tr(i, cc * 128, NE, 128, cp_evac(spb, cc * NE, NE))
            st.wait("gpsimd", sDV)
            st.emit("gpsimd", lambda e: e.dma_start(
                out=AP(self.gate_sp, 0, [[NCD * NE, 128], [1, NCD * NE]]),
                in_=spb[0:128, 0:NCD * NE]), sGP, 16)

            st.wait("vector", sGP)
            for co in range(NCD):
                ii = cur[0] % 3
                q = "sync" if cur[0] % 2 == 0 else "scalar"
                cur[0] += 1
                b = stage[ii]
                st.wait(q, sPE, stage_free[ii])
                st.emit(q, lambda e, c0=co * 128, bb=b: e.dma_start(
                    out=bb[0:128, 0:ER],
                    in_=AP(self.lora_b, c0 * RLORA,
                           [[RLORA, 128], [D * RLORA, NE], [1, RLORA]])), sLDs[ii], 16)
                tr(ii, 0, 128, 128, cp_evac(spb, co * 128, 128))
            st.wait("gpsimd", sDV)
            st.emit("gpsimd", lambda e: e.dma_start(
                out=AP(self.bcat_sp, 0, [[NCD * 128, 128], [1, NCD * 128]]),
                in_=spb[0:128, 0:NCD * 128]), sGP, 16)
            st.wait("gpsimd", sGP)
            st.flush(nc)

    def p1(self):
        nc = self.nc
        sLD, sGP, sPE, sDV, sAC, sST = self.sLD, self.sGP, self.sPE, self.sDV, self.sAC, self.sST
        st = Streams()
        with ExitStack() as p1:
            def rsb(name, shape, dt=F32):
                return p1.enter_context(nc.sbuf_tensor(name, list(shape), dt, side="right"))

            xT = rsb("xT", [128, NCD * SW], BF16)
            kvn = rsb("kvn", [128, NCKV * SW], BF16)
            raw = rsb("raw", [128, NCKV * SW], BF16)
            cqraw = rsb("cqraw", [128, NCQ * SW], BF16)
            cqn = cqraw
            krr = rsb("krr", [64, SW], BF16)
            xv = rsb("xv", [128, 4 * D], BF16)
            cqr, ckvr = cqraw, raw
            xbf, vsb = xv, xv
            sq = [rsb("sq%d" % i, [128, SW], BF16) for i in range(2)]
            sqkr = rsb("sqkr", [64, SW], BF16)
            _s2 = rsb("s2", [128, SW], F32)
            s2 = [_s2, _s2]
            qT = [rsb("qT%d" % i, [128, SW], BF16) for i in range(2)]
            knr = [rsb("knr%d" % i, [128, SW], BF16) for i in range(2)]
            murow = rsb("murow", [1, SW], F32)
            rrow = rsb("rrow", [1, SW], F32)
            varow = rsb("varow", [1, SW], F32)
            kr2row = rsb("kr2row", [1, SW], BF16)
            tiny = rsb("tiny", [1, 8], F32)
            nrmbf = rsb("nrmbf", [1, 1024], BF16)

            pA = [p1.enter_context(nc.psum_tensor("p1a%d" % i, [128, 512], F32))
                  for i in range(2)]
            pS = p1.enter_context(nc.psum_tensor("p1s", [1, 1024], F32))
            pS2 = p1.enter_context(nc.psum_tensor("p1s2", [1, 1024], F32))
            pMu = p1.enter_context(nc.psum_tensor("p1mu", [128, 512], F32))
            pR = p1.enter_context(nc.psum_tensor("p1r", [128, 512], F32))

            self._pa_free = [0, 0]
            self._pa_i = 0
            self._tr_free = [0, 0]
            p1pend = [None]

            def pend_flush():
                if p1pend[0] is not None:
                    p1pend[0]()
                    p1pend[0] = None

            def pend_set(fn):
                pend_flush()
                p1pend[0] = fn
            sQT = [self.sQTa, self.sQTb]
            sKN = [self.sKNa, self.sKNb]
            sKR = self.sKR

            def proj(nchunks, stat_fn, mov, mov_base_fn, out_rows, evac, kdims=None,
                     extra=None):
                k = self._pa_i % 2
                self._pa_i += 1
                p = pA[k]
                st.wait("tensor", sDV, self._pa_free[k])
                for c in range(nchunks):
                    kd = 128 if kdims is None else kdims[c]
                    last = (c == nchunks - 1) and extra is None
                    st.emit("tensor",
                            lambda e, pp=p, s=stat_fn(c), m=mov, mb=mov_base_fn(c), orr=out_rows,
                            r=kd, c0=(c == 0), st_=last: e.matmul(
                                pp[0:orr, 0:SW], s, m[0:r, mb:mb + SW], start=c0, stop=st_,
                                skip_group_check=True),
                            sPE if last else None)
                    if c == 1:
                        pend_flush()
                if extra is not None:
                    extra(p)
                st.wait("vector", sPE)
                evac(p)
                self._pa_free[k] = sDV.n
                return p

            for s in range(NSTRIP):
                st.wait("gpsimd", sPE)
                st.wait("gpsimd", sST)
                st.wait("tensor", sDV)
                sX = [self.sXa, self.sXb]
                for half in range(2):
                    st.emit("gpsimd", lambda e, ss=s, hf=half: e.dma_start(
                        out=xbf[0:128, 0:2 * D] if hf == 0 else xbf[0:128, 2 * D:4 * D],
                        in_=AP(self.x_own, (ss * SW + hf * 256) * D,
                               [[D, 128], [128 * D, 2], [1, D]])), sX[half], 16)
                    st.wait("tensor", sX[half], sX[half].n)
                    for bank in range(4):
                        k = bank % 2
                        p = pMu if k == 0 else pR
                        st.wait("tensor", sDV, self._tr_free[k])
                        for j in range(8):
                            t_idx = bank * 8 + j
                            cx, i = t_idx // 2, t_idx % 2
                            st.emit("tensor", lambda e, pp=p, j_=j, cx_=cx, i_=i, hf=half:
                                    e.transpose(
                                        pp.bitcast(BF16)[0:128, j_ * 128:j_ * 128 + 128],
                                        xbf[0:128, (2 * hf + i_) * D + cx_ * 128:
                                            (2 * hf + i_) * D + cx_ * 128 + 128],
                                        self.identb[:]), sPE if j == 7 else None)
                        st.wait("vector", sPE)
                        st.emit("vector", lambda e, pp=p, b=bank, hf=half: e.tensor_copy(
                            AP(xT, (b * 4) * SW + hf * 256,
                               [[NCD * SW, 128], [SW, 4], [1, 256]]),
                            AP(pp.bitcast(BF16), 0, [[1024, 128], [256, 4], [1, 256]])), sDV)
                        self._tr_free[k] = sDV.n

                st.wait("tensor", sDV)
                for co in range(NCQ):
                    w = 128 if co < 5 else CQ_TAIL

                    def evac_cq(p, co=co, w=w):
                        st.emit("vector", lambda e, pp=p: e.tensor_scalar(
                            cqr[0:w, co * SW:(co + 1) * SW], pp[0:w, 0:SW],
                            self.bdq_c[0:w, co:co + 1], None, op0=ALU.add), sDV)
                        st.wait("vector", sDV)
                        st.emit("vector", lambda e: e.tensor_tensor(
                            sq[co % 2][0:w, 0:SW], cqr[0:w, co * SW:(co + 1) * SW],
                            cqr[0:w, co * SW:(co + 1) * SW], op=ALU.mult), sDV)

                    proj(NCD, lambda c, co=co, w=w:
                         self.WdqT[0:128, c * DQ + co * 128:c * DQ + co * 128 + w],
                         xT, lambda c: c * SW, w, evac_cq)

                    def stat_cq(co=co, w=w, dvn=None):
                        def f():
                            st.wait("tensor", sDV)
                            st.emit("tensor", lambda e: e.matmul(
                                pS[0:1, 0:512], self.ones_bf[0:w, :],
                                cqr[0:w, co * SW:(co + 1) * SW],
                                start=(co == 0), stop=(co == NCQ - 1),
                                skip_group_check=True), None)
                            st.emit("tensor", lambda e: e.matmul(
                                pS[0:1, 512:1024], self.ones_bf[0:w, :],
                                sq[co % 2][0:w, 0:SW],
                                start=(co == 0), stop=(co == NCQ - 1),
                                skip_group_check=True),
                                sPE if co == NCQ - 1 else None)
                        return f
                    pend_set(stat_cq())

                def ckv_section(s=s):
                  for co in range(9):
                    w = 128 if co < 8 else 64

                    def evac_ckv(p, co=co, w=w):
                        tgt = (ckvr, co * SW) if co < 8 else (krr, 0)
                        if co == 8:
                            st.wait("vector", sKR, sKR.n)
                        st.emit("vector", lambda e, pp=p, tg=tgt: e.tensor_scalar(
                            tg[0][0:w, tg[1]:tg[1] + SW], pp[0:w, 0:SW],
                            self.bdkv_c[0:w, co:co + 1], None, op0=ALU.add), sDV)
                        st.wait("vector", sDV)
                        dst_sq = sq[co % 2] if co < 8 else sqkr
                        st.emit("vector", lambda e, tg=tgt, dsq=dst_sq: e.tensor_tensor(
                            dsq[0:w, 0:SW], tg[0][0:w, tg[1]:tg[1] + SW],
                            tg[0][0:w, tg[1]:tg[1] + SW], op=ALU.mult), sDV)

                    proj(NCD, lambda c, co=co, w=w:
                         self.WdkvT[0:128, c * DCKV + co * 128:c * DCKV + co * 128 + w],
                         xT, lambda c: c * SW, w, evac_ckv)
                    if co < 8:
                        def stat_ckv(co=co):
                            def f():
                                st.wait("tensor", sDV)
                                st.emit("tensor", lambda e: e.matmul(
                                    pS2[0:1, 0:512], self.ones_bf[:],
                                    ckvr[0:128, co * SW:(co + 1) * SW],
                                    start=(co == 0), stop=(co == 7),
                                    skip_group_check=True), None)
                                st.emit("tensor", lambda e: e.matmul(
                                    pS2[0:1, 512:1024], self.ones_bf[:], sq[co % 2][:],
                                    start=(co == 0), stop=(co == 7),
                                    skip_group_check=True),
                                    sPE if co == 7 else None)
                            return f
                        pend_set(stat_ckv())

                ckv_section()
                pend_flush()
                self._ln(st, pS, pMu, pR, murow, rrow, varow, DQ, NCQ, cqr, cqn, s2,
                         self.qng_c, self.qnb_c, CQ_TAIL, SW)
                dv_cqn = self.sDV.n

                pend_flush()
                self._ln(st, pS2, pMu, pR, murow, rrow, varow, DKV, NCKV, ckvr, kvn, s2,
                         self.kvg_c, self.kvb_c, 128, SW)

                st.wait("tensor", sDV, dv_cqn)
                kdims_q = [128] * 5 + [CQ_TAIL]
                for h in range(H):
                    def evac_q(p, h=h):
                        qb = qT[h % 2]
                        st.wait("vector", sQT[h % 2], sQT[h % 2].n)
                        st.emit("vector", lambda e, pp=p: e.tensor_scalar(
                            qb[:], pp[0:128, 0:SW], self.buq_c[0:128, h:h + 1], None,
                            op0=ALU.add), sDV)
                        st.wait("vector", sDV)
                        st.emit("vector", lambda e: e.tensor_tensor(
                            sq[h % 2][:], qb[:], qb[:], op=ALU.mult), sDV)

                    proj(NCQ, lambda c, h=h:
                         self.WuqT[0:kdims_q[c], c * D + h * 128:c * D + h * 128 + 128],
                         cqn, lambda c: c * SW, 128, evac_q, kdims=kdims_q)
                    st.wait("sync", sDV)
                    st.emit("sync", lambda e, h=h, ss=s: e.dma_start(
                        out=AP(self.q_spill, (h * QNT + ss) * DH * SW, [[SW, 128], [1, SW]]),
                        in_=qT[h % 2][0:128, 0:SW]), sQT[h % 2], 16)
                    def stat_q(h=h, ss=s):
                        def f():
                            st.wait("tensor", sDV)
                            st.emit("tensor", lambda e: e.matmul(
                                pS[0:1, 0:512], self.ones_bf[:], sq[h % 2][:],
                                start=True, stop=True, skip_group_check=True), sPE)
                            st.wait("vector", sPE)
                            st.emit("vector", lambda e: e.tensor_reduce(
                                self.maxq2[0:1, h * QNT + ss:h * QNT + ss + 1],
                                pS[0:1, 0:SW], axis=AX.X, op=ALU.max), sDV)
                        return f
                    pend_set(stat_q())

                st.wait("tensor", sDV)
                st.emit("tensor", lambda e: e.matmul(
                    pS2[0:1, 0:512], self.ones_bf[0:64, :],
                    sqkr[0:64, 0:SW], start=True, stop=True, skip_group_check=True), sPE)
                st.wait("vector", sPE)
                st.emit("vector", lambda e: e.tensor_copy(kr2row[0:1, :], pS2[0:1, 0:512]), sDV)
                st.wait("tensor", sDV)

                st.wait("sync", sDV)
                for h in range(H):
                    st.emit("sync", lambda e, h=h, ss=s: e.dma_start(
                        out=AP(self.contrib, ss * SBLK + (h * DH + 64) * SW,
                               [[SW, 64], [1, SW]]),
                        in_=krr[0:64, 0:SW]), sKR, 16)

                st.wait("tensor", sDV)
                for co in range(NCKV):
                    def evac_k(p, co=co):
                        kb = knr[co % 2]
                        st.wait("vector", sKN[co % 2], sKN[co % 2].n)
                        st.emit("vector", lambda e, pp=p: e.tensor_scalar(
                            kb[:], pp[0:128, 0:SW], self.buk_c[0:128, co:co + 1], None,
                            op0=ALU.add), sDV)
                        st.wait("vector", sDV)
                        st.emit("vector", lambda e: e.tensor_tensor(
                            sq[co % 2][:], kb[:], kb[:], op=ALU.mult), sDV)

                    proj(NCKV, lambda c, co=co:
                         self.WukT[0:128, c * DKV + co * 128:c * DKV + co * 128 + 128],
                         kvn, lambda c: c * SW, 128, evac_k)
                    st.wait("sync", sDV)
                    for hf in range(2):
                        h = 2 * co + hf
                        st.emit("sync", lambda e, h=h, hf=hf, co=co, ss=s: e.dma_start(
                            out=AP(self.contrib, ss * SBLK + h * DH * SW,
                                   [[SW, 64], [1, SW]]),
                            in_=knr[co % 2][hf * 64:hf * 64 + 64, 0:SW]), sKN[co % 2], 16)
                    def knorm(co=co):
                        def f():
                            st.wait("tensor", sDV)
                            st.emit("tensor", lambda e: e.matmul(
                                pS2[0:1, 0:512], self.onesA[:], sq[co % 2][:],
                                start=True, stop=True, skip_group_check=True), None)
                            st.emit("tensor", lambda e: e.matmul(
                                pS2[0:1, 512:1024], self.onesB[:], sq[co % 2][:],
                                start=True, stop=True, skip_group_check=True), sPE)
                            st.wait("vector", sPE)
                            st.emit("vector", lambda e: e.tensor_tensor(
                                varow[0:1, 0:SW], AP(pS2, 0, [[1024, 1], [1, SW]]),
                                kr2row[0:1, 0:SW], op=ALU.add), sDV)
                            st.emit("vector", lambda e: e.tensor_tensor(
                                murow[0:1, 0:SW], AP(pS2, 512, [[1024, 1], [1, SW]]),
                                kr2row[0:1, 0:SW], op=ALU.add), sDV)
                            st.wait("vector", sDV)
                            st.emit("vector", lambda e: e.tensor_reduce(
                                tiny[0:1, 0:1], varow[0:1, 0:SW], axis=AX.X,
                                op=ALU.max), sDV)
                            st.emit("vector", lambda e: e.tensor_reduce(
                                tiny[0:1, 1:2], murow[0:1, 0:SW], axis=AX.X,
                                op=ALU.max), sDV)
                            st.wait("vector", sDV)
                            for hf in range(2):
                                h = 2 * co + hf
                                st.emit("vector", lambda e, h=h, hf=hf: e.tensor_copy(
                                    self.maxk2[0:1, h:h + 1], tiny[0:1, hf:hf + 1]), sDV)
                        return f
                    pend_set(knorm())

                for i in range(4):
                    for vs in range(4):
                        def evac_v(p, i=i, vs=vs):
                            st.emit("vector", lambda e, pp=p: e.tensor_copy(
                                vsb[0:128, i * D + vs * 512:i * D + vs * 512 + 512],
                                pp[0:128, 0:512]), sDV)

                        def extra_bias(p, vs=vs):
                            st.emit("tensor", lambda e, pp=p: e.matmul(
                                pp[0:128, 0:512], self.onesrow_bf[0:1, :],
                                self.bukvv_r[0:1, vs * 512:(vs + 1) * 512],
                                start=False, stop=True, skip_group_check=True), sPE)

                        proj(NCKV, lambda c, i=i:
                             kvn[0:128, c * SW + i * 128:c * SW + i * 128 + 128],
                             self.WuvT, lambda c, vs=vs: c * D + vs * 512, 128,
                             evac_v, extra=extra_bias)
                st.wait("sync", sDV)
                for h in range(H):
                    st.emit("sync", lambda e, h=h, ss=s: e.dma_start(
                        out=AP(self.contrib, ss * SBLK + VOFFs + h * SW * DH,
                               [[DH, 128], [128 * DH, 4], [1, DH]]),
                        in_=AP(vsb, h * 128, [[4 * D, 128], [D, 4], [1, DH]])), sST, 16)

                if s == 0:
                    st.emit("gpsimd", lambda e: e.memset(nrmbf[:], 0.0), self.sGE)
                st.wait("vector", sDV)
                st.wait("vector", self.sGE)
                st.wait("vector", sST)
                st.emit("vector", lambda e: e.tensor_scalar(
                    nrmbf[0:1, 0:16], self.maxk2[0:1, :], 1.02, None, op0=ALU.mult), sDV)
                st.wait("sync", sDV)
                st.emit("sync", lambda e, ss=s: e.dma_start(
                    out=AP(self.contrib, ss * SBLK + NOFFs, [[1024, 1], [1, 1024]]),
                    in_=nrmbf[0:1, :]), sST, 16)
                for sx in (sST, sKR, sKN[0], sKN[1]):
                    st.wait("gpsimd", sx, sx.n)
                st.emit("gpsimd", lambda e, ss=s: e.collective_compute(
                    "AllGather", ALU.bypass,
                    replica_groups=[[0, 1, 2, 3], [4, 5, 6, 7]],
                    ins=[AP(self.contrib, ss * SBLK, [[SBLK, 1], [1, SBLK]])],
                    outs=[AP(self.gathered, ss * 4 * SBLK,
                             [[4 * SBLK, 1], [1, 4 * SBLK]])]), self.sCC)

            st.wait("sync", sST)
            st.wait("gpsimd", sGP)
            for sx in (sQT[0], sQT[1],):
                st.wait("gpsimd", sx, sx.n)
            st.flush(nc)

    def _ln(self, st, pS, pMu, pR, murow, rrow, varow, DIM, NCH, raw, dst, s2, g_c, b_c,
            tail, width):
        sPE, sDV, sAC = self.sPE, self.sDV, self.sAC
        inv = 1.0 / DIM
        st.wait("vector", sPE)
        st.emit("vector", lambda e: e.tensor_scalar(
            murow[0:1, 0:width], pS[0:1, 0:512], inv, None, op0=ALU.mult), sDV)
        st.emit("vector", lambda e: e.tensor_scalar(
            varow[0:1, 0:width], pS[0:1, 512:1024], inv, None, op0=ALU.mult), sDV)
        st.wait("vector", sDV)
        st.emit("vector", lambda e: e.tensor_tensor(
            rrow[0:1, 0:width], murow[0:1, 0:width], murow[0:1, 0:width], op=ALU.mult), sDV)
        st.wait("vector", sDV)
        st.emit("vector", lambda e: e.tensor_tensor(
            varow[0:1, 0:width], varow[0:1, 0:width], rrow[0:1, 0:width], op=ALU.subtract), sDV)
        st.wait("scalar", sDV)
        st.emit("scalar", lambda e: e.activation(out=rrow[0:1, 0:width], in_=varow[0:1, 0:width],
                                                 func=AF.Ln, bias=self.epsb[0:1, :],
                                                 scale=1.0), sAC)
        st.wait("scalar", sAC)
        st.emit("scalar", lambda e: e.activation(out=rrow[0:1, 0:width], in_=rrow[0:1, 0:width],
                                                 func=AF.Exp, bias=0.0, scale=-0.5), sAC)
        st.wait("tensor", sAC)
        st.wait("tensor", sDV)
        st.emit("tensor", lambda e: e.matmul(pMu[0:128, 0:width], self.onesrow_f[0:1, :],
                                             murow[0:1, 0:width], start=True, stop=True,
                                             skip_group_check=True), None)
        st.emit("tensor", lambda e: e.matmul(pR[0:128, 0:width], self.onesrow_f[0:1, :],
                                             rrow[0:1, 0:width], start=True, stop=True,
                                             skip_group_check=True), sPE)
        st.wait("vector", sPE)
        for c in range(NCH):
            w = 128 if c < NCH - 1 else tail
            st.emit("vector", lambda e, c=c, w=w: e.tensor_tensor(
                s2[c % 2][0:w, 0:width], raw[0:w, c * width:(c + 1) * width],
                pMu[0:w, 0:width], op=ALU.subtract), sDV)
            st.wait("vector", sDV)
            st.emit("vector", lambda e, c=c, w=w: e.tensor_tensor(
                s2[c % 2][0:w, 0:width], s2[c % 2][0:w, 0:width], pR[0:w, 0:width],
                op=ALU.mult), sDV)
            st.wait("vector", sDV)
            st.emit("vector", lambda e, c=c, w=w: e.tensor_scalar(
                dst[0:w, c * width:(c + 1) * width], s2[c % 2][0:w, 0:width],
                g_c[0:w, c:c + 1], b_c[0:w, c:c + 1], op0=ALU.mult, op1=ALU.add), sDV)
            st.wait("vector", sDV)

    def ag(self):
        nc = self.nc
        st = Streams()
        sGE, sDV, sLD = self.sGE, self.sDV, self.sLD
        with ExitStack() as ages:
            scr = ages.enter_context(nc.sbuf_tensor("agscr", [128, 16 * SW], F32, side="right"))
            mthr_t = ages.enter_context(nc.sbuf_tensor("agmthr", [128, 1], F32, side="right"))
            vr = ages.enter_context(nc.sbuf_tensor("agvr", [1, D], F32, side="right"))
            st.emit("gpsimd", lambda e: e.iota(scr[:], [[-128, 16], [1, SW]],
                                               channel_multiplier=-1,
                                               allow_small_or_imprecise_dtypes=True), sGE)
            st.emit("sync", lambda e: e.dma_start(out=mthr_t[:], in_=self.mthr.ap()), sLD, 16)
            st.wait("vector", sGE)
            st.wait("vector", sLD)
            st.emit("vector", lambda e: e.tensor_scalar(self.mask[:], scr[:], mthr_t[:, 0:1],
                                                        None, op0=ALU.is_ge), sDV)
            for dram, dst, n in ((self.b_o, self.bo_r, D), (self.b_ffn, self.bffn_r, D),
                                 (self.gate_b, self.gateb_r, NE)):
                st.wait("sync", sDV)
                st.emit("sync", lambda e, d=dram, nn=n: e.dma_start(
                    out=vr[0:1, 0:nn], in_=AP(d, 0, [[nn, 1], [1, nn]])), sLD, 16)
                st.wait("vector", sLD)
                st.emit("vector", lambda e, dd=dst, nn=n: e.tensor_copy(
                    dd[0:1, 0:nn], vr[0:1, 0:nn]), sDV)
            st.wait("gpsimd", sDV)
            st.flush(nc)

    def p2(self):
        nc = self.nc
        sPE, sDV, sAC, sCC = self.sPE, self.sDV, self.sAC, self.sCC
        sWL = self.sWL
        sKV = [self.sKVa, self.sKVb]
        sQB = [self.sQBa, self.sQBb]
        sOT = [self.sOTa, self.sOTb]
        st = Streams()
        with ExitStack() as p2:
            def rsb(name, shape, dt=BF16):
                return p2.enter_context(nc.sbuf_tensor(name, list(shape), dt, side="right"))

            qbuf = [rsb("qbuf%d" % i, [128, SW]) for i in range(2)]
            Kbuf = [rsb("Kbuf%d" % i, [128, 1024]) for i in range(2)]
            Vbuf = [rsb("Vbuf%d" % i, [128, 1024]) for i in range(2)]
            pt = [rsb("pt%d" % i, [128, 1024]) for i in range(2)]
            ot_sb = [rsb("ot%d" % i, [128, SW]) for i in range(2)]
            recip = p2.enter_context(nc.sbuf_tensor("recip", [1, SW], F32, side="right"))
            recip_bc = p2.enter_context(nc.sbuf_tensor("recip_bc", [128, SW], F32, side="right"))
            nrm4 = p2.enter_context(nc.sbuf_tensor("nrm4", [1, 64], BF16, side="right"))
            nrm4f = p2.enter_context(nc.sbuf_tensor("nrm4f", [1, 64], F32, side="right"))
            mkrun = p2.enter_context(nc.sbuf_tensor("mkrun", [1, H], F32, side="right"))
            prodr = p2.enter_context(nc.sbuf_tensor("prodr", [1, H], F32, side="right"))
            psS = [p2.enter_context(nc.psum_tensor("p2s%d" % i, [128, 1024], F32))
                   for i in range(2)]
            psO = [p2.enter_context(nc.psum_tensor("p2o%d" % i, [128, SW], F32))
                   for i in range(2)]
            psR = p2.enter_context(nc.psum_tensor("p2r", [1, SW], F32))
            psB = p2.enter_context(nc.psum_tensor("p2b", [128, SW], F32))

            for sp, tgt, cols in ((self.wo_sp, self.WoT, NCD * D),
                                  (self.wffn_sp, self.WffnT, NCD * D),
                                  (self.acat_sp, self.AcatT, NCD * ER),
                                  (self.gate_sp, self.gateT, NCD * NE),
                                  (self.bcat_sp, self.BcatT, D)):
                st.emit("gpsimd", lambda e, s=sp, t=tgt, cc=cols: e.dma_start(
                    out=t[0:128, 0:cc], in_=AP(s, 0, [[cc, 128], [1, cc]])), sWL, 16)

            self._p2_send = [None]
            pe_qfree = [0, 0]
            pe_kvfree = [0, 0]
            ac_psfree = [0, 0]
            pe_ptfree = [0, 0]
            dv_osfree = [0, 0]
            ac_rfree = 0
            self._p2_dvb = [0]

            for kcls in range(1, QNT + 1):
                st.wait("gpsimd", sCC, kcls)
                st.emit("gpsimd", lambda e, k=kcls: e.dma_start(
                    out=nrm4[0:1, 0:64],
                    in_=AP(self.gathered, (k - 1) * 4 * SBLK + NOFFs,
                           [[64, 1], [SBLK, 4], [1, 16]])), self.sGP, 16)
                st.wait("vector", self.sGP)
                st.emit("vector", lambda e: e.tensor_copy(nrm4f[:], nrm4[:]), sDV)
                st.wait("vector", sDV)
                st.emit("vector", lambda e: e.tensor_reduce(
                    prodr[0:1, 0:H], AP(nrm4f, 0, [[64, 1], [1, 16], [16, 4]]),
                    axis=AX.X, op=ALU.max), sDV)
                st.wait("vector", sDV)
                if kcls == 1:
                    st.emit("vector", lambda e: e.tensor_copy(mkrun[:], prodr[0:1, 0:H]), sDV)
                else:
                    st.emit("vector", lambda e: e.tensor_tensor(
                        mkrun[:], mkrun[:], prodr[0:1, 0:H], op=ALU.max), sDV)
                st.wait("vector", sDV)
                st.emit("vector", lambda e, k=kcls: e.tensor_tensor(
                    prodr[0:1, 0:H],
                    AP(self.maxq2, k - 1, [[H * QNT, 1], [QNT, H]]),
                    mkrun[0:1, 0:H], op=ALU.mult), sDV)
                st.wait("scalar", sDV)
                st.emit("scalar", lambda e: e.activation(out=prodr[:], in_=prodr[:],
                                                         func=AF.Ln, bias=self.epsb[0:1, :],
                                                         scale=1.0), sAC)
                st.wait("scalar", sAC)
                st.emit("scalar", lambda e: e.activation(out=prodr[:], in_=prodr[:],
                                                         func=AF.Exp, bias=0.0,
                                                         scale=0.5), sAC)
                st.wait("vector", sAC)
                st.emit("vector", lambda e: e.tensor_scalar(
                    prodr[:], prodr[:], ATT_SCALE, 80.0, op0=ALU.mult, op1=ALU.min), sDV)
                st.wait("vector", sDV)
                st.emit("vector", lambda e: e.tensor_scalar(
                    prodr[:], prodr[:], -1.0, None, op0=ALU.mult), sDV)
                if self._p2_send[0] is not None:
                    self._p2_send[0]()
                    self._p2_send[0] = None
                st.wait("tensor", sDV)
                st.wait("tensor", sDV, self._p2_dvb[0])
                st.emit("tensor", lambda e: e.matmul(
                    psB[0:128, 0:H], self.onesrow_f[0:1, :], prodr[0:1, 0:H],
                    start=True, stop=True, skip_group_check=True), sPE)
                st.wait("vector", sPE)
                st.emit("vector", lambda e, k=kcls: e.tensor_copy(
                    self.negm_bc[0:128, (k - 1) * H:k * H], psB[0:128, 0:H]), sDV)
                self._p2_dvb[0] = self.sDV.n
                st.wait("tensor", sDV)

                for h in range(H):
                    sidx = (kcls - 1) * H + h
                    psw = sidx % 2
                    nblk = 2 * kcls
                    npair = 4 * nblk

                    st.wait("sync", sPE, pe_qfree[psw])
                    st.emit("sync", lambda e, h=h, k=kcls, b=qbuf[psw]: e.dma_start(
                        out=b[:], in_=AP(self.q_spill, (h * QNT + k - 1) * DH * SW,
                                         [[SW, 128], [1, SW]])), sQB[psw], 16)
                    qld = sQB[psw].n

                    kvld = [0] * nblk

                    def emit_load(blk, h=h):
                        pb = blk % 2
                        strip, rank0 = blk // 2, (2 * blk) % 4
                        base = strip * 4 * SBLK + rank0 * SBLK
                        st.wait("sync", sPE, pe_kvfree[pb])
                        st.emit("sync", lambda e, ba=base, b=Kbuf[pb]: e.dma_start(
                            out=AP(b, 0, [[1024, 128], [512, 2], [1, 512]]),
                            in_=AP(self.gathered, ba + KOFFs + h * DH * SW,
                                   [[SW, 128], [SBLK, 2], [1, 512]])), sKV[pb], 16)
                        for pc in range(2):
                            st.emit("sync", lambda e, ba=base, b=Vbuf[pb], pc=pc:
                                    e.dma_start(
                                        out=AP(b, pc * 512,
                                               [[1024, 128], [128, 4], [1, 128]]),
                                        in_=AP(self.gathered,
                                               ba + pc * SBLK + VOFFs + h * SW * DH,
                                               [[DH, 128], [128 * DH, 4], [1, DH]])),
                                    sKV[pb], 16)
                        kvld[blk] = sKV[pb].n

                    emit_load(0)
                    if nblk > 1:
                        emit_load(1)

                    pend = [None]

                    def flush_rpv():
                        if pend[0] is not None:
                            pend[0]()
                            pend[0] = None

                    def flush_sweepend():
                        if self._p2_send[0] is not None:
                            self._p2_send[0]()
                            self._p2_send[0] = None

                    for g in range(npair):
                        blk, gb = g // 4, g % 4
                        pb = blk % 2
                        gp = g % 2
                        masked = (blk >= nblk - 2)
                        first = (g == 0)
                        last = (g == npair - 1)

                        if gb == 0:
                            st.wait("tensor", sKV[pb], kvld[blk])
                        if first:
                            st.wait("tensor", sQB[psw], qld)
                        st.wait("tensor", sAC, ac_psfree[gp])
                        for j in range(2):
                            cc = 2 * gb + j
                            st.emit("tensor", lambda e, pS=psS[gp], Kb=Kbuf[pb],
                                    qb=qbuf[psw], cc=cc, j=j: e.matmul(
                                        pS[0:128, j * SW:(j + 1) * SW],
                                        Kb[0:128, cc * 128:cc * 128 + 128], qb[:],
                                        start=True, stop=True, skip_group_check=True),
                                    sPE if j == 1 else None)
                        spe_s = sPE.n

                        st.wait("scalar", sPE, spe_s)
                        st.wait("scalar", sPE, pe_ptfree[gp])
                        st.emit("scalar", lambda e, pS=psS[gp], p=pt[gp], sidx=sidx:
                                e.activation(out=p[:], in_=pS[0:128, 0:1024], func=AF.Exp,
                                             bias=self.negm_bc[0:128, sidx:sidx + 1],
                                             scale=ATT_SCALE), sAC)
                        ac_psfree[gp] = sAC.n
                        ac_exp = sAC.n

                        dv_mask = None
                        if masked:
                            jp = g - (npair - 8)
                            st.wait("vector", sAC, ac_exp)
                            st.emit("vector", lambda e, p=pt[gp], jp=jp: e.tensor_tensor(
                                p[:], p[:], self.mask[0:128, jp * 1024:(jp + 1) * 1024],
                                op=ALU.mult), sDV)
                            dv_mask = sDV.n

                        flush_rpv()
                        if g == 0:
                            flush_sweepend()

                        def mk_rpv(g=g, gp=gp, pb=pb, ac_exp=ac_exp, dv_mask=dv_mask,
                                   first=first, last=last, psw=psw):
                            def emit():
                                if dv_mask is not None:
                                    st.wait("tensor", sDV, dv_mask)
                                else:
                                    st.wait("tensor", sAC, ac_exp)
                                if first:
                                    st.wait("tensor", sDV, dv_osfree[psw])
                                    st.wait("tensor", sAC, ac_rfree)
                                for j in range(2):
                                    cc = 2 * (g % 4) + j
                                    st.emit("tensor", lambda e, p=pt[gp], j=j: e.matmul(
                                        psR[0:1, :], self.ones_bf[:],
                                        p[0:128, j * SW:(j + 1) * SW],
                                        start=(first and j == 0), stop=(last and j == 1),
                                        skip_group_check=True), None)
                                    st.emit("tensor", lambda e, p=pt[gp], Vb=Vbuf[pb],
                                            cc=cc, j=j: e.matmul(
                                                psO[psw][:],
                                                Vb[0:128, cc * 128:cc * 128 + 128],
                                                p[0:128, j * SW:(j + 1) * SW],
                                                start=(first and j == 0),
                                                stop=(last and j == 1),
                                                skip_group_check=True),
                                            sPE if j == 1 else None)
                                pe_ptfree[gp] = sPE.n
                                if g % 4 == 3 or last:
                                    pe_kvfree[pb] = sPE.n
                                if last:
                                    pe_qfree[psw] = sPE.n
                            return emit

                        pend[0] = mk_rpv()
                        if g >= 1 and (g - 1) % 4 == 3:
                            nxt = (g - 1) // 4 + 2
                            if nxt < nblk:
                                emit_load(nxt)
                    flush_rpv()
                    spe_sweep = sPE.n

                    st.wait("scalar", sPE, spe_sweep)
                    st.emit("scalar", lambda e: e.activation(
                        out=recip[0:1, :], in_=psR[0:1, :], func=AF.Ln,
                        bias=0.0, scale=1.0), sAC)
                    ac_rfree = sAC.n
                    st.wait("scalar", sAC)
                    st.emit("scalar", lambda e: e.activation(
                        out=recip[0:1, :], in_=recip[0:1, :], func=AF.Exp,
                        bias=0.0, scale=-1.0), sAC)
                    ac_rec = sAC.n

                    def mk_sweepend(h=h, kcls=kcls, psw=psw, ac_rec=ac_rec):
                        def emit():
                            nonlocal_dv = None
                            st.wait("tensor", sAC, ac_rec)
                            st.wait("tensor", sDV, self._p2_dvb[0])
                            st.emit("tensor", lambda e: e.matmul(
                                psB[:], self.onesrow_f[0:1, :], recip[0:1, :],
                                start=True, stop=True, skip_group_check=True), sPE)
                            spe_bc = sPE.n
                            st.wait("vector", sPE, spe_bc)
                            st.emit("vector", lambda e: e.tensor_copy(
                                recip_bc[:], psB[:]), sDV)
                            self._p2_dvb[0] = self.sDV.n
                            st.wait("vector", sDV, self._p2_dvb[0])
                            st.wait("vector", sOT[psw], sOT[psw].n)
                            st.emit("vector", lambda e, pO=psO[psw], o=ot_sb[psw]:
                                    e.tensor_tensor(o[:], pO[:], recip_bc[:],
                                                    op=ALU.mult), sDV)
                            dv_osfree[psw] = self.sDV.n
                            st.wait("gpsimd", sDV, dv_osfree[psw])
                            st.emit("gpsimd", lambda e, h=h, k=kcls, o=ot_sb[psw]:
                                    e.dma_start(
                                        out=AP(self.ot_spill, (h * QNT + k - 1) * DH * SW,
                                               [[SW, 128], [1, SW]]),
                                        in_=o[:]), sOT[psw], 16)
                        return emit

                    self._p2_send[0] = mk_sweepend()

            if self._p2_send[0] is not None:
                self._p2_send[0]()
                self._p2_send[0] = None
            st.wait("gpsimd", sOT[0], sOT[0].n)
            st.wait("gpsimd", sOT[1], sOT[1].n)
            st.wait("gpsimd", sWL, sWL.n)
            st.flush(nc)

    def p3(self):
        nc = self.nc
        sPE, sDV, sAC, sST = self.sPE, self.sDV, self.sAC, self.sST
        sOL = [self.sOLa, self.sOLb]
        sXT = self.sXT
        st = Streams()
        with ExitStack() as p3:
            def rsb(name, shape, dt=F32):
                return p3.enter_context(nc.sbuf_tensor(name, list(shape), dt, side="right"))

            otb = [rsb("otb%d" % i, [128, NCD * 128], BF16) for i in range(2)]
            xt = rsb("xt", [128, D])
            ysb = rsb("ysb", [128, D])
            x1f = rsb("x1f", [128, D])
            x1b = rsb("x1b", [128, D], BF16)
            x1T = rsb("x1T", [128, NCD * 128], BF16)
            xaw = rsb("xaw", [128, ER], BF16)
            xawT = rsb("xawT", [128, ER], BF16)
            stat = rsb("stat", [128, 24])
            mv = rsb("mv", [128, 2])
            rln = rsb("rln", [128, 1])
            eq1 = rsb("eq1", [128, 8])
            l2t = rsb("l2t", [128, 8])
            w8t = rsb("w8t", [128, 8])
            t1 = rsb("t1", [128, 1])
            t2 = rsb("t2", [128, 1])
            te = rsb("te", [128, 1])
            tg1 = rsb("tg1", [128, 1])
            tg2 = rsb("tg2", [128, 1])
            pBig = p3.enter_context(nc.psum_tensor("p3big", [128, D], F32))
            psG = p3.enter_context(nc.psum_tensor("p3g", [128, ER], F32))
            psL = p3.enter_context(nc.psum_tensor("p3l", [128, NE], F32))
            psT = [p3.enter_context(nc.psum_tensor("p3t%d" % i, [128, 1024], BF16))
                   for i in range(2)]

            pe_otfree = [0, 0]
            dv_pbig = 0
            dv_xt = [0]
            xtlds, olds = {}, {}

            def emit_loads(i):
                par = i % 2
                st.wait("sync", sPE, pe_otfree[par])
                for h in range(H):
                    st.emit("sync", lambda e, h=h, i=i, b=otb[par]: e.dma_start(
                        out=b[0:128, h * 128:h * 128 + 128],
                        in_=AP(self.ot_spill, (h * QNT + i // 4) * DH * SW + (i % 4) * 128,
                               [[SW, 128], [1, 128]])), sOL[par], 16)
                olds[i] = sOL[par].n
                st.wait("sync", sDV, dv_xt[0])
                st.emit("sync", lambda e, i=i: e.dma_start(
                    out=xt[:], in_=AP(self.x_own, i * 128 * D, [[D, 128], [1, D]])),
                    sXT, 16)
                xtlds[i] = self.sXT.n

            st.wait("tensor", self.sWL, self.sWL.n)
            emit_loads(0)
            for i in range(NT * 2):
                par = i % 2
                xtld = xtlds[i]

                st.wait("tensor", sOL[par], olds[i])
                st.wait("tensor", sDV, dv_pbig)
                for cd in range(NCD):
                    for vs in range(4):
                        st.emit("tensor", lambda e, cd=cd, vs=vs, b=otb[par]: e.matmul(
                            pBig[0:128, vs * 512:(vs + 1) * 512],
                            b[0:128, cd * 128:cd * 128 + 128],
                            self.WoT[0:128, cd * D + vs * 512:cd * D + vs * 512 + 512],
                            start=(cd == 0), stop=False, skip_group_check=True), None)
                for vs in range(4):
                    st.emit("tensor", lambda e, vs=vs: e.matmul(
                        pBig[0:128, vs * 512:(vs + 1) * 512], self.onesrow_bf[0:1, :],
                        self.bo_r[0:1, vs * 512:(vs + 1) * 512],
                        start=False, stop=True, skip_group_check=True),
                        sPE if vs == 3 else None)
                spe_wo = self.sPE.n
                pe_otfree[par] = spe_wo

                st.wait("vector", sPE, spe_wo)
                st.wait("vector", sXT, xtld)
                st.wait("vector", sST, self.sST.n)
                st.emit("vector", lambda e: e.tensor_tensor(ysb[:], pBig[:], xt[:],
                                                            op=ALU.add), sDV)
                dv_xt[0] = self.sDV.n
                if i + 1 < NT * 2:
                    emit_loads(i + 1)
                st.wait("vector", sDV)
                for cst in range(4):
                    st.emit("vector", lambda e, cst=cst: e.bn_stats(
                        stat[0:128, cst * 6:cst * 6 + 6],
                        ysb[0:128, cst * 512:(cst + 1) * 512]), sDV)
                st.wait("vector", sDV)
                st.emit("vector", lambda e: e.bn_aggr(mv[:], stat[:]), sDV)
                st.wait("scalar", sDV, self.sDV.n)
                st.emit("scalar", lambda e: e.activation(out=rln[:], in_=mv[0:128, 1:2],
                                                         func=AF.Ln, bias=self.epsb[:],
                                                         scale=1.0), sAC)
                st.wait("scalar", sAC)
                st.emit("scalar", lambda e: e.activation(out=rln[:], in_=rln[:],
                                                         func=AF.Exp, bias=0.0,
                                                         scale=-0.5), sAC)
                st.wait("vector", sAC)
                st.emit("vector", lambda e: e.tensor_scalar(
                    x1f[:], ysb[:], mv[0:128, 0:1], rln[:], op0=ALU.subtract,
                    op1=ALU.mult), sDV)
                st.wait("vector", sDV)
                st.emit("vector", lambda e: e.tensor_copy(x1b[:], x1f[:]), sDV)
                dv_x1 = self.sDV.n
                if self.taps:
                    st.wait("gpsimd", sDV, self.sDV.n)
                    st.emit("gpsimd", lambda e, i=i: e.dma_start(
                        out=AP(self.dbg2, i * 128 * D, [[D, 128], [1, D]]),
                        in_=x1b[:]), self.sGP, 16)
                    st.wait("vector", self.sGP, self.sGP.n)

                st.wait("tensor", sDV, dv_x1)
                for bank in range(2):
                    for j in range(8):
                        cd = bank * 8 + j
                        st.emit("tensor", lambda e, cd=cd, j=j, p=psT[bank]: e.transpose(
                            p[0:128, j * 128:j * 128 + 128],
                            x1b[0:128, cd * 128:cd * 128 + 128], self.identb[:]),
                            sPE if j == 7 else None)
                    st.wait("vector", sPE)
                    st.emit("vector", lambda e, bank=bank, p=psT[bank]: e.tensor_copy(
                        x1T[0:128, bank * 1024:(bank + 1) * 1024], p[:]), sDV)
                dv_x1T = self.sDV.n

                st.wait("tensor", sDV, dv_x1T)
                for cd in range(NCD):
                    st.emit("tensor", lambda e, cd=cd: e.matmul(
                        psL[0:128, 0:NE], x1T[0:128, cd * 128:cd * 128 + 128],
                        self.gateT[0:128, cd * NE:(cd + 1) * NE],
                        start=(cd == 0), stop=False, skip_group_check=True), None)
                    st.emit("tensor", lambda e, cd=cd: e.matmul(
                        psG[0:128, 0:ER], x1T[0:128, cd * 128:cd * 128 + 128],
                        self.AcatT[0:128, cd * ER:(cd + 1) * ER],
                        start=(cd == 0), stop=(cd == NCD - 1), skip_group_check=True), None)
                st.emit("tensor", lambda e: e.matmul(
                    psL[0:128, 0:NE], self.onesrow_bf[0:1, :], self.gateb_r[0:1, :],
                    start=False, stop=True, skip_group_check=True), sPE)
                spe_gx = self.sPE.n
                for cd in range(NCD):
                    for vs in range(4):
                        st.emit("tensor", lambda e, cd=cd, vs=vs: e.matmul(
                            pBig[0:128, vs * 512:(vs + 1) * 512],
                            x1T[0:128, cd * 128:cd * 128 + 128],
                            self.WffnT[0:128, cd * D + vs * 512:cd * D + vs * 512 + 512],
                            start=(cd == 0), stop=False, skip_group_check=True), None)
                for vs in range(4):
                    st.emit("tensor", lambda e, vs=vs: e.matmul(
                        pBig[0:128, vs * 512:(vs + 1) * 512], self.onesrow_bf[0:1, :],
                        self.bffn_r[0:1, vs * 512:(vs + 1) * 512],
                        start=False, stop=False, skip_group_check=True), None)

                st.wait("vector", sPE, spe_gx)
                st.emit("vector", lambda e: e.tensor_reduce(
                    t1[:], psL[0:128, 0:NE], axis=AX.X, op=ALU.max), sDV)
                st.wait("vector", sDV)
                st.emit("vector", lambda e: e.tensor_scalar(
                    eq1[:], psL[0:128, 0:NE], t1[:], None, op0=ALU.is_equal), sDV)
                st.wait("vector", sDV)
                st.emit("vector", lambda e: e.scalar_tensor_tensor(
                    l2t[:], eq1[:], -1e30, psL[0:128, 0:NE], op0=ALU.mult,
                    op1=ALU.add), sDV)
                st.wait("vector", sDV)
                st.emit("vector", lambda e: e.tensor_reduce(
                    t2[:], l2t[:], axis=AX.X, op=ALU.max), sDV)
                st.wait("vector", sDV)
                st.emit("vector", lambda e: e.tensor_tensor(
                    te[:], t2[:], t1[:], op=ALU.subtract), sDV)
                st.wait("scalar", sDV, self.sDV.n)
                st.emit("scalar", lambda e: e.activation(out=te[:], in_=te[:], func=AF.Exp,
                                                         bias=0.0, scale=1.0), sAC)
                st.wait("vector", sAC)
                st.emit("vector", lambda e: e.tensor_scalar(
                    tg1[:], te[:], 1.0, None, op0=ALU.add), sDV)
                st.wait("vector", sDV)
                st.emit("vector", lambda e: e.reciprocal(tg1[:], tg1[:]), sDV)
                st.wait("vector", sDV)
                st.emit("vector", lambda e: e.tensor_scalar(
                    tg2[:], tg1[:], -1.0, 1.0, op0=ALU.mult, op1=ALU.add), sDV)
                st.emit("vector", lambda e: e.tensor_scalar(
                    eq1[:], eq1[:], tg1[:], None, op0=ALU.mult), sDV)
                st.emit("vector", lambda e: e.tensor_scalar(
                    l2t[:], l2t[:], t2[:], None, op0=ALU.is_equal), sDV)
                st.wait("vector", sDV)
                st.emit("vector", lambda e: e.scalar_tensor_tensor(
                    w8t[:], l2t[:], tg2[:], eq1[:], op0=ALU.mult, op1=ALU.add), sDV)
                st.wait("vector", sDV)
                st.emit("vector", lambda e: e.tensor_tensor(
                    xaw[:].rearrange("p (e r) -> p e r", r=RLORA),
                    AP(psG, 0, [[ER, 128], [16, NE], [1, RLORA]]),
                    w8t[:].unsqueeze(2).broadcast_to([128, NE, RLORA]),
                    op=ALU.mult), sDV)
                dv_xaw = self.sDV.n

                st.wait("tensor", sDV, dv_xaw)
                st.emit("tensor", lambda e: e.transpose(
                    psT[0][0:128, 0:128], xaw[:], self.identb[:]), sPE)
                st.wait("vector", sPE)
                st.emit("vector", lambda e: e.tensor_copy(
                    xawT[:], psT[0][0:128, 0:128]), sDV)
                st.wait("tensor", sDV)
                for vs in range(4):
                    st.emit("tensor", lambda e, vs=vs: e.matmul(
                        pBig[0:128, vs * 512:(vs + 1) * 512], xawT[:],
                        self.BcatT[0:128, vs * 512:(vs + 1) * 512],
                        start=False, stop=True, skip_group_check=True),
                        sPE if vs == 3 else None)
                spe_ffn = self.sPE.n

                st.wait("vector", sPE, spe_ffn)
                st.emit("vector", lambda e: e.tensor_tensor(ysb[:], pBig[:], x1f[:],
                                                            op=ALU.add), sDV)
                dv_pbig = self.sDV.n
                if self.taps:
                    st.wait("gpsimd", sDV, self.sDV.n)
                    st.emit("gpsimd", lambda e, i=i: e.dma_start(
                        out=AP(self.dbg1, i * 128 * D, [[D, 128], [1, D]]),
                        in_=ysb[:]), self.sGP, 16)
                    st.emit("gpsimd", lambda e, i=i: e.dma_start(
                        out=AP(self.dbg3, i * 128 * ER, [[ER, 128], [1, ER]]),
                        in_=xaw[:]), self.sGP, 16)
                    st.wait("vector", self.sGP, self.sGP.n)
                st.wait("vector", sDV)
                for cst in range(4):
                    st.emit("vector", lambda e, cst=cst: e.bn_stats(
                        stat[0:128, cst * 6:cst * 6 + 6],
                        ysb[0:128, cst * 512:(cst + 1) * 512]), sDV)
                st.wait("vector", sDV)
                st.emit("vector", lambda e: e.bn_aggr(mv[:], stat[:]), sDV)
                st.wait("scalar", sDV, self.sDV.n)
                st.emit("scalar", lambda e: e.activation(out=rln[:], in_=mv[0:128, 1:2],
                                                         func=AF.Ln, bias=self.epsb[:],
                                                         scale=1.0), sAC)
                st.wait("scalar", sAC)
                st.emit("scalar", lambda e: e.activation(out=rln[:], in_=rln[:],
                                                         func=AF.Exp, bias=0.0,
                                                         scale=-0.5), sAC)
                st.wait("vector", sAC)
                st.emit("vector", lambda e: e.tensor_scalar(
                    ysb[:], ysb[:], mv[0:128, 0:1], rln[:], op0=ALU.subtract,
                    op1=ALU.mult), sDV)
                st.wait("sync", sDV)
                st.emit("sync", lambda e, i=i: e.dma_start(
                    out=AP(self.out, i * 128 * D, [[D, 128], [1, D]]), in_=ysb[:]),
                    sST, 16)

            st.wait("sync", sST)
            st.wait("gpsimd", sST, self.sST.n)
            st.flush(nc)


def _rotary_tables():
    j = np.arange(32, dtype=np.float64)
    f = (1.0 / 10000.0) ** ((2.0 * j) / 64.0)
    h = np.arange(H, dtype=np.float64)
    ang = h[:, None] * f[None, :]
    c, s = np.cos(ang), np.sin(ang)
    C = np.ones(D, dtype=np.float64)
    S = np.zeros(D, dtype=np.float64)
    for hh in range(H):
        C[128 * hh + 64:128 * hh + 96] = c[hh]
        C[128 * hh + 96:128 * hh + 128] = c[hh]
        S[128 * hh + 64:128 * hh + 96] = s[hh]
        S[128 * hh + 96:128 * hh + 128] = -s[hh]
    return (np.ascontiguousarray(np.broadcast_to(C.astype(np.float32), (128, D))),
            np.ascontiguousarray(np.broadcast_to(S.astype(np.float32), (128, D))))


def make_in_maps(inputs):
    C, S = _rotary_tables()
    x = np.asarray(inputs["L0_attn_x"], dtype=np.float32)
    f32c = lambda k: np.ascontiguousarray(np.asarray(inputs[k], dtype=np.float32))
    shared = {
        "w_dq": f32c("L0_attn_W_d_q"), "b_dq": f32c("L0_attn_b_d_q"),
        "qng": f32c("L0_attn_q_norm_g"), "qnb": f32c("L0_attn_q_norm_b"),
        "w_uq": f32c("L0_attn_W_u_q"), "b_uq": f32c("L0_attn_b_u_q"),
        "w_dkv": f32c("L0_attn_W_d_kv"), "b_dkv": f32c("L0_attn_b_d_kv"),
        "kvg": f32c("L0_attn_kv_norm_g"), "kvb": f32c("L0_attn_kv_norm_b"),
        "w_ukv": f32c("L0_attn_W_u_kv"), "b_ukv": f32c("L0_attn_b_u_kv"),
        "w_o": f32c("L0_attn_W_o"), "b_o": f32c("L0_attn_b_o"),
        "w_ffn": f32c("L0_ffn_W"), "b_ffn": f32c("L0_ffn_b"),
        "lora_a": f32c("L0_ffn_lora_A"), "lora_b": f32c("L0_ffn_lora_B"),
        "gate_w": f32c("L0_ffn_gate_w"), "gate_b": f32c("L0_ffn_gate_b"),
        "cosrep": C, "sinrep": S,
    }
    in_maps = []
    for c in range(8):
        b, rho = c // 4, c % 4
        tiles = [x[b, 512 * (4 * j + rho):512 * (4 * j + rho) + 512] for j in range(QNT)]
        m = dict(shared)
        m["x_own"] = np.ascontiguousarray(np.concatenate(tiles, axis=0))
        m["mthr"] = np.full((128, 1), -512.0 * rho, dtype=np.float32)
        in_maps.append(m)
    return in_maps


def assemble_out(results):
    y = np.empty((B, T, D), dtype=np.float32)
    for c in range(8):
        b, rho = c // 4, c % 4
        o = np.asarray(results[c]["out"])
        for j in range(QNT):
            y[b, 512 * (4 * j + rho):512 * (4 * j + rho) + 512] = o[512 * j:512 * j + 512]
    return y


def kernel(**inputs):
    mk = MK()
    nc = mk.build()
    in_maps = make_in_maps(inputs)
    res = run_bass_kernel_spmd(nc, in_maps, core_ids=list(range(8)))
    return assemble_out(res.results)
